# version 05
# baseline (speedup 1.0000x reference)
import contextlib
import numpy as np
import ml_dtypes
import concourse.bass as bass
import concourse.mybir as mybir
from concourse.bass_utils import run_bass_kernel_spmd

F32 = mybir.dt.float32
BF16 = mybir.dt.bfloat16
AF = mybir.ActivationFunctionType
ALU = mybir.AluOpType
AXX = mybir.AxisListType.X

B, L, D, H, S, Q = 4, 4096, 1024, 2048, 16, 16
TL = 2048
NJ = TL // Q
NT = H // 128
ND = D // 128
NCORE = 8
EPS = 1e-5

DEBUG = False
CUT = None


def _tt_perm():
    slots = np.arange(TL)
    tau, j = slots // NJ, slots % NJ
    return j * Q + tau


def _rep(v):
    v = np.asarray(v, np.float32).reshape(1, -1)
    return np.ascontiguousarray(np.broadcast_to(v, (128, v.shape[1])))


def _host_inputs(inputs):
    perm = _tt_perm()
    x = np.ascontiguousarray(np.asarray(inputs["L0_attn_x"], np.float32))

    k = np.arange(D)
    ang = 2.0 * np.pi * np.outer(k, k) / D
    cdf = np.cos(ang).astype(np.float32)
    sdf = np.sin(ang).astype(np.float32)
    cd = cdf.astype(ml_dtypes.bfloat16)
    sd = sdf.astype(ml_dtypes.bfloat16)
    sn = (-sdf).astype(ml_dtypes.bfloat16)

    qq, rr = np.arange(Q), np.arange(Q)
    tcwv = _rep(np.broadcast_to((Q - 1.0 - qq)[:, None], (Q, Q)).reshape(-1))
    tcs2 = _rep(np.broadcast_to((rr + 1.0)[None, :], (Q, Q)).reshape(-1))
    tck8 = _rep(np.broadcast_to((qq + 0.0)[:, None], (Q, Q)).reshape(-1))

    dm = np.zeros((128, 128), np.float32)
    for p in range(128):
        dm[p, (p // 16) * 16:(p // 16 + 1) * 16] = 1.0
    dm = dm.astype(ml_dtypes.bfloat16)
    rst = np.ones((128, NT * NJ), np.float32)
    rst[:, ::NJ] = 0.0
    rst = rst.astype(ml_dtypes.bfloat16)

    def t2(w):
        return np.ascontiguousarray(np.asarray(w, np.float32).T)

    wxt, wgt, wdtt = t2(inputs["L0_attn_W_x"]), t2(inputs["L0_attn_W_g"]), t2(inputs["L0_attn_W_dt"])
    woutt = np.ascontiguousarray(np.asarray(inputs["L0_attn_W_out"], np.float32).T)
    wrt = np.ascontiguousarray(np.asarray(inputs["L0_ffn_w_freq_real"], np.float32).T)
    wit = np.ascontiguousarray(np.asarray(inputs["L0_ffn_w_freq_imag"], np.float32).T)

    def b2(b):
        return np.ascontiguousarray(np.asarray(b, np.float32).reshape(NT, 128).T)

    bx2, bg2, bdt2 = b2(inputs["L0_attn_b_x"]), b2(inputs["L0_attn_b_g"]), b2(inputs["L0_attn_b_dt"])
    cw = np.asarray(inputs["L0_attn_conv_w"], np.float32)
    convw2 = np.concatenate([b2(cw[:, 0, t]) for t in range(3)], axis=1)
    convb2 = b2(inputs["L0_attn_conv_b"])

    def hs2(a):
        return np.ascontiguousarray(np.asarray(a, np.float32)
                                    .reshape(NT, 128, S).transpose(1, 0, 2).reshape(128, NT * S))

    anat, bnat, cnat = hs2(inputs["L0_attn_A_raw"]), hs2(inputs["L0_attn_B_ssm"]), hs2(inputs["L0_attn_C_ssm"])

    boutB = _rep(inputs["L0_attn_b_out"])
    lngB = _rep(inputs["L0_attn_ln_g"])
    lnbB = _rep(inputs["L0_attn_ln_b"])
    fbB = _rep(inputs["L0_ffn_bias"])
    idf = np.eye(128, dtype=np.float32)

    in_maps = []
    for c in range(NCORE):
        b, half = c >> 1, c & 1
        t0 = half * TL
        xs = np.ascontiguousarray(x[b, t0 + perm])
        halo = np.zeros((2, D), np.float32)
        hm = np.zeros(2, np.float32)
        if t0 - 1 >= 0:
            halo[0] = x[b, t0 - 1]
            hm[0] = 1.0
        if t0 + TL < L:
            halo[1] = x[b, t0 + TL]
            hm[1] = 1.0
        cm = np.array([1.0 - half, float(half)], np.float32)
        sel16 = np.zeros((1, 16), np.float32)
        sel16[0, c] = 1.0
        sel16[0, 8 + c] = 1.0
        sel8 = np.zeros((1, 8), np.float32)
        sel8[0, c] = 1.0
        m = dict(
            xin=xs, xhalo=halo, halomask=_rep(hm), cmask=_rep(cm),
            wxt=wxt, wgt=wgt, wdtt=wdtt, woutt=woutt,
            wrtc=np.ascontiguousarray(wrt[:, c * 128:(c + 1) * 128]),
            witc=np.ascontiguousarray(wit[:, c * 128:(c + 1) * 128]),
            cdft=cd, sdft=sd, sneg=sn,
            cdcol=np.ascontiguousarray(cd[:, c * 128:(c + 1) * 128]),
            sdcol=np.ascontiguousarray(sd[:, c * 128:(c + 1) * 128]),
            bx2=bx2, bg2=bg2, bdt2=bdt2, convw2=convw2, convb2=convb2,
            anat=anat, bnat=bnat, cnat=cnat,
            tcwv=tcwv, tcs2=tcs2, tck8=tck8,
            dmask=dm, rstmask=rst, idf32=idf,
            boutB=boutB, lngB=lngB, lnbB=lnbB, fbB=fbB,
            sel16=_rep(sel16), sel8=_rep(sel8),
        )
        in_maps.append(m)
    return in_maps


class KB:
    ENGS = ("sync", "scalar", "vector", "tensor", "gpsimd")

    def __init__(self, nc, stack):
        self.nc = nc
        self.stack = stack
        self.ops = {e: [] for e in self.ENGS}
        self.sem = {}
        self.last = {e: (None, 0) for e in self.ENGS}

    def mksem(self, *names):
        for n in names:
            h = self.stack.enter_context(self.nc.semaphore(n))
            self.sem[n] = [h, 0]

    def dma(self, eng, out, in_, sem, **kw):
        rec = self.sem[sem]
        rec[1] += 16

        def f(e, out=out, in_=in_, h=rec[0], kw=kw):
            e.dma_start(out=out, in_=in_, **kw).then_inc(h, 16)
        self.ops[eng].append(f)

    def op(self, eng, build, sem=None):
        if sem is None:
            sem = "f" + eng[0]
        rec = self.sem[sem]
        rec[1] += 1
        h = rec[0]
        self.last[eng] = (sem, rec[1])

        def f(e, build=build, h=h):
            build(e).then_inc(h, 1)
        self.ops[eng].append(f)

    def bar(self, eng):
        name, v = self.last[eng]
        if v <= 0:
            return
        rec = self.sem[name]
        self.ops[eng].append(lambda e, h=rec[0], v=v: e.wait_ge(h, v))

    def wait(self, eng, sem, val=None):
        rec = self.sem[sem]
        v = rec[1] if val is None else val
        if v <= 0:
            return
        self.ops[eng].append(lambda e, h=rec[0], v=v: e.wait_ge(h, v))

    def val(self, sem):
        return self.sem[sem][1]


def build_nc():
    nc = bass.Bass("TRN2", target_bir_lowering=False, debug=False)
    DT = nc.dram_tensor
    skind = "ExternalOutput" if DEBUG else "Internal"

    ins = {}
    for name, shape, dt in [
        ("xin", [TL, D], F32), ("xhalo", [2, D], F32),
        ("halomask", [128, 2], F32), ("cmask", [128, 2], F32),
        ("wxt", [D, H], F32), ("wgt", [D, H], F32), ("wdtt", [D, H], F32),
        ("woutt", [H, D], F32), ("wrtc", [D, 128], F32), ("witc", [D, 128], F32),
        ("cdft", [D, D], BF16), ("sdft", [D, D], BF16), ("sneg", [D, D], BF16),
        ("cdcol", [D, 128], BF16), ("sdcol", [D, 128], BF16),
        ("bx2", [128, NT], F32), ("bg2", [128, NT], F32), ("bdt2", [128, NT], F32),
        ("convw2", [128, 3 * NT], F32), ("convb2", [128, NT], F32),
        ("anat", [128, NT * S], F32), ("bnat", [128, NT * S], F32), ("cnat", [128, NT * S], F32),
        ("tcwv", [128, 256], F32), ("tcs2", [128, 256], F32), ("tck8", [128, 256], F32),
        ("dmask", [128, 128], BF16), ("rstmask", [128, NT * NJ], BF16),
        ("idf32", [128, 128], F32),
        ("boutB", [128, D], F32), ("lngB", [128, D], F32), ("lnbB", [128, D], F32),
        ("fbB", [128, D], F32),
        ("sel16", [128, 16], F32), ("sel8", [128, 8], F32),
    ]:
        ins[name] = DT(name, shape, dt, kind="ExternalInput").ap()

    xout = DT("xout", [TL, D], F32, kind="ExternalOutput").ap()
    udram = DT("udram", [H, TL], BF16, kind=skind).ap()
    ydram = DT("ydram", [H, TL], BF16, kind=skind).ap()
    pdram = DT("pdram", [H, 1024], F32, kind=skind).ap()
    dtin = DT("dtin", [128, NT], F32, kind="Internal").ap()
    dtout = DT("dtout", [128, NT], F32, kind="Internal", addr_space="Shared").ap()
    u12in = DT("u12in", [16 * 128, D], BF16, kind="Internal").ap()
    u12out = DT("u12out", [16 * 128, D], BF16, kind="Internal", addr_space="Shared").ap()
    mtin = DT("mtin", [D, D], BF16, kind="Internal").ap()
    mtout = DT("mtout", [D, D], BF16, kind="Internal", addr_space="Shared").ap()
    ssin = DT("ssin", [128, NT * S], F32, kind="Internal").ap()
    if DEBUG:
        tap_dt = DT("tap_dt", [128, NT], F32, kind="ExternalOutput").ap()
        tap_xn = DT("tap_xn", [128, 16384], BF16, kind="ExternalOutput").ap()
        tap_u = DT("tap_u", [128, TL], F32, kind="ExternalOutput").ap()
        tap_uh = DT("tap_uh", [128, 2 * NT], F32, kind="ExternalOutput").ap()
        tap_u12 = DT("tap_u12", [16 * 128, D], BF16, kind="ExternalOutput").ap()
        tap_ms = DT("tap_ms", [128, 32], F32, kind="ExternalOutput").ap()
        tap_mt = DT("tap_mt", [D, D], BF16, kind="ExternalOutput").ap()
        tap_sh = DT("tap_sh", [128, NT * S], F32, kind="ExternalOutput").ap()
    ssout = DT("ssout", [128, NT * S], F32, kind="Internal").ap()

    with contextlib.ExitStack() as stack:
        E = stack.enter_context

        c_hm = E(nc.sbuf_tensor("c_hm", [128, 2], F32))
        c_cm = E(nc.sbuf_tensor("c_cm", [128, 2], F32))
        c_bx = E(nc.sbuf_tensor("c_bx", [128, NT], F32))
        c_bg = E(nc.sbuf_tensor("c_bg", [128, NT], F32))
        c_bdt = E(nc.sbuf_tensor("c_bdt", [128, NT], F32))
        c_cw = E(nc.sbuf_tensor("c_cw", [128, 3 * NT], F32))
        c_cb = E(nc.sbuf_tensor("c_cb", [128, NT], F32))
        c_an = E(nc.sbuf_tensor("c_an", [128, 256], F32))
        c_bn = E(nc.sbuf_tensor("c_bn", [128, 256], F32))
        c_cn = E(nc.sbuf_tensor("c_cn", [128, 256], F32))
        c_tcwv = E(nc.sbuf_tensor("c_tcwv", [128, 256], F32))
        c_tcs2 = E(nc.sbuf_tensor("c_tcs2", [128, 256], F32))
        c_tck8 = E(nc.sbuf_tensor("c_tck8", [128, 256], F32))
        c_dm = E(nc.sbuf_tensor("c_dm", [128, 128], BF16))
        c_rst = E(nc.sbuf_tensor("c_rst", [128, NT * NJ], BF16))
        c_id = E(nc.sbuf_tensor("c_id", [128, 128], F32))
        c_s16 = E(nc.sbuf_tensor("c_s16", [128, 16], F32))
        c_s8 = E(nc.sbuf_tensor("c_s8", [128, 8], F32))

        bigA = E(nc.sbuf_tensor("bigA", [128, 24576], BF16))
        bigB = E(nc.sbuf_tensor("bigB", [128, 32768], BF16))
        yffn = E(nc.sbuf_tensor("yffn", [128, NT * 1024], BF16))

        xnbf = bigA[:, 0:16384]
        mtres = bigA[:, 16384:24576]
        wv16 = bigA[:, 0:2048]
        ws16 = bigA[:, 2048:4096]
        wi16 = bigA[:, 4096:6144]
        aq16 = bigA[:, 20480:22528]
        strps = [bigA[:, 8192:9216], bigA[:, 9216:10240]]
        gbs = [bigA[:, 10240:12288], bigA[:, 12288:14336]]
        ygbs = [bigA[:, 14336:16384], bigA[:, 16384:18432]]
        sout = bigA[:, 18432:20480]
        wob = bigA[:, 0:16384]

        bigBf = bigB[:].bitcast(F32)
        ut = [bigBf[:, 0:2048], bigBf[:, 2048:4096]]
        cvt = bigBf[:, 4096:6144]
        dsd = bigBf[:, 6144:8192]
        ums = [bigB[:, 16384:18432], bigB[:, 18432:20480]]
        csb = bigB[:, 20480:23552]
        wfb = bigB[:, 23552:25600]
        u1b = bigB[:, 25600:26624]
        u2b = bigB[:, 26624:27648]
        mtblk = bigB[:, 27648:28672]
        stg = bigB[:, 28672:29696]
        mcs = bigB[:, 20480:22528]
        u12mv = bigB[:, 22528:23552]
        stg2 = bigB[:, 22528:23552]
        wpju = [bigB[:, 23552:24576], bigB[:, 24576:25600]]
        wpjg = [bigB[:, 29696:30720], bigB[:, 30720:31744]]
        cfb_f = bigBf[:, 0:1024]
        yhalf = bigB[:, 0:16384]
        tailf = bigBf
        rtk = tailf[:, 8192:9216]
        snm = tailf[:, 9216:10240]
        cbout_t = tailf[:, 10240:11264]
        ptt = tailf[:, 11264:12288]
        clng_t = tailf[:, 12288:13312]
        ot0 = tailf[:, 13312:14336]
        t3t = tailf[:, 14336:15360]
        clnb_t = tailf[:, 15360:16384]

        xt0 = E(nc.sbuf_tensor("xt0", [128, D], F32))
        xt1 = E(nc.sbuf_tensor("xt1", [128, D], F32))
        xts = [xt0, xt1]
        xnt = E(nc.sbuf_tensor("xnt", [128, D], F32))
        xh = E(nc.sbuf_tensor("xh", [128, D], F32))
        sqsc = E(nc.sbuf_tensor("sqsc", [128, D], F32))
        esc = E(nc.sbuf_tensor("esc", [128, 512], F32))
        stat = E(nc.sbuf_tensor("stat", [128, 16], F32))
        mstats = E(nc.sbuf_tensor("mstats", [128, 32], F32))
        xnhbf = E(nc.sbuf_tensor("xnhbf", [128, 16], BF16))
        uh = E(nc.sbuf_tensor("uh", [128, 2 * NT], F32))
        dtacc = E(nc.sbuf_tensor("dtacc", [128, 64], F32))
        dtv = E(nc.sbuf_tensor("dtv", [128, NT], F32))
        lnad = E(nc.sbuf_tensor("lnad", [128, 256], F32))
        a_f = E(nc.sbuf_tensor("a_f", [128, 256], F32))
        adt = E(nc.sbuf_tensor("adt", [128, 256], F32))
        bcn = E(nc.sbuf_tensor("bcn", [128, 256], F32))
        rca = E(nc.sbuf_tensor("rca", [128, 256], F32))
        dsc1 = E(nc.sbuf_tensor("dsc1", [128, 256], F32))
        dsc2 = E(nc.sbuf_tensor("dsc2", [128, 256], F32))
        cd0 = E(nc.sbuf_tensor("cd0", [128, 1024], F32))
        cd1 = E(nc.sbuf_tensor("cd1", [128, 1024], F32))
        kpad = E(nc.sbuf_tensor("kpad", [128, 31], F32))
        shall = E(nc.sbuf_tensor("shall", [128, 256], F32))
        sheff = E(nc.sbuf_tensor("sheff", [128, 256], F32))
        sendst = E(nc.sbuf_tensor("sendst", [128, 256], F32))
        vfix = E(nc.sbuf_tensor("vfix", [128, 16], F32))
        sheffb = E(nc.sbuf_tensor("sheffb", [128, 256], BF16))

        c_eps = E(nc.sbuf_tensor("c_eps", [128, 1], F32))
        c_cb2 = E(nc.sbuf_tensor("c_cb2", [128, NT], F32))
        c_mball = E(nc.sbuf_tensor("c_mball", [128, 2 * NT], F32))
        c_hm1 = E(nc.sbuf_tensor("c_hm1", [128, 2], F32))
        wearly = E(nc.sbuf_tensor("wearly", [128, 4096], BF16))
        psumA = E(nc.psum_tensor("psumA", [128, 2048], F32))
        psumB = E(nc.psum_tensor("psumB", [128, 2048], F32))

        kb = KB(nc, stack)
        marks = {}

        def mark(name):
            marks[name] = {e: len(kb.ops[e]) for e in kb.ENGS}
        kb.mksem("qc", "qx", "qw", "qm", "qpd", "qg", "qs", "qyd", "qyl",
                 "qf", "qdt", "qar", "qo", "qstg", "qxr",
                 "vx", "vxt", "tx", "sx", "tp", "tpt", "sdt", "se", "vc",
                 "vd", "gar", "vb", "sb", "vw", "tv", "vs", "ty", "sy",
                 "tf", "tfk", "tfs", "sf", "sf2", "tt2", "vt", "st2", "gp",
                 "fv", "fs", "ft", "fg", "fy",
                 "qx0", "qx1", "qwf", "qwd0", "qwd1", "qwu0", "qwu1",
                 "qwg0", "qwg1", "qg0", "qg1", "qs0", "qs1", "qxr0", "qxr1",
                 "qmt", "qfb", "qtap", "seu", "seg", "qwd2", "qwd3",
                 "qpd0", "qpd1", "qstg2", "qwe0", "qwe1")

        for nm, dst in [("halomask", c_hm), ("cmask", c_cm), ("bx2", c_bx),
                        ("bg2", c_bg), ("bdt2", c_bdt), ("convw2", c_cw),
                        ("convb2", c_cb), ("anat", c_an), ("bnat", c_bn),
                        ("cnat", c_cn), ("tcwv", c_tcwv), ("tcs2", c_tcs2),
                        ("tck8", c_tck8), ("dmask", c_dm), ("rstmask", c_rst),
                        ("idf32", c_id), ("sel16", c_s16), ("sel8", c_s8)]:
            kb.dma("sync", dst[:], ins[nm], "qc")
        v_qc = kb.val("qc")

        kb.dma("gpsimd", wfb[:, 0:1024].rearrange("p (k d) -> p k d", d=128),
               ins["wrtc"].rearrange("(k p) d -> p k d", k=8), "qwf")
        kb.dma("gpsimd", wfb[:, 1024:2048].rearrange("p (k d) -> p k d", d=128),
               ins["witc"].rearrange("(k p) d -> p k d", k=8), "qwf")
        v_qw_wf = kb.val("qwf")
        for tt in range(2):
            kb.dma("gpsimd", wearly[:, tt * 2048:tt * 2048 + 1024].rearrange("p (k h) -> p k h", h=128),
                   ins["wxt"].rearrange("(k p) h -> p k h", k=8)[:, :, tt * 128:(tt + 1) * 128],
                   "qwe" + str(tt))
            kb.dma("gpsimd", wearly[:, tt * 2048 + 1024:tt * 2048 + 2048].rearrange("p (k h) -> p k h", h=128),
                   ins["wgt"].rearrange("(k p) h -> p k h", k=8)[:, :, tt * 128:(tt + 1) * 128],
                   "qwe" + str(tt))

        kb.wait("vector", "qc", v_qc)
        kb.op("vector", lambda v: v.tensor_add(dsc1[:, 0:NT], c_cw[:, 0:NT], c_cw[:, NT:2 * NT]), None)
        kb.bar("vector")
        kb.op("vector", lambda v: v.tensor_add(dsc1[:, 0:NT], dsc1[:, 0:NT], c_cw[:, 2 * NT:3 * NT]), None)
        kb.bar("vector")
        kb.op("vector", lambda v: v.tensor_mul(c_cb2[:], c_bx[:], dsc1[:, 0:NT]), None)
        kb.bar("vector")
        kb.op("vector", lambda v: v.tensor_add(c_cb2[:], c_cb2[:], c_cb[:]), None)
        kb.op("vector", lambda v: v.tensor_scalar_add(c_hm1[:], c_hm[:], -1.0), None)
        kb.bar("vector")
        kb.op("vector", lambda v: v.tensor_tensor(
            out=c_mball[:].rearrange("p (t e) -> p t e", e=2),
            in0=c_hm1[:].unsqueeze(1).broadcast_to([128, NT, 2]),
            in1=c_bx[:].unsqueeze(2).broadcast_to([128, NT, 2]),
            op=ALU.mult), "vb")

        kb.op("vector", lambda v: v.memset(c_eps[:], EPS), "vx")
        kb.op("vector", lambda v: v.memset(xh[:], 0.0), "vx")
        kb.wait("sync", "vx")
        kb.dma("sync", xh[0:2, :], ins["xhalo"], "qx")
        v_qx_h = kb.val("qx")
        for tt in range(NT):
            if tt >= 2:
                kb.wait("sync", "vxt", tt - 1)
            kb.dma("sync", xts[tt % 2][:], ins["xin"][tt * 128:(tt + 1) * 128, :],
                   "qx0" if tt % 2 == 0 else "qx1")

        def ln_tile(src, out_ap, scr, save_col=None):
            sm = stat[:, 0:1]
            sq = stat[:, 1:2]
            sv = stat[:, 2:3]
            sr = stat[:, 3:4]
            m2 = stat[:, 4:5]
            kb.bar("vector")
            kb.bar("scalar")
            kb.op("vector", lambda v, src=src, sm=sm: v.reduce_sum(sm, src, axis=AXX), "vx")
            kb.bar("vector")
            kb.wait("scalar", "vx")
            kb.op("scalar", lambda s, src=src, sq=sq, scr=scr: s.activation(
                out=scr, in_=src, func=AF.Square, accum_out=sq), "sx")
            kb.op("vector", lambda v, sm=sm: v.tensor_scalar_mul(sm, sm, 1.0 / D), "vx")
            kb.wait("vector", "sx")
            kb.op("vector", lambda v, sq=sq, sv=sv: v.tensor_scalar_mul(sv, sq, 1.0 / D), "vx")
            kb.bar("vector")
            kb.op("vector", lambda v, sm=sm, m2=m2: v.tensor_mul(m2, sm, sm), "vx")
            kb.bar("vector")
            kb.op("vector", lambda v, sv=sv, m2=m2: v.tensor_sub(sv, sv, m2), "vx")
            kb.wait("scalar", "vx")
            kb.op("scalar", lambda s, sv=sv: s.activation(out=sv, in_=sv, func=AF.Ln, bias=c_eps[:, 0:1]), "sx")
            kb.bar("scalar")
            kb.op("scalar", lambda s, sv=sv, sr=sr: s.activation(out=sr, in_=sv, func=AF.Exp, scale=-0.5), "sx")
            kb.wait("vector", "sx")
            kb.op("vector", lambda v, src=src, out_ap=out_ap, sm=sm, sr=sr: v.tensor_scalar(
                out=out_ap, in0=src, scalar1=sm, scalar2=sr,
                op0=ALU.subtract, op1=ALU.mult), "vx")
            if save_col is not None:
                kb.op("vector", lambda v, sm=sm, save_col=save_col: v.tensor_copy(
                    mstats[:, save_col:save_col + 1], sm), "vx")
                kb.op("vector", lambda v, sr=sr, save_col=save_col: v.tensor_copy(
                    mstats[:, 16 + save_col:16 + save_col + 1], sr), "vx")

        kb.wait("vector", "qx", v_qx_h)
        ln_tile(xh[:], xh[:], sqsc[:])
        kb.wait("tensor", "vx")
        kb.wait("tensor", "qc", v_qc)
        for kd in range(ND):
            kb.op("tensor", lambda t, kd=kd: t.transpose(
                psumA[:, 1024 + kd * 128:1024 + (kd + 1) * 128],
                xh[:, kd * 128:(kd + 1) * 128], c_id[:]), "tx")
        kb.wait("scalar", "tx")
        for kd in range(ND):
            kb.op("scalar", lambda s, kd=kd: s.copy(
                out=xnhbf[:, kd * 2:kd * 2 + 2],
                in_=psumA[:, 1024 + kd * 128:1024 + kd * 128 + 2]), "sx")
        v_sx_halo = kb.val("sx")

        for tt in range(NT):
            kb.wait("vector", "qx0" if tt % 2 == 0 else "qx1", 16 * (tt // 2 + 1))
            kb.wait("vector", "tx")
            ln_tile(xts[tt % 2][:], xnt[:], sqsc[:], save_col=tt)
            kb.op("vector", lambda v: v.tensor_copy(stat[:, 5:6], stat[:, 0:1]), "vxt")
            kb.bar("vector")
            kb.wait("tensor", "vx")
            for h4 in range(2):
                kb.wait("tensor", "sx")
                for kk in range(4):
                    kd = h4 * 4 + kk
                    kb.op("tensor", lambda t, h4=h4, kk=kk, kd=kd: t.transpose(
                        psumA[:, h4 * 512 + kk * 128:h4 * 512 + (kk + 1) * 128],
                        xnt[:, kd * 128:(kd + 1) * 128], c_id[:]), "tx")
                kb.wait("scalar", "tx")
                kb.op("scalar", lambda s, tt=tt, h4=h4: s.copy(
                    out=xnbf.rearrange("p (d t) -> p d t", t=TL)[:, h4 * 4:(h4 + 1) * 4, tt * 128:(tt + 1) * 128],
                    in_=psumA[:, h4 * 512:(h4 + 1) * 512].rearrange("p (d c) -> p d c", c=128)), "sx")
        v_sx_xn = kb.val("sx")
        if DEBUG:
            kb.wait("sync", "sx", v_sx_xn)
            kb.dma("sync", tap_xn, xnbf, "qtap")
            kb.wait("sync", "vx")
            kb.dma("sync", tap_ms, mstats[:], "qtap")

        mark("input")
        v_qf0 = kb.val("qf")
        for kk in range(ND):
            kb.wait("sync", "tfk")
            kb.dma("sync", csb[:, 0:1024], ins["cdft"][kk * 128:(kk + 1) * 128, :], "qf")
            kb.dma("sync", csb[:, 1024:2048], ins["sdft"][kk * 128:(kk + 1) * 128, :], "qf")
            kb.dma("sync", csb[:, 2048:3072], ins["sneg"][kk * 128:(kk + 1) * 128, :], "qf")
            kb.wait("tensor", "qf")
            if kk == 0:
                kb.wait("tensor", "qwf", v_qw_wf)
            for nb in range(2):
                kb.op("tensor", lambda t, kk=kk, nb=nb: t.matmul(
                    psumB[:, nb * 512:(nb + 1) * 512],
                    wfb[:, 0:1024].rearrange("p (k d) -> p k d", d=128)[:, kk, :],
                    csb[:, nb * 512:(nb + 1) * 512],
                    start=(kk == 0), stop=False), None)
                kb.op("tensor", lambda t, kk=kk, nb=nb: t.matmul(
                    psumB[:, nb * 512:(nb + 1) * 512],
                    wfb[:, 1024:2048].rearrange("p (k d) -> p k d", d=128)[:, kk, :],
                    csb[:, 2048 + nb * 512:2048 + (nb + 1) * 512],
                    start=False, stop=(kk == ND - 1)), None)
                kb.op("tensor", lambda t, kk=kk, nb=nb: t.matmul(
                    psumB[:, 1024 + nb * 512:1024 + (nb + 1) * 512],
                    wfb[:, 1024:2048].rearrange("p (k d) -> p k d", d=128)[:, kk, :],
                    csb[:, nb * 512:(nb + 1) * 512],
                    start=(kk == 0), stop=False), None)
                kb.op("tensor", lambda t, kk=kk, nb=nb: t.matmul(
                    psumB[:, 1024 + nb * 512:1024 + (nb + 1) * 512],
                    wfb[:, 0:1024].rearrange("p (k d) -> p k d", d=128)[:, kk, :],
                    csb[:, 1024 + nb * 512:1024 + (nb + 1) * 512],
                    start=False, stop=(kk == ND - 1)),
                    "tfk" if nb == 1 else None)
        kb.wait("scalar", "tfk", 8)
        kb.op("scalar", lambda s: s.copy(out=u1b[:], in_=psumB[:, 0:1024]), "sf")
        kb.op("scalar", lambda s: s.copy(out=u2b[:], in_=psumB[:, 1024:2048]), "sf")
        kb.wait("vector", "sf")
        for pos in range(16):
            srcb = u1b if pos < 8 else u2b
            kb.wait("vector", "qstg")
            kb.op("vector", lambda v, srcb=srcb, pos=pos: v.tensor_scalar_mul(
                stg[:], srcb[:], c_s16[:, pos:pos + 1]), "vb")
            kb.wait("scalar", "vb")
            kb.dma("scalar", u12in[pos * 128:(pos + 1) * 128, :], stg[:], "qstg")
        def pj_load(wname, tt, bufs, gate_sem, gate_val, sembase):
            if gate_val > 0:
                kb.wait("gpsimd", gate_sem, gate_val)
            kb.dma("gpsimd", bufs[tt % 2].rearrange("p (k h) -> p k h", h=128),
                   ins[wname].rearrange("(k p) h -> p k h", k=8)[:, :, tt * 128:(tt + 1) * 128],
                   sembase + str(tt % 2))

        wpj4 = [wpjg[0], wpjg[1], wpju[0], wpju[1]]
        for tt in range(NT):
            if tt == 2:
                kb.wait("gpsimd", "tfk", 8)
            if tt >= 4:
                kb.wait("gpsimd", "tp", 4 * (tt - 3))
            kb.dma("gpsimd", wpj4[tt % 4].rearrange("p (k h) -> p k h", h=128),
                   ins["wdtt"].rearrange("(k p) h -> p k h", k=8)[:, :, tt * 128:(tt + 1) * 128],
                   "qwd" + str(tt % 4))
        kb.wait("gpsimd", "qstg")
        kb.op("gpsimd", lambda g: g.collective_compute(
            "AllReduce", ALU.add, replica_groups=[list(range(8))],
            ins=[u12in], outs=[u12out]), "gar")
        v_gar_u12 = kb.val("gar")
        pj_load("wgt", 2, wpjg, "tp", 52, "qwg")
        pj_load("wgt", 3, wpjg, "tp", 56, "qwg")
        pj_load("wxt", 2, wpju, "tp", 60, "qwu")
        pj_load("wxt", 3, wpju, "tp", 64, "qwu")
        kb.wait("gpsimd", "tfk", 8)

        if DEBUG:
            kb.wait("sync", "gar", v_gar_u12)
            kb.dma("sync", tap_u12, u12out, "qtap")

        mark("ubuild")
        kb.wait("tensor", "sx", v_sx_xn)
        for tt in range(NT):
            kb.wait("tensor", "qwd" + str(tt % 4), 16 * (tt // 4 + 1))
            for nb in range(4):
                if 4 * tt + nb - 1 > 0:
                    kb.wait("tensor", "sdt", 4 * tt + nb - 1)
                for kk in range(ND):
                    kb.op("tensor", lambda t, tt=tt, nb=nb, kk=kk: t.matmul(
                        psumA[:, (nb % 2) * 512:(nb % 2 + 1) * 512],
                        wpj4[tt % 4].rearrange("p (k h) -> p k h", h=128)[:, kk, :],
                        xnbf[:, kk * TL + nb * 512:kk * TL + (nb + 1) * 512],
                        start=(kk == 0), stop=(kk == ND - 1)), "tp" if kk == ND - 1 else None)
                kb.wait("scalar", "tp")
                kb.bar("scalar")
                kb.op("scalar", lambda s, tt=tt, nb=nb: s.activation(
                    out=esc[:], in_=psumA[:, (nb % 2) * 512:(nb % 2 + 1) * 512],
                    func=AF.Exp, bias=c_bdt[:, tt:tt + 1]), None)
                kb.bar("scalar")
                kb.op("scalar", lambda s, tt=tt, nb=nb: s.activation(
                    out=esc[:], in_=esc[:], func=AF.Ln, bias=1.0,
                    accum_out=dtacc[:, tt * 4 + nb:tt * 4 + nb + 1]), "sdt")

        kb.wait("scalar", "qc", v_qc)
        kb.op("scalar", lambda s: s.activation(out=a_f[:], in_=c_an[:], func=AF.Exp), None)
        kb.bar("scalar")
        kb.op("scalar", lambda s: s.activation(out=a_f[:], in_=a_f[:], func=AF.Ln, bias=1.0), "sb")
        kb.wait("vector", "sb")
        kb.op("vector", lambda v: v.tensor_scalar_mul(a_f[:], a_f[:], -1.0), "vb")
        v_vb_af = kb.val("vb")

        kb.wait("vector", "sdt")
        kb.op("vector", lambda v: v.tensor_reduce(
            out=dtv[:], in_=dtacc[:].rearrange("p (t f) -> p t f", f=4),
            axis=AXX, op=ALU.add), "vd")
        kb.wait("scalar", "vd")
        kb.dma("scalar", dtin, dtv[:], "qdt")
        kb.wait("gpsimd", "qdt")
        kb.op("gpsimd", lambda g: g.collective_compute(
            "AllReduce", ALU.add, replica_groups=[list(range(8))],
            ins=[dtin], outs=[dtout]), "gar")
        v_gar_dt = kb.val("gar")
        kb.wait("sync", "gar", v_gar_dt)
        kb.dma("sync", dtv[:], dtout, "qdt")
        kb.wait("vector", "qdt")
        kb.op("vector", lambda v: v.tensor_scalar(
            out=dtv[:], in0=dtv[:], scalar1=1.0 / (B * L), scalar2=0.001,
            op0=ALU.mult, op1=ALU.max), "vd")
        kb.bar("vector")
        kb.op("vector", lambda v: v.tensor_scalar_min(dtv[:], dtv[:], 2.0), "vd")
        kb.bar("vector")
        v_vd_dt = kb.val("vd")
        if DEBUG:
            kb.wait("sync", "vd", v_vd_dt)
            kb.dma("sync", tap_dt, dtv[:], "qtap")

        mark("dt")
        kb.wait("sync", "gar", v_gar_u12)
        kb.dma("sync", mcs[:, 0:1024].rearrange("p (k d) -> p k d", d=128),
               ins["cdcol"].rearrange("(k p) d -> p k d", k=8), "qf")
        kb.dma("sync", mcs[:, 1024:2048].rearrange("p (k d) -> p k d", d=128),
               ins["sdcol"].rearrange("(k p) d -> p k d", k=8), "qf")
        for kk in range(ND):
            kb.wait("sync", "tfs")
            kb.dma("sync", u12mv[:], u12out[kk * 128:(kk + 1) * 128, :], "qf")
            kb.wait("tensor", "qf")
            if kk == 0:
                kb.wait("tensor", "sx", v_sx_xn)
            for nb in range(2):
                kb.op("tensor", lambda t, kk=kk, nb=nb: t.matmul(
                    psumA[:, 1024 + nb * 512:1024 + (nb + 1) * 512],
                    mcs[:, 0:1024].rearrange("p (k d) -> p k d", d=128)[:, kk, :],
                    u12mv[:, nb * 512:(nb + 1) * 512],
                    start=(kk == 0), stop=False), "tfs" if nb == 1 else None)
            kb.wait("sync", "tfs")
            kb.dma("sync", u12mv[:], u12out[1024 + kk * 128:1024 + (kk + 1) * 128, :], "qf")
            kb.wait("tensor", "qf")
            for nb in range(2):
                kb.op("tensor", lambda t, kk=kk, nb=nb: t.matmul(
                    psumA[:, 1024 + nb * 512:1024 + (nb + 1) * 512],
                    mcs[:, 1024:2048].rearrange("p (k d) -> p k d", d=128)[:, kk, :],
                    u12mv[:, nb * 512:(nb + 1) * 512],
                    start=False, stop=(kk == ND - 1)),
                    "tfs" if nb == 1 else None)
        kb.wait("scalar", "tfs", kb.val("tfs"))
        kb.op("scalar", lambda s: s.activation(out=mtblk[:], in_=psumA[:, 1024:2048],
                                               func=AF.Copy, scale=1.0 / D), "sf")
        kb.wait("vector", "sf")
        kb.wait("vector", "tfs", 16)
        v_qstg_b = kb.val("qstg")
        v_qstg2_b = kb.val("qstg2")
        for pos in range(8):
            sbuf_s = stg if pos % 2 == 0 else stg2
            qs_ = "qstg" if pos % 2 == 0 else "qstg2"
            base = v_qstg_b if pos % 2 == 0 else v_qstg2_b
            if pos >= 2:
                kb.wait("vector", qs_, base + 16 * (pos // 2))
            kb.op("vector", lambda v, pos=pos, sbuf_s=sbuf_s: v.tensor_scalar_mul(
                sbuf_s[:], mtblk[:], c_s8[:, pos:pos + 1]), "vb")
            kb.wait("scalar", "vb")
            kb.dma("scalar", mtin[pos * 128:(pos + 1) * 128, :], sbuf_s[:], qs_)
        kb.wait("gpsimd", "qstg")
        kb.wait("gpsimd", "qstg2")
        kb.op("gpsimd", lambda g: g.collective_compute(
            "AllReduce", ALU.add, replica_groups=[list(range(8))],
            ins=[mtin], outs=[mtout]), "gar")
        v_gar_mt = kb.val("gar")
        kb.wait("sync", "gar", v_gar_mt)
        kb.dma("sync", mtres.rearrange("p (k d) -> p k d", d=1024),
               mtout.rearrange("(k p) d -> p k d", k=8), "qmt")
        v_qf_mt = kb.val("qmt")
        if DEBUG:
            kb.dma("sync", tap_mt, mtout, "qtap")

        mark("mt")
        kb.wait("gpsimd", "tfk", 8)
        for tt in range(4, NT):
            pj_load("wxt", tt, wpju, "tp", 64 + 9 * max(0, tt - 1), "qwu")
            pj_load("wgt", tt, wpjg, "tp", 64 + 9 * max(0, tt - 1), "qwg")

        kb.wait("tensor", "sdt", 64)
        kb.wait("tensor", "sf", 2)
        TPB = 64
        for tt in range(NT):
            upj = wearly[:, tt * 2048:tt * 2048 + 1024] if tt < 2 else wpju[tt % 2]
            gpj = wearly[:, tt * 2048 + 1024:tt * 2048 + 2048] if tt < 2 else wpjg[tt % 2]
            if DEBUG and tt == 2:
                kb.wait("scalar", "qtap")
            if tt < 2:
                kb.wait("tensor", "qwe" + str(tt), 32)
            else:
                kb.wait("tensor", "qwu" + str(tt % 2), 16 * (tt // 2))
            for nb in range(4):
                if 4 * tt + nb - 1 > 0:
                    kb.wait("tensor", "seu", 4 * tt + nb - 1)
                for kk in range(ND):
                    kb.op("tensor", lambda t, upj=upj, nb=nb, kk=kk: t.matmul(
                        psumA[:, (nb % 2) * 512:(nb % 2 + 1) * 512],
                        upj.rearrange("p (k h) -> p k h", h=128)[:, kk, :],
                        xnbf[:, kk * TL + nb * 512:kk * TL + (nb + 1) * 512],
                        start=(kk == 0), stop=(kk == ND - 1)), "tp" if kk == ND - 1 else None)
                kb.wait("scalar", "tp", TPB + 9 * tt + nb + 1)
                if tt >= 2:
                    kb.wait("scalar", "vc", 2 * tt - 2)
                kb.op("scalar", lambda s, tt=tt, nb=nb: s.copy(
                    out=ut[tt % 2][:, nb * 512:(nb + 1) * 512],
                    in_=psumA[:, (nb % 2) * 512:(nb % 2 + 1) * 512]), "seu")
            if tt == 0:
                kb.wait("tensor", "sx", v_sx_halo)
                kb.wait("tensor", "sf", 3)
            if tt >= 1:
                kb.wait("tensor", "vc", 2 * tt - 1)
            for kk in range(ND):
                kb.op("tensor", lambda t, upj=upj, kk=kk: t.matmul(
                    psumA[:, 1024:1026],
                    upj.rearrange("p (k h) -> p k h", h=128)[:, kk, :],
                    xnhbf[:, kk * 2:(kk + 1) * 2],
                    start=(kk == 0), stop=(kk == ND - 1)), "tp" if kk == ND - 1 else None)
            kb.wait("vector", "tp", TPB + 9 * tt + 5)
            kb.op("vector", lambda v, tt=tt: v.tensor_mul(
                uh[:, tt * 2:(tt + 1) * 2], psumA[:, 1024:1026], c_hm[:]), None)
            kb.bar("vector")
            kb.op("vector", lambda v, tt=tt: v.tensor_add(
                uh[:, tt * 2:(tt + 1) * 2], uh[:, tt * 2:(tt + 1) * 2],
                c_mball[:, tt * 2:(tt + 1) * 2]), "vc")
            kb.bar("vector")
            if tt >= 2:
                kb.wait("tensor", "qwg" + str(tt % 2), 16 * (tt // 2))
            if tt >= 1:
                kb.wait("tensor", "vc", 2 * tt)
            for nb in range(4):
                for kk in range(ND):
                    kb.op("tensor", lambda t, gpj=gpj, nb=nb, kk=kk: t.matmul(
                        psumB[:, nb * 512:(nb + 1) * 512],
                        gpj.rearrange("p (k h) -> p k h", h=128)[:, kk, :],
                        xnbf[:, kk * TL + nb * 512:kk * TL + (nb + 1) * 512],
                        start=(kk == 0), stop=(kk == ND - 1)), "tp" if kk == ND - 1 else None)
            for nb in range(4):
                kb.wait("scalar", "tp", TPB + 9 * tt + 6 + nb)
                kb.op("scalar", lambda s, tt=tt, nb=nb: s.activation(
                    out=psumB[:, nb * 512:(nb + 1) * 512],
                    in_=psumB[:, nb * 512:(nb + 1) * 512],
                    func=AF.Sigmoid, bias=c_bg[:, tt:tt + 1]), "seg")
            if DEBUG and tt == 0:
                kb.wait("sync", "seu", 4)
                kb.dma("sync", tap_u, ut[0][:], "qtap")
            u_c = ut[tt % 2]
            um = ums[tt % 2]
            w0 = c_cw[:, tt:tt + 1]
            w1 = c_cw[:, NT + tt:NT + tt + 1]
            w2 = c_cw[:, 2 * NT + tt:2 * NT + tt + 1]
            cb = c_cb2[:, tt:tt + 1]
            kb.wait("vector", "seu", 4 * tt + 4)
            kb.op("vector", lambda v, u_c=u_c, w1=w1, cb=cb: v.tensor_scalar(
                out=cvt[:, 0:2048], in0=u_c[:], scalar1=w1, scalar2=cb,
                op0=ALU.mult, op1=ALU.add), None)
            kb.bar("vector")
            kb.op("vector", lambda v, u_c=u_c, w0=w0: v.scalar_tensor_tensor(
                out=cvt[:, 128:2048], in0=u_c[:, 0:1920], scalar=w0,
                in1=cvt[:, 128:2048], op0=ALU.mult, op1=ALU.add), None)
            kb.op("vector", lambda v, u_c=u_c, w0=w0: v.scalar_tensor_tensor(
                out=cvt[:, 1:128], in0=u_c[:, 1920:2047], scalar=w0,
                in1=cvt[:, 1:128], op0=ALU.mult, op1=ALU.add), None)
            kb.op("vector", lambda v, tt=tt, w0=w0: v.scalar_tensor_tensor(
                out=cvt[:, 0:1], in0=uh[:, tt * 2:tt * 2 + 1], scalar=w0,
                in1=cvt[:, 0:1], op0=ALU.mult, op1=ALU.add), None)
            kb.bar("vector")
            kb.op("vector", lambda v, u_c=u_c, w2=w2: v.scalar_tensor_tensor(
                out=cvt[:, 0:1920], in0=u_c[:, 128:2048], scalar=w2,
                in1=cvt[:, 0:1920], op0=ALU.mult, op1=ALU.add), None)
            kb.op("vector", lambda v, u_c=u_c, w2=w2: v.scalar_tensor_tensor(
                out=cvt[:, 1920:2047], in0=u_c[:, 1:128], scalar=w2,
                in1=cvt[:, 1920:2047], op0=ALU.mult, op1=ALU.add), None)
            kb.op("vector", lambda v, tt=tt, w2=w2: v.scalar_tensor_tensor(
                out=cvt[:, 2047:2048], in0=uh[:, tt * 2 + 1:tt * 2 + 2], scalar=w2,
                in1=cvt[:, 2047:2048], op0=ALU.mult, op1=ALU.add), None)
            kb.wait("vector", "seg", 4 * tt + 4)
            kb.bar("vector")
            kb.op("vector", lambda v, u_c=u_c, tt=tt: v.scalar_tensor_tensor(
                out=dsd[:, 0:2048], in0=u_c[:], scalar=c_bx[:, tt:tt + 1],
                in1=cvt[:, 0:2048], op0=ALU.add, op1=ALU.subtract), None)
            kb.bar("vector")
            kb.op("vector", lambda v: v.tensor_mul(dsd[:, 0:2048], dsd[:, 0:2048], psumB[:, 0:2048]), None)
            kb.bar("vector")
            kb.wait("vector", "qm")
            kb.op("vector", lambda v, um=um: v.tensor_add(um[:], dsd[:, 0:2048], cvt[:, 0:2048]), "vc")
            kb.wait("scalar", "vc")
            kb.dma("scalar", udram[tt * 128:(tt + 1) * 128, :], um[:], "qm")
        v_qm = kb.val("qm")
        if DEBUG:
            kb.wait("sync", "vc")
            kb.dma("sync", tap_uh, uh[:], "qtap")

        mark("proj")
        kb.wait("sync", "vc")
        kb.dma("sync", cfb_f[:], ins["fbB"], "qfb")
        v_qf_fb = kb.val("qfb")
        kb.wait("tensor", "qmt", v_qf_mt)
        kb.wait("tensor", "seu")
        kb.wait("tensor", "seg")
        kb.wait("tensor", "vc")
        for tt in range(NT):
            pa = psumA if tt % 2 == 0 else psumB
            kb.wait("tensor", "sf")
            for kk in range(ND):
                for nb in range(2):
                    kb.op("tensor", lambda t, tt=tt, kk=kk, nb=nb, pa=pa: t.matmul(
                        pa[:, nb * 512:(nb + 1) * 512],
                        xnbf[:, kk * TL + tt * 128:kk * TL + (tt + 1) * 128],
                        mtres[:, kk * 1024 + nb * 512:kk * 1024 + (nb + 1) * 512],
                        start=(kk == 0), stop=(kk == ND - 1)),
                        "tf" if (kk == ND - 1 and nb == 1) else None)
            kb.wait("vector", "tf")
            kb.wait("vector", "qfb", v_qf_fb)
            kb.op("vector", lambda v, tt=tt, pa=pa: v.tensor_add(
                yffn[:, tt * 1024:(tt + 1) * 1024], pa[:, 0:1024], cfb_f[:]), "sf")
        kb.wait("scalar", "sf")
        for tt in range(NT):
            kb.op("scalar", lambda s, tt=tt: s.activation(
                out=yffn[:, tt * 1024:(tt + 1) * 1024],
                in_=yffn[:, tt * 1024:(tt + 1) * 1024], func=AF.Silu), "sf2")
        v_sf2 = kb.val("sf2")

        mark("ffn")
        kb.wait("vector", "vd", v_vd_dt)
        kb.wait("vector", "vb", v_vb_af)
        kb.op("vector", lambda v: v.tensor_tensor(
            out=lnad[:].rearrange("p (t s) -> p t s", s=S),
            in0=a_f[:].rearrange("p (t s) -> p t s", s=S),
            in1=dtv[:].unsqueeze(2).broadcast_to([128, NT, S]),
            op=ALU.mult), "vb")
        kb.wait("scalar", "vb")
        kb.op("scalar", lambda s: s.activation(out=adt[:], in_=lnad[:], func=AF.Exp), "sb")
        kb.op("vector", lambda v: v.tensor_scalar_add(rca[:], a_f[:], 1e-6), None)
        kb.bar("vector")
        kb.op("vector", lambda v: v.reciprocal(rca[:], rca[:]), "vb")
        kb.wait("vector", "sb")
        kb.op("vector", lambda v: v.tensor_scalar_add(bcn[:], adt[:], -1.0), None)
        kb.bar("vector")
        kb.op("vector", lambda v: v.tensor_mul(bcn[:], bcn[:], rca[:]), None)
        kb.bar("vector")
        kb.op("vector", lambda v: v.tensor_mul(bcn[:], bcn[:], c_bn[:]), None)
        kb.bar("vector")
        kb.op("vector", lambda v: v.tensor_mul(bcn[:], bcn[:], c_cn[:]), "vb")
        kb.bar("vector")

        def dense_tile(tt):
            sl = slice(tt * 16, (tt + 1) * 16)
            lnsl = lnad[:, sl]
            bcsl = bcn[:, sl]
            cdb = cd0 if tt % 2 == 0 else cd1
            if tt >= 2:
                kb.wait("vector", "qpd" + str(tt % 2), 16 * (tt // 2))
            kb.op("vector", lambda v, lnsl=lnsl: v.tensor_tensor(
                out=dsc1[:], in0=c_tcwv[:],
                in1=lnsl.unsqueeze(1).broadcast_to([128, Q, S]), op=ALU.mult), "vb")
            kb.wait("scalar", "vb")
            kb.op("scalar", lambda s, cdb=cdb: s.activation(
                out=cdb[:].rearrange("p (q f r) -> p q f r", q=16, f=4)[:, :, 0, :],
                in_=dsc1[:], func=AF.Exp), "sb")
            kb.wait("vector", "sb")
            kb.op("vector", lambda v, lnsl=lnsl: v.tensor_tensor(
                out=dsc1[:], in0=c_tcs2[:],
                in1=lnsl.unsqueeze(2).broadcast_to([128, S, Q]), op=ALU.mult), "vb")
            kb.wait("scalar", "vb")
            kb.op("scalar", lambda s: s.activation(out=dsc2[:], in_=dsc1[:], func=AF.Exp), "sb")
            kb.wait("vector", "sb")
            kb.op("vector", lambda v, cdb=cdb: v.tensor_copy(
                cdb[:].rearrange("p (q f r) -> p q f r", q=16, f=4)[:, :, 3, :],
                dsc2[:].rearrange("p (q r) -> p q r", q=16)), "vb")
            kb.op("vector", lambda v, bcsl=bcsl, cdb=cdb: v.tensor_tensor(
                out=cdb[:].rearrange("p (q f r) -> p q f r", q=16, f=4)[:, :, 1, :],
                in0=dsc2[:].rearrange("p (q r) -> p q r", q=16),
                in1=bcsl.unsqueeze(2).broadcast_to([128, S, Q]), op=ALU.mult), "vb")
            kb.wait("vector", "sb")
            kb.op("vector", lambda v, lnsl=lnsl: v.tensor_tensor(
                out=dsc1[:], in0=c_tck8[:],
                in1=lnsl.unsqueeze(1).broadcast_to([128, Q, S]), op=ALU.mult), "vb")
            kb.wait("scalar", "vb")
            kb.op("scalar", lambda s: s.activation(out=dsc2[:], in_=dsc1[:], func=AF.Exp), "sb")
            kb.wait("vector", "sb")
            kb.op("vector", lambda v, bcsl=bcsl: v.tensor_tensor(
                out=dsc2[:].rearrange("p (q r) -> p q r", q=16),
                in0=dsc2[:].rearrange("p (q r) -> p q r", q=16),
                in1=bcsl.unsqueeze(1).broadcast_to([128, Q, S]), op=ALU.mult), None)
            kb.bar("vector")
            kb.op("vector", lambda v: v.memset(kpad[:, 0:15], 0.0), None)
            kb.op("vector", lambda v: v.tensor_reduce(
                out=kpad[:, 15:31], in_=dsc2[:].rearrange("p (q r) -> p q r", q=16),
                axis=AXX, op=ALU.add), "vb")
            kb.bar("vector")
            for tq in range(Q):
                kb.op("vector", lambda v, tq=tq, cdb=cdb: v.tensor_copy(
                    cdb[:, tq * 64 + 32:tq * 64 + 48],
                    kpad[:, 15 - tq:31 - tq]), "vb")
            kb.wait("scalar", "vb")
            kb.wait("scalar", "sb")
            kb.dma("scalar", pdram[tt * 128:(tt + 1) * 128, :], cdb[:],
                   "qpd" + str(tt % 2))

        def strips_load(tt):
            kb.dma("gpsimd", strps[tt % 2].rearrange("p (g f) -> p g f", g=16),
                   pdram.rearrange("(t g i) (q f) -> t i q g f", t=NT, g=16, q=16)[tt],
                   "qs" + str(tt % 2))

        def g_load(tt):
            kb.dma("sync", gbs[tt % 2].rearrange("p (g j) -> p g j", g=16),
                   udram.rearrange("(t g i) (tau j) -> t i tau g j", t=NT, g=16, tau=16)[tt],
                   "qg" + str(tt % 2))

        def expand(dst, fam, strp, sem):
            kb.op("vector", lambda v, dst=dst, fam=fam, strp=strp: v.tensor_tensor(
                out=dst.rearrange("p (g i r) -> p g i r", i=8, r=16),
                in0=c_dm[:].rearrange("p (i r) -> p i r", r=16)
                          .unsqueeze(1).broadcast_to([128, 16, 8, 16]),
                in1=strp.rearrange("p (g f) -> p g f", g=16)[:, :, fam * 16:(fam + 1) * 16]
                        .unsqueeze(2).broadcast_to([128, 16, 8, 16]),
                op=ALU.mult), sem)

        def aq_build_to(dst, strp, sem):
            kb.op("vector", lambda v, dst=dst, strp=strp: v.tensor_tensor(
                out=dst.rearrange("p (g j) -> p g j", g=16),
                in0=c_rst[:].rearrange("p (g j) -> p g j", g=16),
                in1=strp.rearrange("p (g f) -> p g f", f=64)[:, :, 63]
                        .unsqueeze(2).broadcast_to([128, 16, 128]),
                op=ALU.mult), sem)

        def aq_build(strp, sem):
            aq_build_to(aq16, strp, sem)

        kb.wait("gpsimd", "sf2", v_sf2)
        for tt in range(NT):
            kb.wait("gpsimd", "qpd" + str(tt % 2), 16 * (tt // 2 + 1))
            if tt >= 2:
                kb.wait("gpsimd", "vw", 2 * (tt - 1))
            strips_load(tt)
        kb.wait("sync", "qm", v_qm)
        kb.wait("sync", "sf2", v_sf2)
        for tt in range(NT):
            if tt >= 2:
                kb.wait("sync", "tv", 16 * (tt - 1))
            g_load(tt)

        kb.wait("tensor", "sf")
        kb.wait("tensor", "seu")
        kb.wait("tensor", "seg")
        kb.wait("tensor", "vc")

        wvb = [bigA[:, 0:2048], bigA[:, 2048:4096]]
        aqb = [bigA[:, 20480:22528], bigA[:, 22528:24576]]
        pVs = [psumA, psumB]
        TV0 = kb.val("tv")
        VS0 = kb.val("vs")

        def s1_front(tt):
            strp = strps[tt % 2]
            kb.wait("vector", "qs" + str(tt % 2), 16 * (tt // 2 + 1))
            if tt >= 2:
                kb.wait("vector", "tv", TV0 + 16 * (tt - 1))
            expand(wvb[tt % 2], 0, strp, "vw")
            aq_build_to(aqb[tt % 2], strp, "vw")
            kb.wait("tensor", "vw")
            kb.wait("tensor", "qg" + str(tt % 2), 16 * (tt // 2 + 1))
            if tt >= 2:
                kb.wait("tensor", "vs", VS0 + 2 * (tt - 1))
            for g in range(16):
                kb.op("tensor", lambda t, g=g, tt=tt: t.matmul(
                    pVs[tt % 2][:, g * 128:(g + 1) * 128],
                    wvb[tt % 2][:, g * 128:(g + 1) * 128],
                    gbs[tt % 2][:, g * 128:(g + 1) * 128],
                    start=True, stop=True), "tv")

        def s1_back(tt):
            kb.wait("vector", "tv", TV0 + 16 * (tt + 1))
            kb.wait("vector", "vw")
            if tt >= 1:
                kb.wait("vector", "vs", VS0 + 2 * tt)
            kb.op("vector", lambda v, tt=tt: v.tensor_tensor_scan(
                out=sout[:, 0:2048], data0=aqb[tt % 2][:, 0:2048],
                data1=pVs[tt % 2][:, 0:2048],
                initial=0.0, op0=ALU.mult, op1=ALU.add), "vs")
            kb.bar("vector")
            kb.op("vector", lambda v, tt=tt: v.tensor_copy(
                sendst[:, tt * 16:(tt + 1) * 16],
                sout.rearrange("p (g j) -> p g j", j=128)[:, :, 127]), "vs")

        for k in range(NT + 1):
            if k < NT:
                dense_tile(k)
                s1_front(k)
            if k >= 1:
                s1_back(k - 1)
        mark("dense")
        kb.bar("vector")
        kb.op("vector", lambda v: v.tensor_scalar_mul(sendst[:], sendst[:], c_cm[:, 0:1]), "vs")
        kb.bar("vector")
        kb.wait("scalar", "vs")
        kb.dma("scalar", ssin, sendst[:], "qar")
        kb.wait("gpsimd", "qar")
        kb.op("gpsimd", lambda g: g.collective_compute(
            "AllReduce", ALU.add, replica_groups=[[0, 1], [2, 3], [4, 5], [6, 7]],
            ins=[ssin], outs=[ssout]), "gar")
        v_gar_ss = kb.val("gar")
        kb.wait("sync", "gar", v_gar_ss)
        kb.dma("sync", shall[:], ssout, "qar")
        if DEBUG:
            kb.dma("sync", tap_sh, ssout, "qtap")
        kb.wait("vector", "qar")
        kb.op("vector", lambda v: v.tensor_scalar_mul(sheff[:], shall[:], c_cm[:, 1:2]), "vs")
        kb.bar("vector")
        kb.op("vector", lambda v: v.tensor_copy(sheffb[:], sheff[:]), "vs")
        kb.bar("vector")
        v_vs_sh = kb.val("vs")

        mark("s1")
        v_vw_s1 = kb.val("vw")
        for tt in range(NT):
            if tt >= 2:
                kb.wait("gpsimd", "vw", v_vw_s1 + 3 * (tt - 1))
            strips_load(tt)
        for tt in range(NT):
            if tt >= 2:
                kb.wait("sync", "ty", tt - 1)
            g_load(tt)

        kb.wait("vector", "vs", v_vs_sh)
        for tt in range(NT):
            strp = strps[tt % 2]
            kb.wait("vector", "qs" + str(tt % 2), 16 * (NT // 2 + tt // 2 + 1))
            kb.wait("vector", "tv")
            kb.wait("vector", "ty")
            kb.op("vector", lambda v, strp=strp: v.tensor_tensor(
                out=bigA[:, 0:8192].rearrange("p (c i r) -> p c i r", c=64, i=8),
                in0=c_dm[:].rearrange("p (i r) -> p i r", r=16)
                          .unsqueeze(1).broadcast_to([128, 64, 8, 16]),
                in1=strp.rearrange("p (c r) -> p c r", r=16)
                        .unsqueeze(2).broadcast_to([128, 64, 8, 16]),
                op=ALU.mult), "vw")
            aq_build(strp, "vw")
            kb.wait("tensor", "vw")
            kb.wait("tensor", "qg" + str(tt % 2), 16 * (NT // 2 + tt // 2 + 1))
            kb.wait("tensor", "vs")
            for g in range(16):
                kb.op("tensor", lambda t, g=g, tt=tt: t.matmul(
                    psumA[:, g * 128:(g + 1) * 128],
                    bigA[:, g * 512:g * 512 + 128],
                    gbs[tt % 2][:, g * 128:(g + 1) * 128],
                    start=True, stop=True), "tv")
            kb.wait("vector", "tv")
            kb.op("vector", lambda v, tt=tt, strp=strp: v.tensor_tensor(
                out=vfix[:], in0=sheff[:, tt * 16:(tt + 1) * 16],
                in1=strp.rearrange("p (g f) -> p g f", f=64)[:, :, 63],
                op=ALU.mult), "vw")
            kb.bar("vector")
            kb.op("vector", lambda v: v.tensor_add(
                psumA[:, 0:2048].rearrange("p (g j) -> p g j", j=128)[:, :, 0],
                psumA[:, 0:2048].rearrange("p (g j) -> p g j", j=128)[:, :, 0],
                vfix[:]), None)
            kb.bar("vector")
            kb.wait("vector", "vw")
            kb.op("vector", lambda v: v.tensor_tensor_scan(
                out=sout[:, 0:2048], data0=aq16[:, 0:2048], data1=psumA[:, 0:2048],
                initial=0.0, op0=ALU.mult, op1=ALU.add), "vs")
            kb.wait("tensor", "vs")
            kb.wait("tensor", "sy")
            for g in range(16):
                kb.op("tensor", lambda t, g=g, tt=tt: t.matmul(
                    psumB[:, g * 128:g * 128 + 1],
                    bigA[:, g * 512 + 128:g * 512 + 256],
                    sheffb[:, tt * 16 + g:tt * 16 + g + 1],
                    start=True, stop=False), None)
                kb.op("tensor", lambda t, g=g: t.matmul(
                    psumB[:, g * 128 + 1:(g + 1) * 128],
                    bigA[:, g * 512 + 128:g * 512 + 256],
                    sout[:, g * 128:g * 128 + 127],
                    start=False, stop=False), None)
                kb.op("tensor", lambda t, g=g, tt=tt: t.matmul(
                    psumB[:, g * 128:(g + 1) * 128],
                    bigA[:, g * 512 + 256:g * 512 + 384],
                    gbs[tt % 2][:, g * 128:(g + 1) * 128],
                    start=False, stop=True), "ty" if g == 15 else None)
            kb.wait("scalar", "ty")
            kb.wait("scalar", "qyd")
            for nb in range(4):
                kb.op("scalar", lambda s, tt=tt, nb=nb: s.copy(
                    out=ygbs[tt % 2][:, nb * 512:(nb + 1) * 512],
                    in_=psumB[:, nb * 512:(nb + 1) * 512]), "sy")
            kb.wait("scalar", "sy")
            kb.dma("scalar",
                   ydram.rearrange("(t g i) (tp j) -> t i tp g j", t=NT, g=16, tp=16)[tt],
                   ygbs[tt % 2].rearrange("p (g j) -> p g j", g=16), "qyd")
        v_qyd = kb.val("qyd")

        mark("s3")
        kb.wait("gpsimd", "vs")
        kb.wait("gpsimd", "ty")
        kb.wait("gpsimd", "qyd", v_qyd)
        for kk in range(NT):
            kb.dma("gpsimd", wob[:, kk * 1024:(kk + 1) * 1024],
                   ins["woutt"][kk * 128:(kk + 1) * 128, :], "qw")
        v_qw_wo = kb.val("qw")

        kb.wait("sync", "ty")
        kb.dma("sync", cbout_t[:], ins["boutB"], "qc")
        kb.dma("sync", clng_t[:], ins["lngB"], "qc")
        kb.dma("sync", clnb_t[:], ins["lnbB"], "qc")
        v_qc_t = kb.val("qc")

        ybuf = [bigB[:, 0:2048], bigB[:, 2048:4096]]
        kb.wait("sync", "qyd", v_qyd)
        if True:
            for tl in range(NT):
                tt = tl
                pa = psumA if tl % 2 == 0 else psumB
                if tt >= 2:
                    kb.wait("sync", "tt2", tt - 1)
                kb.dma("sync", ybuf[tt % 2].rearrange("p (k c) -> p k c", c=128),
                       ydram.rearrange("(k p) (t c) -> t p k c", k=NT, t=NT)[tt], "qyl")
                if tt >= 2:
                    kb.wait("sync", "vt", 7 * (tt - 2) + 7)
                kb.dma("sync", xts[tl % 2][:], ins["xin"][tt * 128:(tt + 1) * 128, :],
                       "qxr" + str(tl % 2))
                kb.wait("tensor", "qyl", 16 * (tt + 1))
                if tt == 0:
                    kb.wait("tensor", "qw", v_qw_wo)
                if tt >= 2:
                    kb.wait("tensor", "vt", 7 * (tt - 2) + 2)
                for kk in range(NT):
                    for nb in range(2):
                        kb.op("tensor", lambda t, kk=kk, nb=nb, tt=tt, pa=pa: t.matmul(
                            pa[:, nb * 512:(nb + 1) * 512],
                            ybuf[tt % 2][:, kk * 128:(kk + 1) * 128],
                            wob[:, kk * 1024 + nb * 512:kk * 1024 + (nb + 1) * 512],
                            start=(kk == 0), stop=(kk == NT - 1)),
                            "tt2" if (kk == NT - 1 and nb == 1) else None)
                kb.wait("vector", "qxr" + str(tl % 2), 16 * (tl // 2 + 1))
                kb.op("vector", lambda v, tl=tl, tt=tt: v.tensor_scalar(
                    out=xnt[:], in0=xts[tl % 2][:],
                    scalar1=mstats[:, tt:tt + 1], scalar2=mstats[:, 16 + tt:16 + tt + 1],
                    op0=ALU.subtract, op1=ALU.mult), "vt")
                kb.bar("vector")
                kb.wait("vector", "tt2")
                kb.wait("vector", "qc", v_qc_t)
                kb.op("vector", lambda v, pa=pa: v.tensor_add(rtk[:], pa[:, 0:1024], xnt[:]), "vt")
                kb.bar("vector")
                kb.op("vector", lambda v: v.tensor_add(rtk[:], rtk[:], cbout_t[:]), "vt")
                kb.bar("vector")
                kb.op("vector", lambda v: v.reduce_sum(stat[:, 8:9], rtk[:], axis=AXX), "vt")
                kb.bar("vector")
                kb.wait("scalar", "vt")
                kb.op("scalar", lambda s: s.activation(out=sqsc[:], in_=rtk[:], func=AF.Square,
                                                       accum_out=stat[:, 9:10]), "st2")
                kb.wait("vector", "st2")
                kb.op("vector", lambda v: v.tensor_scalar_mul(stat[:, 8:9], stat[:, 8:9], 1.0 / D), None)
                kb.op("vector", lambda v: v.tensor_scalar_mul(stat[:, 9:10], stat[:, 9:10], 1.0 / D), None)
                kb.bar("vector")
                kb.op("vector", lambda v: v.tensor_mul(stat[:, 10:11], stat[:, 8:9], stat[:, 8:9]), None)
                kb.bar("vector")
                kb.op("vector", lambda v: v.tensor_sub(stat[:, 9:10], stat[:, 9:10], stat[:, 10:11]), "vt")
                kb.wait("scalar", "vt")
                kb.op("scalar", lambda s: s.activation(out=stat[:, 9:10], in_=stat[:, 9:10],
                                                       func=AF.Ln, bias=c_eps[:, 0:1]), None)
                kb.bar("scalar")
                kb.op("scalar", lambda s: s.activation(out=stat[:, 10:11], in_=stat[:, 9:10],
                                                       func=AF.Exp, scale=-0.5), "st2")
                kb.wait("vector", "st2")
                kb.bar("vector")
                kb.op("vector", lambda v: v.tensor_scalar(
                    out=snm[:], in0=rtk[:], scalar1=stat[:, 8:9], scalar2=stat[:, 10:11],
                    op0=ALU.subtract, op1=ALU.mult), "vt")
                kb.bar("vector")
                kb.op("vector", lambda v: v.tensor_mul(t3t[:], snm[:], clng_t[:]), None)
                kb.bar("vector")
                kb.op("vector", lambda v: v.tensor_add(t3t[:], t3t[:], clnb_t[:]), None)
                kb.bar("vector")
                kb.wait("vector", "sf2", v_sf2)
                kb.op("vector", lambda v, tt=tt: v.tensor_mul(
                    ptt[:], t3t[:], yffn[:, tt * 1024:(tt + 1) * 1024]), None)
                kb.bar("vector")
                kb.wait("vector", "qo")
                kb.op("vector", lambda v, tl=tl: v.tensor_add(ot0[:], ptt[:], xts[tl % 2][:]), "vt")
                kb.wait("sync", "vt")
                kb.dma("sync", xout[tt * 128:(tt + 1) * 128, :], ot0[:], "qo")
        kb.wait("sync", "qo")
        mark("tail")

        if CUT is not None:
            cut = marks[CUT]
            for e in kb.ENGS:
                kb.ops[e] = kb.ops[e][:cut[e]]
        block = E(nc.Block())

        @block.sync
        def _(sync):
            for f in kb.ops["sync"]:
                f(sync)

        @block.scalar
        def _(scalar):
            for f in kb.ops["scalar"]:
                f(scalar)

        @block.vector
        def _(vector):
            for f in kb.ops["vector"]:
                f(vector)

        @block.tensor
        def _(tensor):
            for f in kb.ops["tensor"]:
                f(tensor)

        @block.gpsimd
        def _(gpsimd):
            for f in kb.ops["gpsimd"]:
                f(gpsimd)

    return nc


def kernel(**inputs):
    in_maps = _host_inputs(inputs)
    nc = build_nc()
    res = run_bass_kernel_spmd(nc, in_maps, core_ids=list(range(NCORE)))
    perm = _tt_perm()
    out = np.zeros((B, L, D), np.float32)
    for c in range(NCORE):
        b, half = c >> 1, c & 1
        out[b, half * TL + perm] = res.results[c]["xout"]
    return out


# version 06
# speedup vs baseline: 1.1928x; 1.1928x over previous
import contextlib
import numpy as np
import ml_dtypes
import concourse.bass as bass
import concourse.mybir as mybir
from concourse.bass_utils import run_bass_kernel_spmd

F32 = mybir.dt.float32
BF16 = mybir.dt.bfloat16
AF = mybir.ActivationFunctionType
ALU = mybir.AluOpType
AXX = mybir.AxisListType.X

B, L, D, H, S, Q = 4, 4096, 1024, 2048, 16, 16
TL = 2048
NJ = TL // Q
NT = H // 128
ND = D // 128
NCORE = 8
EPS = 1e-5

DEBUG = False
CUT = None


def _tt_perm():
    slots = np.arange(TL)
    tau, j = slots // NJ, slots % NJ
    return j * Q + tau


def _rep(v):
    v = np.asarray(v, np.float32).reshape(1, -1)
    return np.ascontiguousarray(np.broadcast_to(v, (128, v.shape[1])))


def _host_inputs(inputs):
    perm = _tt_perm()
    x = np.ascontiguousarray(np.asarray(inputs["L0_attn_x"], np.float32))

    k = np.arange(D)
    ang = 2.0 * np.pi * np.outer(k, k) / D
    cdf = np.cos(ang).astype(np.float32)
    sdf = np.sin(ang).astype(np.float32)
    cd = cdf.astype(ml_dtypes.bfloat16)
    sd = sdf.astype(ml_dtypes.bfloat16)
    sn = (-sdf).astype(ml_dtypes.bfloat16)

    qq, rr = np.arange(Q), np.arange(Q)
    tcwv = _rep(np.broadcast_to((Q - 1.0 - qq)[:, None], (Q, Q)).reshape(-1))
    tcs2 = _rep(np.broadcast_to((rr + 1.0)[None, :], (Q, Q)).reshape(-1))
    tck8 = _rep(np.broadcast_to((qq + 0.0)[:, None], (Q, Q)).reshape(-1))

    dm = np.zeros((128, 128), np.float32)
    for p in range(128):
        dm[p, (p // 16) * 16:(p // 16 + 1) * 16] = 1.0
    dm = dm.astype(ml_dtypes.bfloat16)
    rst = np.ones((128, NT * NJ), np.float32)
    rst[:, ::NJ] = 0.0
    rst = rst.astype(ml_dtypes.bfloat16)

    def t2(w):
        return np.ascontiguousarray(np.asarray(w, np.float32).T)

    wxt, wgt, wdtt = t2(inputs["L0_attn_W_x"]), t2(inputs["L0_attn_W_g"]), t2(inputs["L0_attn_W_dt"])
    woutt = np.ascontiguousarray(np.asarray(inputs["L0_attn_W_out"], np.float32).T)
    wrt = np.ascontiguousarray(np.asarray(inputs["L0_ffn_w_freq_real"], np.float32).T)
    wit = np.ascontiguousarray(np.asarray(inputs["L0_ffn_w_freq_imag"], np.float32).T)

    def b2(b):
        return np.ascontiguousarray(np.asarray(b, np.float32).reshape(NT, 128).T)

    bx2, bg2, bdt2 = b2(inputs["L0_attn_b_x"]), b2(inputs["L0_attn_b_g"]), b2(inputs["L0_attn_b_dt"])
    cw = np.asarray(inputs["L0_attn_conv_w"], np.float32)
    convw2 = np.concatenate([b2(cw[:, 0, t]) for t in range(3)], axis=1)
    convb2 = b2(inputs["L0_attn_conv_b"])

    def hs2(a):
        return np.ascontiguousarray(np.asarray(a, np.float32)
                                    .reshape(NT, 128, S).transpose(1, 0, 2).reshape(128, NT * S))

    anat, bnat, cnat = hs2(inputs["L0_attn_A_raw"]), hs2(inputs["L0_attn_B_ssm"]), hs2(inputs["L0_attn_C_ssm"])

    boutB = _rep(inputs["L0_attn_b_out"])
    lngB = _rep(inputs["L0_attn_ln_g"])
    lnbB = _rep(inputs["L0_attn_ln_b"])
    fbB = _rep(inputs["L0_ffn_bias"])
    idf = np.eye(128, dtype=np.float32)

    in_maps = []
    for c in range(NCORE):
        b, half = c >> 1, c & 1
        t0 = half * TL
        xs = np.ascontiguousarray(x[b, t0 + perm])
        halo = np.zeros((2, D), np.float32)
        hm = np.zeros(2, np.float32)
        if t0 - 1 >= 0:
            halo[0] = x[b, t0 - 1]
            hm[0] = 1.0
        if t0 + TL < L:
            halo[1] = x[b, t0 + TL]
            hm[1] = 1.0
        cm = np.array([1.0 - half, float(half)], np.float32)
        sel16 = np.zeros((1, 16), np.float32)
        sel16[0, c] = 1.0
        sel16[0, 8 + c] = 1.0
        sel8 = np.zeros((1, 8), np.float32)
        sel8[0, c] = 1.0
        m = dict(
            xin=xs, xhalo=halo, halomask=_rep(hm), cmask=_rep(cm),
            wxt=wxt, wgt=wgt, wdtt=wdtt, woutt=woutt,
            wrtc=np.ascontiguousarray(wrt[:, c * 128:(c + 1) * 128]),
            witc=np.ascontiguousarray(wit[:, c * 128:(c + 1) * 128]),
            cdft=cd, sdft=sd, sneg=sn,
            cdcol=np.ascontiguousarray(cd[:, c * 128:(c + 1) * 128]),
            sdcol=np.ascontiguousarray(sd[:, c * 128:(c + 1) * 128]),
            bx2=bx2, bg2=bg2, bdt2=bdt2, convw2=convw2, convb2=convb2,
            anat=anat, bnat=bnat, cnat=cnat,
            tcwv=tcwv, tcs2=tcs2, tck8=tck8,
            dmask=dm, rstmask=rst, idf32=idf,
            boutB=boutB, lngB=lngB, lnbB=lnbB, fbB=fbB,
            sel16=_rep(sel16), sel8=_rep(sel8),
        )
        in_maps.append(m)
    return in_maps


class KB:
    ENGS = ("sync", "scalar", "vector", "tensor", "gpsimd")

    def __init__(self, nc, stack):
        self.nc = nc
        self.stack = stack
        self.ops = {e: [] for e in self.ENGS}
        self.sem = {}
        self.last = {e: (None, 0) for e in self.ENGS}

    def mksem(self, *names):
        for n in names:
            h = self.stack.enter_context(self.nc.semaphore(n))
            self.sem[n] = [h, 0]

    def dma(self, eng, out, in_, sem, **kw):
        rec = self.sem[sem]
        rec[1] += 16

        def f(e, out=out, in_=in_, h=rec[0], kw=kw):
            e.dma_start(out=out, in_=in_, **kw).then_inc(h, 16)
        self.ops[eng].append(f)

    def op(self, eng, build, sem=None):
        if sem is None:
            sem = "f" + eng[0]
        rec = self.sem[sem]
        rec[1] += 1
        h = rec[0]
        self.last[eng] = (sem, rec[1])

        def f(e, build=build, h=h):
            build(e).then_inc(h, 1)
        self.ops[eng].append(f)

    def bar(self, eng):
        name, v = self.last[eng]
        if v <= 0:
            return
        rec = self.sem[name]
        self.ops[eng].append(lambda e, h=rec[0], v=v: e.wait_ge(h, v))

    def wait(self, eng, sem, val=None):
        rec = self.sem[sem]
        v = rec[1] if val is None else val
        if v <= 0:
            return
        self.ops[eng].append(lambda e, h=rec[0], v=v: e.wait_ge(h, v))

    def val(self, sem):
        return self.sem[sem][1]


def build_nc():
    nc = bass.Bass("TRN2", target_bir_lowering=False, debug=False)
    DT = nc.dram_tensor
    skind = "ExternalOutput" if DEBUG else "Internal"

    ins = {}
    for name, shape, dt in [
        ("xin", [TL, D], F32), ("xhalo", [2, D], F32),
        ("halomask", [128, 2], F32), ("cmask", [128, 2], F32),
        ("wxt", [D, H], F32), ("wgt", [D, H], F32), ("wdtt", [D, H], F32),
        ("woutt", [H, D], F32), ("wrtc", [D, 128], F32), ("witc", [D, 128], F32),
        ("cdft", [D, D], BF16), ("sdft", [D, D], BF16), ("sneg", [D, D], BF16),
        ("cdcol", [D, 128], BF16), ("sdcol", [D, 128], BF16),
        ("bx2", [128, NT], F32), ("bg2", [128, NT], F32), ("bdt2", [128, NT], F32),
        ("convw2", [128, 3 * NT], F32), ("convb2", [128, NT], F32),
        ("anat", [128, NT * S], F32), ("bnat", [128, NT * S], F32), ("cnat", [128, NT * S], F32),
        ("tcwv", [128, 256], F32), ("tcs2", [128, 256], F32), ("tck8", [128, 256], F32),
        ("dmask", [128, 128], BF16), ("rstmask", [128, NT * NJ], BF16),
        ("idf32", [128, 128], F32),
        ("boutB", [128, D], F32), ("lngB", [128, D], F32), ("lnbB", [128, D], F32),
        ("fbB", [128, D], F32),
        ("sel16", [128, 16], F32), ("sel8", [128, 8], F32),
    ]:
        ins[name] = DT(name, shape, dt, kind="ExternalInput").ap()

    xout = DT("xout", [TL, D], F32, kind="ExternalOutput").ap()
    udram = DT("udram", [H, TL], BF16, kind=skind).ap()
    ydram = DT("ydram", [H, TL], BF16, kind=skind).ap()
    pdram = DT("pdram", [H, 1024], F32, kind=skind).ap()
    dtin = DT("dtin", [128, NT], F32, kind="Internal").ap()
    dtout = DT("dtout", [128, NT], F32, kind="Internal", addr_space="Shared").ap()
    u12in = DT("u12in", [16 * 128, D], BF16, kind="Internal").ap()
    u12out = DT("u12out", [16 * 128, D], BF16, kind="Internal", addr_space="Shared").ap()
    mtin = DT("mtin", [D, D], BF16, kind="Internal").ap()
    mtout = DT("mtout", [D, D], BF16, kind="Internal", addr_space="Shared").ap()
    ssin = DT("ssin", [128, NT * S], F32, kind="Internal").ap()
    if DEBUG:
        tap_dt = DT("tap_dt", [128, NT], F32, kind="ExternalOutput").ap()
        tap_xn = DT("tap_xn", [128, 16384], BF16, kind="ExternalOutput").ap()
        tap_u = DT("tap_u", [128, TL], F32, kind="ExternalOutput").ap()
        tap_uh = DT("tap_uh", [128, 2 * NT], F32, kind="ExternalOutput").ap()
        tap_u12 = DT("tap_u12", [16 * 128, D], BF16, kind="ExternalOutput").ap()
        tap_ms = DT("tap_ms", [128, 32], F32, kind="ExternalOutput").ap()
        tap_mt = DT("tap_mt", [D, D], BF16, kind="ExternalOutput").ap()
        tap_sh = DT("tap_sh", [128, NT * S], F32, kind="ExternalOutput").ap()
    ssout = DT("ssout", [128, NT * S], F32, kind="Internal").ap()

    with contextlib.ExitStack() as stack:
        E = stack.enter_context

        c_hm = E(nc.sbuf_tensor("c_hm", [128, 2], F32))
        c_cm = E(nc.sbuf_tensor("c_cm", [128, 2], F32))
        c_bx = E(nc.sbuf_tensor("c_bx", [128, NT], F32))
        c_bg = E(nc.sbuf_tensor("c_bg", [128, NT], F32))
        c_bdt = E(nc.sbuf_tensor("c_bdt", [128, NT], F32))
        c_cw = E(nc.sbuf_tensor("c_cw", [128, 3 * NT], F32))
        c_cb = E(nc.sbuf_tensor("c_cb", [128, NT], F32))
        c_an = E(nc.sbuf_tensor("c_an", [128, 256], F32))
        c_bn = E(nc.sbuf_tensor("c_bn", [128, 256], F32))
        c_cn = E(nc.sbuf_tensor("c_cn", [128, 256], F32))
        c_tcwv = E(nc.sbuf_tensor("c_tcwv", [128, 256], F32))
        c_tcs2 = E(nc.sbuf_tensor("c_tcs2", [128, 256], F32))
        c_tck8 = E(nc.sbuf_tensor("c_tck8", [128, 256], F32))
        c_dm = E(nc.sbuf_tensor("c_dm", [128, 128], BF16))
        c_rst = E(nc.sbuf_tensor("c_rst", [128, NT * NJ], BF16))
        c_id = E(nc.sbuf_tensor("c_id", [128, 128], F32))
        c_s16 = E(nc.sbuf_tensor("c_s16", [128, 16], F32))
        c_s8 = E(nc.sbuf_tensor("c_s8", [128, 8], F32))

        bigA = E(nc.sbuf_tensor("bigA", [128, 24576], BF16))
        bigB = E(nc.sbuf_tensor("bigB", [128, 32768], BF16))
        yffn = E(nc.sbuf_tensor("yffn", [128, NT * 1024], BF16))

        xnbf = bigA[:, 0:16384]
        mtres = bigA[:, 16384:24576]
        wv16 = bigA[:, 0:2048]
        ws16 = bigA[:, 2048:4096]
        wi16 = bigA[:, 4096:6144]
        aq16 = bigA[:, 20480:22528]
        strps = [bigA[:, 8192:9216], bigA[:, 9216:10240]]
        gbs = [bigA[:, 10240:12288], bigA[:, 12288:14336]]
        ygbs = [bigA[:, 14336:16384], bigA[:, 16384:18432]]
        sout = bigA[:, 18432:20480]
        wob = bigA[:, 0:16384]

        bigBf = bigB[:].bitcast(F32)
        ut = [bigBf[:, 0:2048], bigBf[:, 2048:4096]]
        cvt = bigBf[:, 4096:6144]
        dsd = bigBf[:, 6144:8192]
        ums = [bigB[:, 16384:18432], bigB[:, 18432:20480]]
        csb = bigB[:, 20480:23552]
        wfb = bigB[:, 23552:25600]
        u1b = bigB[:, 25600:26624]
        u2b = bigB[:, 26624:27648]
        mtblk = bigB[:, 27648:28672]
        stg = bigB[:, 28672:29696]
        mcs = bigB[:, 20480:22528]
        u12mv = bigB[:, 22528:23552]
        stg2 = bigB[:, 22528:23552]
        wpju = [bigB[:, 23552:24576], bigB[:, 24576:25600]]
        wpjg = [bigB[:, 29696:30720], bigB[:, 30720:31744]]
        cfb_f = bigBf[:, 0:1024]
        yhalf = bigB[:, 0:16384]
        tailf = bigBf
        rtk = tailf[:, 8192:9216]
        snm = tailf[:, 9216:10240]
        cbout_t = tailf[:, 10240:11264]
        ptt = tailf[:, 11264:12288]
        clng_t = tailf[:, 12288:13312]
        ot0 = tailf[:, 13312:14336]
        t3t = tailf[:, 14336:15360]
        clnb_t = tailf[:, 15360:16384]

        xt0 = E(nc.sbuf_tensor("xt0", [128, D], F32))
        xt1 = E(nc.sbuf_tensor("xt1", [128, D], F32))
        xts = [xt0, xt1]
        xnt = E(nc.sbuf_tensor("xnt", [128, D], F32))
        xh = E(nc.sbuf_tensor("xh", [128, D], F32))
        sqsc = E(nc.sbuf_tensor("sqsc", [128, D], F32))
        esc = E(nc.sbuf_tensor("esc", [128, 512], F32))
        stat = E(nc.sbuf_tensor("stat", [128, 16], F32))
        mstats = E(nc.sbuf_tensor("mstats", [128, 32], F32))
        xnhbf = E(nc.sbuf_tensor("xnhbf", [128, 16], BF16))
        uh = E(nc.sbuf_tensor("uh", [128, 2 * NT], F32))
        dtacc = E(nc.sbuf_tensor("dtacc", [128, 64], F32))
        dtv = E(nc.sbuf_tensor("dtv", [128, NT], F32))
        lnad = E(nc.sbuf_tensor("lnad", [128, 256], F32))
        a_f = E(nc.sbuf_tensor("a_f", [128, 256], F32))
        adt = E(nc.sbuf_tensor("adt", [128, 256], F32))
        bcn = E(nc.sbuf_tensor("bcn", [128, 256], F32))
        rca = E(nc.sbuf_tensor("rca", [128, 256], F32))
        dsc1 = E(nc.sbuf_tensor("dsc1", [128, 256], F32))
        dsc2 = E(nc.sbuf_tensor("dsc2", [128, 256], F32))
        cd0 = E(nc.sbuf_tensor("cd0", [128, 1024], F32))
        cd1 = E(nc.sbuf_tensor("cd1", [128, 1024], F32))
        kpad = E(nc.sbuf_tensor("kpad", [128, 31], F32))
        shall = E(nc.sbuf_tensor("shall", [128, 256], F32))
        sheff = E(nc.sbuf_tensor("sheff", [128, 256], F32))
        sendst = E(nc.sbuf_tensor("sendst", [128, 256], F32))
        vfix = E(nc.sbuf_tensor("vfix", [128, 16], F32))
        sheffb = E(nc.sbuf_tensor("sheffb", [128, 256], BF16))

        c_eps = E(nc.sbuf_tensor("c_eps", [128, 1], F32))
        c_cb2 = E(nc.sbuf_tensor("c_cb2", [128, NT], F32))
        c_mball = E(nc.sbuf_tensor("c_mball", [128, 2 * NT], F32))
        c_hm1 = E(nc.sbuf_tensor("c_hm1", [128, 2], F32))
        wearly = E(nc.sbuf_tensor("wearly", [128, 4096], BF16))
        psumA = E(nc.psum_tensor("psumA", [128, 2048], F32))
        psumB = E(nc.psum_tensor("psumB", [128, 2048], F32))

        kb = KB(nc, stack)
        marks = {}

        def mark(name):
            marks[name] = {e: len(kb.ops[e]) for e in kb.ENGS}
        kb.mksem("qc", "qx", "qw", "qm", "qpd", "qg", "qs", "qyd", "qyl",
                 "qf", "qdt", "qar", "qo", "qstg", "qxr",
                 "vx", "vxt", "tx", "sx", "tp", "tpt", "sdt", "se", "vc",
                 "vd", "gar", "vb", "sb", "vw", "tv", "vs", "ty", "sy",
                 "tf", "tfk", "tfs", "sf", "sf2", "tt2", "vt", "st2", "gp",
                 "fv", "fs", "ft", "fg", "fy",
                 "qx0", "qx1", "qwf", "qwd0", "qwd1", "qwu0", "qwu1",
                 "qwg0", "qwg1", "qg0", "qg1", "qs0", "qs1", "qxr0", "qxr1",
                 "qmt", "qfb", "qtap", "seu", "seg", "qwd2", "qwd3",
                 "qpd0", "qpd1", "qstg2", "qwe0", "qwe1",
                 "qyl0", "qyl1", "qyl2", "qyl3", "qu0", "qu1")

        for nm, dst in [("halomask", c_hm), ("cmask", c_cm), ("bx2", c_bx),
                        ("bg2", c_bg), ("bdt2", c_bdt), ("convw2", c_cw),
                        ("convb2", c_cb), ("anat", c_an), ("bnat", c_bn),
                        ("cnat", c_cn), ("tcwv", c_tcwv), ("tcs2", c_tcs2),
                        ("tck8", c_tck8), ("dmask", c_dm), ("rstmask", c_rst),
                        ("idf32", c_id), ("sel16", c_s16), ("sel8", c_s8)]:
            kb.dma("sync", dst[:], ins[nm], "qc")
        v_qc = kb.val("qc")

        kb.dma("gpsimd", wfb[:, 0:1024].rearrange("p (k d) -> p k d", d=128),
               ins["wrtc"].rearrange("(k p) d -> p k d", k=8), "qwf")
        kb.dma("gpsimd", wfb[:, 1024:2048].rearrange("p (k d) -> p k d", d=128),
               ins["witc"].rearrange("(k p) d -> p k d", k=8), "qwf")
        v_qw_wf = kb.val("qwf")
        for tt in range(2):
            kb.dma("gpsimd", wearly[:, tt * 2048:tt * 2048 + 1024].rearrange("p (k h) -> p k h", h=128),
                   ins["wxt"].rearrange("(k p) h -> p k h", k=8)[:, :, tt * 128:(tt + 1) * 128],
                   "qwe" + str(tt))
            kb.dma("gpsimd", wearly[:, tt * 2048 + 1024:tt * 2048 + 2048].rearrange("p (k h) -> p k h", h=128),
                   ins["wgt"].rearrange("(k p) h -> p k h", k=8)[:, :, tt * 128:(tt + 1) * 128],
                   "qwe" + str(tt))

        kb.wait("vector", "qc", v_qc)
        kb.op("vector", lambda v: v.tensor_add(dsc1[:, 0:NT], c_cw[:, 0:NT], c_cw[:, NT:2 * NT]), None)
        kb.bar("vector")
        kb.op("vector", lambda v: v.tensor_add(dsc1[:, 0:NT], dsc1[:, 0:NT], c_cw[:, 2 * NT:3 * NT]), None)
        kb.bar("vector")
        kb.op("vector", lambda v: v.tensor_mul(c_cb2[:], c_bx[:], dsc1[:, 0:NT]), None)
        kb.bar("vector")
        kb.op("vector", lambda v: v.tensor_add(c_cb2[:], c_cb2[:], c_cb[:]), None)
        kb.op("vector", lambda v: v.tensor_scalar_add(c_hm1[:], c_hm[:], -1.0), None)
        kb.bar("vector")
        kb.op("vector", lambda v: v.tensor_tensor(
            out=c_mball[:].rearrange("p (t e) -> p t e", e=2),
            in0=c_hm1[:].unsqueeze(1).broadcast_to([128, NT, 2]),
            in1=c_bx[:].unsqueeze(2).broadcast_to([128, NT, 2]),
            op=ALU.mult), "vb")

        kb.op("vector", lambda v: v.memset(c_eps[:], EPS), "vx")
        kb.op("vector", lambda v: v.memset(xh[:], 0.0), "vx")
        kb.wait("sync", "vx")
        kb.dma("sync", xh[0:2, :], ins["xhalo"], "qx")
        v_qx_h = kb.val("qx")
        for tt in range(NT):
            if tt >= 2:
                kb.wait("sync", "vxt", tt - 1)
            kb.dma("sync", xts[tt % 2][:], ins["xin"][tt * 128:(tt + 1) * 128, :],
                   "qx0" if tt % 2 == 0 else "qx1")

        def ln_tile(src, out_ap, scr, save_col=None):
            sm = stat[:, 0:1]
            sq = stat[:, 1:2]
            sv = stat[:, 2:3]
            sr = stat[:, 3:4]
            m2 = stat[:, 4:5]
            kb.bar("vector")
            kb.bar("scalar")
            kb.op("vector", lambda v, src=src, sm=sm: v.reduce_sum(sm, src, axis=AXX), "vx")
            kb.bar("vector")
            kb.wait("scalar", "vx")
            kb.op("scalar", lambda s, src=src, sq=sq, scr=scr: s.activation(
                out=scr, in_=src, func=AF.Square, accum_out=sq), "sx")
            kb.op("vector", lambda v, sm=sm: v.tensor_scalar_mul(sm, sm, 1.0 / D), "vx")
            kb.wait("vector", "sx")
            kb.op("vector", lambda v, sq=sq, sv=sv: v.tensor_scalar_mul(sv, sq, 1.0 / D), "vx")
            kb.bar("vector")
            kb.op("vector", lambda v, sm=sm, m2=m2: v.tensor_mul(m2, sm, sm), "vx")
            kb.bar("vector")
            kb.op("vector", lambda v, sv=sv, m2=m2: v.tensor_sub(sv, sv, m2), "vx")
            kb.wait("scalar", "vx")
            kb.op("scalar", lambda s, sv=sv: s.activation(out=sv, in_=sv, func=AF.Ln, bias=c_eps[:, 0:1]), "sx")
            kb.bar("scalar")
            kb.op("scalar", lambda s, sv=sv, sr=sr: s.activation(out=sr, in_=sv, func=AF.Exp, scale=-0.5), "sx")
            kb.wait("vector", "sx")
            kb.op("vector", lambda v, src=src, out_ap=out_ap, sm=sm, sr=sr: v.tensor_scalar(
                out=out_ap, in0=src, scalar1=sm, scalar2=sr,
                op0=ALU.subtract, op1=ALU.mult), "vx")
            if save_col is not None:
                kb.op("vector", lambda v, sm=sm, save_col=save_col: v.tensor_copy(
                    mstats[:, save_col:save_col + 1], sm), "vx")
                kb.op("vector", lambda v, sr=sr, save_col=save_col: v.tensor_copy(
                    mstats[:, 16 + save_col:16 + save_col + 1], sr), "vx")

        kb.wait("vector", "qx", v_qx_h)
        ln_tile(xh[:], xh[:], sqsc[:])
        kb.wait("tensor", "vx")
        kb.wait("tensor", "qc", v_qc)
        for kd in range(ND):
            kb.op("tensor", lambda t, kd=kd: t.transpose(
                psumA[:, 1024 + kd * 128:1024 + (kd + 1) * 128],
                xh[:, kd * 128:(kd + 1) * 128], c_id[:]), "tx")
        kb.wait("scalar", "tx")
        for kd in range(ND):
            kb.op("scalar", lambda s, kd=kd: s.copy(
                out=xnhbf[:, kd * 2:kd * 2 + 2],
                in_=psumA[:, 1024 + kd * 128:1024 + kd * 128 + 2]), "sx")
        v_sx_halo = kb.val("sx")

        for tt in range(NT):
            kb.wait("vector", "qx0" if tt % 2 == 0 else "qx1", 16 * (tt // 2 + 1))
            kb.wait("vector", "tx")
            ln_tile(xts[tt % 2][:], xnt[:], sqsc[:], save_col=tt)
            kb.op("vector", lambda v: v.tensor_copy(stat[:, 5:6], stat[:, 0:1]), "vxt")
            kb.bar("vector")
            kb.wait("tensor", "vx")
            for h4 in range(2):
                kb.wait("tensor", "sx")
                for kk in range(4):
                    kd = h4 * 4 + kk
                    kb.op("tensor", lambda t, h4=h4, kk=kk, kd=kd: t.transpose(
                        psumA[:, h4 * 512 + kk * 128:h4 * 512 + (kk + 1) * 128],
                        xnt[:, kd * 128:(kd + 1) * 128], c_id[:]), "tx")
                kb.wait("scalar", "tx")
                kb.op("scalar", lambda s, tt=tt, h4=h4: s.copy(
                    out=xnbf.rearrange("p (d t) -> p d t", t=TL)[:, h4 * 4:(h4 + 1) * 4, tt * 128:(tt + 1) * 128],
                    in_=psumA[:, h4 * 512:(h4 + 1) * 512].rearrange("p (d c) -> p d c", c=128)), "sx")
        v_sx_xn = kb.val("sx")
        if DEBUG:
            kb.wait("sync", "sx", v_sx_xn)
            kb.dma("sync", tap_xn, xnbf, "qtap")
            kb.wait("sync", "vx")
            kb.dma("sync", tap_ms, mstats[:], "qtap")

        mark("input")
        v_qf0 = kb.val("qf")
        for kk in range(ND):
            kb.wait("sync", "tfk")
            kb.dma("sync", csb[:, 0:1024], ins["cdft"][kk * 128:(kk + 1) * 128, :], "qf")
            kb.dma("sync", csb[:, 1024:2048], ins["sdft"][kk * 128:(kk + 1) * 128, :], "qf")
            kb.dma("sync", csb[:, 2048:3072], ins["sneg"][kk * 128:(kk + 1) * 128, :], "qf")
            kb.wait("tensor", "qf")
            if kk == 0:
                kb.wait("tensor", "qwf", v_qw_wf)
            for nb in range(2):
                kb.op("tensor", lambda t, kk=kk, nb=nb: t.matmul(
                    psumB[:, nb * 512:(nb + 1) * 512],
                    wfb[:, 0:1024].rearrange("p (k d) -> p k d", d=128)[:, kk, :],
                    csb[:, nb * 512:(nb + 1) * 512],
                    start=(kk == 0), stop=False), None)
                kb.op("tensor", lambda t, kk=kk, nb=nb: t.matmul(
                    psumB[:, nb * 512:(nb + 1) * 512],
                    wfb[:, 1024:2048].rearrange("p (k d) -> p k d", d=128)[:, kk, :],
                    csb[:, 2048 + nb * 512:2048 + (nb + 1) * 512],
                    start=False, stop=(kk == ND - 1)), None)
                kb.op("tensor", lambda t, kk=kk, nb=nb: t.matmul(
                    psumB[:, 1024 + nb * 512:1024 + (nb + 1) * 512],
                    wfb[:, 1024:2048].rearrange("p (k d) -> p k d", d=128)[:, kk, :],
                    csb[:, nb * 512:(nb + 1) * 512],
                    start=(kk == 0), stop=False), None)
                kb.op("tensor", lambda t, kk=kk, nb=nb: t.matmul(
                    psumB[:, 1024 + nb * 512:1024 + (nb + 1) * 512],
                    wfb[:, 0:1024].rearrange("p (k d) -> p k d", d=128)[:, kk, :],
                    csb[:, 1024 + nb * 512:1024 + (nb + 1) * 512],
                    start=False, stop=(kk == ND - 1)),
                    "tfk" if nb == 1 else None)
        kb.wait("scalar", "tfk", 8)
        kb.op("scalar", lambda s: s.copy(out=u1b[:], in_=psumB[:, 0:1024]), "sf")
        kb.op("scalar", lambda s: s.copy(out=u2b[:], in_=psumB[:, 1024:2048]), "sf")
        kb.wait("vector", "sf")
        for pos in range(16):
            srcb = u1b if pos < 8 else u2b
            kb.wait("vector", "qstg")
            kb.op("vector", lambda v, srcb=srcb, pos=pos: v.tensor_scalar_mul(
                stg[:], srcb[:], c_s16[:, pos:pos + 1]), "vb")
            kb.wait("scalar", "vb")
            kb.dma("scalar", u12in[pos * 128:(pos + 1) * 128, :], stg[:], "qstg")
        def pj_load(wname, tt, bufs, gate_sem, gate_val, sembase):
            if gate_val > 0:
                kb.wait("gpsimd", gate_sem, gate_val)
            kb.dma("gpsimd", bufs[tt % 2].rearrange("p (k h) -> p k h", h=128),
                   ins[wname].rearrange("(k p) h -> p k h", k=8)[:, :, tt * 128:(tt + 1) * 128],
                   sembase + str(tt % 2))

        wpj4 = [wpjg[0], wpjg[1], wpju[0], wpju[1]]
        for tt in range(NT):
            if tt == 2:
                kb.wait("gpsimd", "tfk", 8)
            if tt >= 4:
                kb.wait("gpsimd", "tp", 4 * (tt - 3))
            kb.dma("gpsimd", wpj4[tt % 4].rearrange("p (k h) -> p k h", h=128),
                   ins["wdtt"].rearrange("(k p) h -> p k h", k=8)[:, :, tt * 128:(tt + 1) * 128],
                   "qwd" + str(tt % 4))
        kb.wait("gpsimd", "qstg")
        kb.op("gpsimd", lambda g: g.collective_compute(
            "AllReduce", ALU.add, replica_groups=[list(range(8))],
            ins=[u12in], outs=[u12out]), "gar")
        v_gar_u12 = kb.val("gar")
        pj_load("wgt", 2, wpjg, "tp", 52, "qwg")
        pj_load("wgt", 3, wpjg, "tp", 56, "qwg")
        pj_load("wxt", 2, wpju, "tp", 60, "qwu")
        pj_load("wxt", 3, wpju, "tp", 64, "qwu")
        kb.wait("gpsimd", "tfk", 8)

        if DEBUG:
            kb.wait("sync", "gar", v_gar_u12)
            kb.dma("sync", tap_u12, u12out, "qtap")

        mark("ubuild")
        kb.wait("tensor", "sx", v_sx_xn)
        for tt in range(NT):
            kb.wait("tensor", "qwd" + str(tt % 4), 16 * (tt // 4 + 1))
            for nb in range(4):
                if 4 * tt + nb - 1 > 0:
                    kb.wait("tensor", "sdt", 4 * tt + nb - 1)
                for kk in range(ND):
                    kb.op("tensor", lambda t, tt=tt, nb=nb, kk=kk: t.matmul(
                        psumA[:, (nb % 2) * 512:(nb % 2 + 1) * 512],
                        wpj4[tt % 4].rearrange("p (k h) -> p k h", h=128)[:, kk, :],
                        xnbf[:, kk * TL + nb * 512:kk * TL + (nb + 1) * 512],
                        start=(kk == 0), stop=(kk == ND - 1)), "tp" if kk == ND - 1 else None)
                kb.wait("scalar", "tp")
                kb.bar("scalar")
                kb.op("scalar", lambda s, tt=tt, nb=nb: s.activation(
                    out=esc[:], in_=psumA[:, (nb % 2) * 512:(nb % 2 + 1) * 512],
                    func=AF.Exp, bias=c_bdt[:, tt:tt + 1]), None)
                kb.bar("scalar")
                kb.op("scalar", lambda s, tt=tt, nb=nb: s.activation(
                    out=esc[:], in_=esc[:], func=AF.Ln, bias=1.0,
                    accum_out=dtacc[:, tt * 4 + nb:tt * 4 + nb + 1]), "sdt")

        kb.wait("scalar", "qc", v_qc)
        kb.op("scalar", lambda s: s.activation(out=a_f[:], in_=c_an[:], func=AF.Exp), None)
        kb.bar("scalar")
        kb.op("scalar", lambda s: s.activation(out=a_f[:], in_=a_f[:], func=AF.Ln, bias=1.0), "sb")
        kb.wait("vector", "sb")
        kb.op("vector", lambda v: v.tensor_scalar_mul(a_f[:], a_f[:], -1.0), "vb")
        v_vb_af = kb.val("vb")

        kb.wait("vector", "sdt")
        kb.op("vector", lambda v: v.tensor_reduce(
            out=dtv[:], in_=dtacc[:].rearrange("p (t f) -> p t f", f=4),
            axis=AXX, op=ALU.add), "vd")
        kb.wait("scalar", "vd")
        kb.dma("scalar", dtin, dtv[:], "qdt")
        kb.wait("gpsimd", "qdt")
        kb.op("gpsimd", lambda g: g.collective_compute(
            "AllReduce", ALU.add, replica_groups=[list(range(8))],
            ins=[dtin], outs=[dtout]), "gar")
        v_gar_dt = kb.val("gar")
        kb.wait("sync", "gar", v_gar_dt)
        kb.dma("sync", dtv[:], dtout, "qdt")
        kb.wait("vector", "qdt")
        kb.op("vector", lambda v: v.tensor_scalar(
            out=dtv[:], in0=dtv[:], scalar1=1.0 / (B * L), scalar2=0.001,
            op0=ALU.mult, op1=ALU.max), "vd")
        kb.bar("vector")
        kb.op("vector", lambda v: v.tensor_scalar_min(dtv[:], dtv[:], 2.0), "vd")
        kb.bar("vector")
        v_vd_dt = kb.val("vd")
        if DEBUG:
            kb.wait("sync", "vd", v_vd_dt)
            kb.dma("sync", tap_dt, dtv[:], "qtap")

        mark("dt")
        kb.wait("sync", "gar", v_gar_u12)
        kb.dma("sync", mcs[:, 0:1024].rearrange("p (k d) -> p k d", d=128),
               ins["cdcol"].rearrange("(k p) d -> p k d", k=8), "qf")
        kb.dma("sync", mcs[:, 1024:2048].rearrange("p (k d) -> p k d", d=128),
               ins["sdcol"].rearrange("(k p) d -> p k d", k=8), "qf")
        v_qf_mcs = kb.val("qf")
        u12q = [yffn[:, 0:4096], yffn[:, 4096:8192]]
        for q in range(4):
            if q >= 2:
                kb.wait("sync", "tfs", 4 * (q - 1))
            kb.dma("sync", u12q[q % 2].rearrange("p (j c) -> p j c", j=4),
                   u12out[q * 512:(q + 1) * 512].rearrange("(j p) c -> p j c", j=4),
                   "qu" + str(q % 2))
        kb.wait("tensor", "qf", v_qf_mcs)
        kb.wait("tensor", "sx", v_sx_xn)
        for q in range(4):
            kb.wait("tensor", "qu" + str(q % 2), 16 * (q // 2 + 1))
            half = 0 if q < 2 else 1024
            for j in range(4):
                kk = (q % 2) * 4 + j
                for nb in range(2):
                    kb.op("tensor", lambda t, q=q, j=j, kk=kk, nb=nb, half=half: t.matmul(
                        psumA[:, 1024 + nb * 512:1024 + (nb + 1) * 512],
                        mcs[:, half:half + 1024].rearrange("p (k d) -> p k d", d=128)[:, kk, :],
                        u12q[q % 2][:, j * 1024 + nb * 512:j * 1024 + (nb + 1) * 512],
                        start=(q == 0 and j == 0), stop=(q == 3 and j == 3)),
                        "tfs" if nb == 1 else None)
        kb.wait("scalar", "tfs", kb.val("tfs"))
        kb.op("scalar", lambda s: s.activation(out=mtblk[:], in_=psumA[:, 1024:2048],
                                               func=AF.Copy, scale=1.0 / D), "sf")
        kb.wait("vector", "sf")
        kb.wait("vector", "tfs", 16)
        v_qstg_b = kb.val("qstg")
        v_qstg2_b = kb.val("qstg2")
        for pos in range(8):
            sbuf_s = stg if pos % 2 == 0 else stg2
            qs_ = "qstg" if pos % 2 == 0 else "qstg2"
            base = v_qstg_b if pos % 2 == 0 else v_qstg2_b
            if pos >= 2:
                kb.wait("vector", qs_, base + 16 * (pos // 2))
            kb.op("vector", lambda v, pos=pos, sbuf_s=sbuf_s: v.tensor_scalar_mul(
                sbuf_s[:], mtblk[:], c_s8[:, pos:pos + 1]), "vb")
            kb.wait("scalar", "vb")
            kb.dma("scalar", mtin[pos * 128:(pos + 1) * 128, :], sbuf_s[:], qs_)
        kb.wait("gpsimd", "qstg")
        kb.wait("gpsimd", "qstg2")
        kb.op("gpsimd", lambda g: g.collective_compute(
            "AllReduce", ALU.add, replica_groups=[list(range(8))],
            ins=[mtin], outs=[mtout]), "gar")
        v_gar_mt = kb.val("gar")
        kb.wait("sync", "gar", v_gar_mt)
        kb.dma("sync", mtres.rearrange("p (k d) -> p k d", d=1024),
               mtout.rearrange("(k p) d -> p k d", k=8), "qmt")
        v_qf_mt = kb.val("qmt")
        if DEBUG:
            kb.dma("sync", tap_mt, mtout, "qtap")

        mark("mt")
        kb.wait("gpsimd", "tfk", 8)
        for tt in range(4, NT):
            pj_load("wxt", tt, wpju, "tp", 64 + 9 * max(0, tt - 1), "qwu")
            pj_load("wgt", tt, wpjg, "tp", 64 + 9 * max(0, tt - 1), "qwg")

        kb.wait("tensor", "sdt", 64)
        kb.wait("tensor", "sf", 2)
        TPB = 64
        for tt in range(NT):
            upj = wearly[:, tt * 2048:tt * 2048 + 1024] if tt < 2 else wpju[tt % 2]
            gpj = wearly[:, tt * 2048 + 1024:tt * 2048 + 2048] if tt < 2 else wpjg[tt % 2]
            if DEBUG and tt == 2:
                kb.wait("scalar", "qtap")
            if tt < 2:
                kb.wait("tensor", "qwe" + str(tt), 32)
            else:
                kb.wait("tensor", "qwu" + str(tt % 2), 16 * (tt // 2))
            for nb in range(4):
                if 4 * tt + nb - 1 > 0:
                    kb.wait("tensor", "seu", 4 * tt + nb - 1)
                for kk in range(ND):
                    kb.op("tensor", lambda t, upj=upj, nb=nb, kk=kk: t.matmul(
                        psumA[:, (nb % 2) * 512:(nb % 2 + 1) * 512],
                        upj.rearrange("p (k h) -> p k h", h=128)[:, kk, :],
                        xnbf[:, kk * TL + nb * 512:kk * TL + (nb + 1) * 512],
                        start=(kk == 0), stop=(kk == ND - 1)), "tp" if kk == ND - 1 else None)
                kb.wait("scalar", "tp", TPB + 9 * tt + nb + 1)
                if tt >= 2:
                    kb.wait("scalar", "vc", 2 * tt - 2)
                kb.op("scalar", lambda s, tt=tt, nb=nb: s.copy(
                    out=ut[tt % 2][:, nb * 512:(nb + 1) * 512],
                    in_=psumA[:, (nb % 2) * 512:(nb % 2 + 1) * 512]), "seu")
            if tt == 0:
                kb.wait("tensor", "sx", v_sx_halo)
                kb.wait("tensor", "sf", 3)
            if tt >= 1:
                kb.wait("tensor", "vc", 2 * tt - 1)
            for kk in range(ND):
                kb.op("tensor", lambda t, upj=upj, kk=kk: t.matmul(
                    psumA[:, 1024:1026],
                    upj.rearrange("p (k h) -> p k h", h=128)[:, kk, :],
                    xnhbf[:, kk * 2:(kk + 1) * 2],
                    start=(kk == 0), stop=(kk == ND - 1)), "tp" if kk == ND - 1 else None)
            kb.wait("vector", "tp", TPB + 9 * tt + 5)
            kb.op("vector", lambda v, tt=tt: v.tensor_mul(
                uh[:, tt * 2:(tt + 1) * 2], psumA[:, 1024:1026], c_hm[:]), None)
            kb.bar("vector")
            kb.op("vector", lambda v, tt=tt: v.tensor_add(
                uh[:, tt * 2:(tt + 1) * 2], uh[:, tt * 2:(tt + 1) * 2],
                c_mball[:, tt * 2:(tt + 1) * 2]), "vc")
            kb.bar("vector")
            if tt >= 2:
                kb.wait("tensor", "qwg" + str(tt % 2), 16 * (tt // 2))
            if tt >= 1:
                kb.wait("tensor", "vc", 2 * tt)
            for nb in range(4):
                for kk in range(ND):
                    kb.op("tensor", lambda t, gpj=gpj, nb=nb, kk=kk: t.matmul(
                        psumB[:, nb * 512:(nb + 1) * 512],
                        gpj.rearrange("p (k h) -> p k h", h=128)[:, kk, :],
                        xnbf[:, kk * TL + nb * 512:kk * TL + (nb + 1) * 512],
                        start=(kk == 0), stop=(kk == ND - 1)), "tp" if kk == ND - 1 else None)
            for nb in range(4):
                kb.wait("scalar", "tp", TPB + 9 * tt + 6 + nb)
                kb.op("scalar", lambda s, tt=tt, nb=nb: s.activation(
                    out=psumB[:, nb * 512:(nb + 1) * 512],
                    in_=psumB[:, nb * 512:(nb + 1) * 512],
                    func=AF.Sigmoid, bias=c_bg[:, tt:tt + 1]), "seg")
            if DEBUG and tt == 0:
                kb.wait("sync", "seu", 4)
                kb.dma("sync", tap_u, ut[0][:], "qtap")
            u_c = ut[tt % 2]
            um = ums[tt % 2]
            w0 = c_cw[:, tt:tt + 1]
            w1 = c_cw[:, NT + tt:NT + tt + 1]
            w2 = c_cw[:, 2 * NT + tt:2 * NT + tt + 1]
            cb = c_cb2[:, tt:tt + 1]
            kb.wait("vector", "seu", 4 * tt + 4)
            kb.op("vector", lambda v, u_c=u_c, w1=w1, cb=cb: v.tensor_scalar(
                out=cvt[:, 0:2048], in0=u_c[:], scalar1=w1, scalar2=cb,
                op0=ALU.mult, op1=ALU.add), None)
            kb.bar("vector")
            kb.op("vector", lambda v, u_c=u_c, w0=w0: v.scalar_tensor_tensor(
                out=cvt[:, 128:2048], in0=u_c[:, 0:1920], scalar=w0,
                in1=cvt[:, 128:2048], op0=ALU.mult, op1=ALU.add), None)
            kb.op("vector", lambda v, u_c=u_c, w0=w0: v.scalar_tensor_tensor(
                out=cvt[:, 1:128], in0=u_c[:, 1920:2047], scalar=w0,
                in1=cvt[:, 1:128], op0=ALU.mult, op1=ALU.add), None)
            kb.op("vector", lambda v, tt=tt, w0=w0: v.scalar_tensor_tensor(
                out=cvt[:, 0:1], in0=uh[:, tt * 2:tt * 2 + 1], scalar=w0,
                in1=cvt[:, 0:1], op0=ALU.mult, op1=ALU.add), None)
            kb.bar("vector")
            kb.op("vector", lambda v, u_c=u_c, w2=w2: v.scalar_tensor_tensor(
                out=cvt[:, 0:1920], in0=u_c[:, 128:2048], scalar=w2,
                in1=cvt[:, 0:1920], op0=ALU.mult, op1=ALU.add), None)
            kb.op("vector", lambda v, u_c=u_c, w2=w2: v.scalar_tensor_tensor(
                out=cvt[:, 1920:2047], in0=u_c[:, 1:128], scalar=w2,
                in1=cvt[:, 1920:2047], op0=ALU.mult, op1=ALU.add), None)
            kb.op("vector", lambda v, tt=tt, w2=w2: v.scalar_tensor_tensor(
                out=cvt[:, 2047:2048], in0=uh[:, tt * 2 + 1:tt * 2 + 2], scalar=w2,
                in1=cvt[:, 2047:2048], op0=ALU.mult, op1=ALU.add), None)
            kb.wait("vector", "seg", 4 * tt + 4)
            kb.bar("vector")
            kb.op("vector", lambda v, u_c=u_c, tt=tt: v.scalar_tensor_tensor(
                out=dsd[:, 0:2048], in0=u_c[:], scalar=c_bx[:, tt:tt + 1],
                in1=cvt[:, 0:2048], op0=ALU.add, op1=ALU.subtract), None)
            kb.bar("vector")
            kb.op("vector", lambda v: v.tensor_mul(dsd[:, 0:2048], dsd[:, 0:2048], psumB[:, 0:2048]), None)
            kb.bar("vector")
            kb.wait("vector", "qm")
            kb.op("vector", lambda v, um=um: v.tensor_add(um[:], dsd[:, 0:2048], cvt[:, 0:2048]), "vc")
            kb.wait("scalar", "vc")
            kb.dma("scalar", udram[tt * 128:(tt + 1) * 128, :], um[:], "qm")
        v_qm = kb.val("qm")
        if DEBUG:
            kb.wait("sync", "vc")
            kb.dma("sync", tap_uh, uh[:], "qtap")

        mark("proj")
        kb.wait("sync", "vc")
        kb.dma("sync", cfb_f[:], ins["fbB"], "qfb")
        v_qf_fb = kb.val("qfb")
        kb.wait("tensor", "qmt", v_qf_mt)
        kb.wait("tensor", "seu")
        kb.wait("tensor", "seg")
        kb.wait("tensor", "vc")
        for tt in range(NT):
            pa = psumA if tt % 2 == 0 else psumB
            kb.wait("tensor", "sf")
            for kk in range(ND):
                for nb in range(2):
                    kb.op("tensor", lambda t, tt=tt, kk=kk, nb=nb, pa=pa: t.matmul(
                        pa[:, nb * 512:(nb + 1) * 512],
                        xnbf[:, kk * TL + tt * 128:kk * TL + (tt + 1) * 128],
                        mtres[:, kk * 1024 + nb * 512:kk * 1024 + (nb + 1) * 512],
                        start=(kk == 0), stop=(kk == ND - 1)),
                        "tf" if (kk == ND - 1 and nb == 1) else None)
            kb.wait("vector", "tf")
            kb.wait("vector", "qfb", v_qf_fb)
            kb.op("vector", lambda v, tt=tt, pa=pa: v.tensor_add(
                yffn[:, tt * 1024:(tt + 1) * 1024], pa[:, 0:1024], cfb_f[:]), "sf")
        kb.wait("scalar", "sf")
        for tt in range(NT):
            kb.op("scalar", lambda s, tt=tt: s.activation(
                out=yffn[:, tt * 1024:(tt + 1) * 1024],
                in_=yffn[:, tt * 1024:(tt + 1) * 1024], func=AF.Silu), "sf2")
        v_sf2 = kb.val("sf2")

        mark("ffn")
        kb.wait("vector", "vd", v_vd_dt)
        kb.wait("vector", "vb", v_vb_af)
        kb.op("vector", lambda v: v.tensor_tensor(
            out=lnad[:].rearrange("p (t s) -> p t s", s=S),
            in0=a_f[:].rearrange("p (t s) -> p t s", s=S),
            in1=dtv[:].unsqueeze(2).broadcast_to([128, NT, S]),
            op=ALU.mult), "vb")
        kb.wait("scalar", "vb")
        kb.op("scalar", lambda s: s.activation(out=adt[:], in_=lnad[:], func=AF.Exp), "sb")
        kb.op("vector", lambda v: v.tensor_scalar_add(rca[:], a_f[:], 1e-6), None)
        kb.bar("vector")
        kb.op("vector", lambda v: v.reciprocal(rca[:], rca[:]), "vb")
        kb.wait("vector", "sb")
        kb.op("vector", lambda v: v.tensor_scalar_add(bcn[:], adt[:], -1.0), None)
        kb.bar("vector")
        kb.op("vector", lambda v: v.tensor_mul(bcn[:], bcn[:], rca[:]), None)
        kb.bar("vector")
        kb.op("vector", lambda v: v.tensor_mul(bcn[:], bcn[:], c_bn[:]), None)
        kb.bar("vector")
        kb.op("vector", lambda v: v.tensor_mul(bcn[:], bcn[:], c_cn[:]), "vb")
        kb.bar("vector")

        def dense_tile(tt):
            sl = slice(tt * 16, (tt + 1) * 16)
            lnsl = lnad[:, sl]
            bcsl = bcn[:, sl]
            cdb = cd0 if tt % 2 == 0 else cd1
            if tt >= 2:
                kb.wait("vector", "qpd" + str(tt % 2), 16 * (tt // 2))
            kb.op("vector", lambda v, lnsl=lnsl: v.tensor_tensor(
                out=dsc1[:], in0=c_tcwv[:],
                in1=lnsl.unsqueeze(1).broadcast_to([128, Q, S]), op=ALU.mult), "vb")
            kb.wait("scalar", "vb")
            kb.op("scalar", lambda s, cdb=cdb: s.activation(
                out=cdb[:].rearrange("p (q f r) -> p q f r", q=16, f=4)[:, :, 0, :],
                in_=dsc1[:], func=AF.Exp), "sb")
            kb.wait("vector", "sb")
            kb.op("vector", lambda v, lnsl=lnsl: v.tensor_tensor(
                out=dsc1[:], in0=c_tcs2[:],
                in1=lnsl.unsqueeze(2).broadcast_to([128, S, Q]), op=ALU.mult), "vb")
            kb.wait("scalar", "vb")
            kb.op("scalar", lambda s: s.activation(out=dsc2[:], in_=dsc1[:], func=AF.Exp), "sb")
            kb.wait("vector", "sb")
            kb.op("vector", lambda v, cdb=cdb: v.tensor_copy(
                cdb[:].rearrange("p (q f r) -> p q f r", q=16, f=4)[:, :, 3, :],
                dsc2[:].rearrange("p (q r) -> p q r", q=16)), "vb")
            kb.op("vector", lambda v, bcsl=bcsl, cdb=cdb: v.tensor_tensor(
                out=cdb[:].rearrange("p (q f r) -> p q f r", q=16, f=4)[:, :, 1, :],
                in0=dsc2[:].rearrange("p (q r) -> p q r", q=16),
                in1=bcsl.unsqueeze(2).broadcast_to([128, S, Q]), op=ALU.mult), "vb")
            kb.wait("vector", "sb")
            kb.op("vector", lambda v, lnsl=lnsl: v.tensor_tensor(
                out=dsc1[:], in0=c_tck8[:],
                in1=lnsl.unsqueeze(1).broadcast_to([128, Q, S]), op=ALU.mult), "vb")
            kb.wait("scalar", "vb")
            kb.op("scalar", lambda s: s.activation(out=dsc2[:], in_=dsc1[:], func=AF.Exp), "sb")
            kb.wait("vector", "sb")
            kb.op("vector", lambda v, bcsl=bcsl: v.tensor_tensor(
                out=dsc2[:].rearrange("p (q r) -> p q r", q=16),
                in0=dsc2[:].rearrange("p (q r) -> p q r", q=16),
                in1=bcsl.unsqueeze(1).broadcast_to([128, Q, S]), op=ALU.mult), None)
            kb.bar("vector")
            kb.op("vector", lambda v: v.memset(kpad[:, 0:15], 0.0), None)
            kb.op("vector", lambda v: v.tensor_reduce(
                out=kpad[:, 15:31], in_=dsc2[:].rearrange("p (q r) -> p q r", q=16),
                axis=AXX, op=ALU.add), "vb")
            kb.bar("vector")
            for tq in range(Q):
                kb.op("vector", lambda v, tq=tq, cdb=cdb: v.tensor_copy(
                    cdb[:, tq * 64 + 32:tq * 64 + 48],
                    kpad[:, 15 - tq:31 - tq]), "vb")
            kb.wait("scalar", "vb")
            kb.wait("scalar", "sb")
            kb.dma("scalar", pdram[tt * 128:(tt + 1) * 128, :], cdb[:],
                   "qpd" + str(tt % 2))

        def strips_load(tt):
            kb.dma("gpsimd", strps[tt % 2].rearrange("p (g f) -> p g f", g=16),
                   pdram.rearrange("(t g i) (q f) -> t i q g f", t=NT, g=16, q=16)[tt],
                   "qs" + str(tt % 2))

        def g_load(tt):
            kb.dma("sync", gbs[tt % 2].rearrange("p (g j) -> p g j", g=16),
                   udram.rearrange("(t g i) (tau j) -> t i tau g j", t=NT, g=16, tau=16)[tt],
                   "qg" + str(tt % 2))

        def expand(dst, fam, strp, sem):
            kb.op("vector", lambda v, dst=dst, fam=fam, strp=strp: v.tensor_tensor(
                out=dst.rearrange("p (g i r) -> p g i r", i=8, r=16),
                in0=c_dm[:].rearrange("p (i r) -> p i r", r=16)
                          .unsqueeze(1).broadcast_to([128, 16, 8, 16]),
                in1=strp.rearrange("p (g f) -> p g f", g=16)[:, :, fam * 16:(fam + 1) * 16]
                        .unsqueeze(2).broadcast_to([128, 16, 8, 16]),
                op=ALU.mult), sem)

        def aq_build_to(dst, strp, sem):
            kb.op("vector", lambda v, dst=dst, strp=strp: v.tensor_tensor(
                out=dst.rearrange("p (g j) -> p g j", g=16),
                in0=c_rst[:].rearrange("p (g j) -> p g j", g=16),
                in1=strp.rearrange("p (g f) -> p g f", f=64)[:, :, 63]
                        .unsqueeze(2).broadcast_to([128, 16, 128]),
                op=ALU.mult), sem)

        def aq_build(strp, sem):
            aq_build_to(aq16, strp, sem)

        kb.wait("gpsimd", "sf2", v_sf2)
        for tt in range(NT):
            kb.wait("gpsimd", "qpd" + str(tt % 2), 16 * (tt // 2 + 1))
            if tt >= 2:
                kb.wait("gpsimd", "vw", 2 * (tt - 1))
            strips_load(tt)
        kb.wait("sync", "qm", v_qm)
        kb.wait("sync", "sf2", v_sf2)
        for tt in range(NT):
            if tt >= 2:
                kb.wait("sync", "tv", 16 * (tt - 1))
            g_load(tt)

        kb.wait("tensor", "sf")
        kb.wait("tensor", "seu")
        kb.wait("tensor", "seg")
        kb.wait("tensor", "vc")

        wvb = [bigA[:, 0:2048], bigA[:, 2048:4096]]
        aqb = [bigA[:, 20480:22528], bigA[:, 22528:24576]]
        pVs = [psumA, psumB]
        TV0 = kb.val("tv")
        VS0 = kb.val("vs")

        def s1_front(tt):
            strp = strps[tt % 2]
            kb.wait("vector", "qs" + str(tt % 2), 16 * (tt // 2 + 1))
            if tt >= 2:
                kb.wait("vector", "tv", TV0 + 16 * (tt - 1))
            expand(wvb[tt % 2], 0, strp, "vw")
            aq_build_to(aqb[tt % 2], strp, "vw")
            kb.wait("tensor", "vw")
            kb.wait("tensor", "qg" + str(tt % 2), 16 * (tt // 2 + 1))
            if tt >= 2:
                kb.wait("tensor", "vs", VS0 + 2 * (tt - 1))
            for g in range(16):
                kb.op("tensor", lambda t, g=g, tt=tt: t.matmul(
                    pVs[tt % 2][:, g * 128:(g + 1) * 128],
                    wvb[tt % 2][:, g * 128:(g + 1) * 128],
                    gbs[tt % 2][:, g * 128:(g + 1) * 128],
                    start=True, stop=True), "tv")

        def s1_back(tt):
            kb.wait("vector", "tv", TV0 + 16 * (tt + 1))
            kb.wait("vector", "vw", 2 * (tt + 1))
            if tt >= 1:
                kb.wait("vector", "vs", VS0 + 2 * tt)
            kb.op("vector", lambda v, tt=tt: v.tensor_tensor_scan(
                out=sout[:, 0:2048], data0=aqb[tt % 2][:, 0:2048],
                data1=pVs[tt % 2][:, 0:2048],
                initial=0.0, op0=ALU.mult, op1=ALU.add), "vs")
            kb.bar("vector")
            kb.op("vector", lambda v, tt=tt: v.tensor_copy(
                sendst[:, tt * 16:(tt + 1) * 16],
                sout.rearrange("p (g j) -> p g j", j=128)[:, :, 127]), "vs")

        for k in range(NT + 1):
            if k < NT:
                dense_tile(k)
                s1_front(k)
            if k >= 1:
                s1_back(k - 1)
        mark("dense")
        kb.bar("vector")
        kb.op("vector", lambda v: v.tensor_scalar_mul(sendst[:], sendst[:], c_cm[:, 0:1]), "vs")
        kb.bar("vector")
        kb.wait("scalar", "vs")
        kb.dma("scalar", ssin, sendst[:], "qar")
        kb.wait("gpsimd", "qar")
        kb.op("gpsimd", lambda g: g.collective_compute(
            "AllReduce", ALU.add, replica_groups=[[0, 1], [2, 3], [4, 5], [6, 7]],
            ins=[ssin], outs=[ssout]), "gar")
        v_gar_ss = kb.val("gar")
        kb.wait("sync", "gar", v_gar_ss)
        kb.dma("sync", shall[:], ssout, "qar")
        if DEBUG:
            kb.dma("sync", tap_sh, ssout, "qtap")
        kb.wait("vector", "qar")
        kb.op("vector", lambda v: v.tensor_scalar_mul(sheff[:], shall[:], c_cm[:, 1:2]), "vs")
        kb.bar("vector")
        kb.op("vector", lambda v: v.tensor_copy(sheffb[:], sheff[:]), "vs")
        kb.bar("vector")
        v_vs_sh = kb.val("vs")

        mark("s1")
        v_vw_s1 = kb.val("vw")
        for tt in range(NT):
            if tt >= 2:
                kb.wait("gpsimd", "vw", v_vw_s1 + 3 * (tt - 1))
            strips_load(tt)
        for tt in range(NT):
            if tt >= 2:
                kb.wait("sync", "ty", tt - 1)
            g_load(tt)

        kb.wait("vector", "vs", v_vs_sh)
        for tt in range(NT):
            strp = strps[tt % 2]
            kb.wait("vector", "qs" + str(tt % 2), 16 * (NT // 2 + tt // 2 + 1))
            kb.wait("vector", "tv")
            kb.wait("vector", "ty")
            kb.op("vector", lambda v, strp=strp: v.tensor_tensor(
                out=bigA[:, 0:8192].rearrange("p (c i r) -> p c i r", c=64, i=8),
                in0=c_dm[:].rearrange("p (i r) -> p i r", r=16)
                          .unsqueeze(1).broadcast_to([128, 64, 8, 16]),
                in1=strp.rearrange("p (c r) -> p c r", r=16)
                        .unsqueeze(2).broadcast_to([128, 64, 8, 16]),
                op=ALU.mult), "vw")
            aq_build(strp, "vw")
            kb.wait("tensor", "vw")
            kb.wait("tensor", "qg" + str(tt % 2), 16 * (NT // 2 + tt // 2 + 1))
            kb.wait("tensor", "vs")
            for g in range(16):
                kb.op("tensor", lambda t, g=g, tt=tt: t.matmul(
                    psumA[:, g * 128:(g + 1) * 128],
                    bigA[:, g * 512:g * 512 + 128],
                    gbs[tt % 2][:, g * 128:(g + 1) * 128],
                    start=True, stop=True), "tv")
            kb.wait("vector", "tv")
            kb.op("vector", lambda v, tt=tt, strp=strp: v.tensor_tensor(
                out=vfix[:], in0=sheff[:, tt * 16:(tt + 1) * 16],
                in1=strp.rearrange("p (g f) -> p g f", f=64)[:, :, 63],
                op=ALU.mult), "vw")
            kb.bar("vector")
            kb.op("vector", lambda v: v.tensor_add(
                psumA[:, 0:2048].rearrange("p (g j) -> p g j", j=128)[:, :, 0],
                psumA[:, 0:2048].rearrange("p (g j) -> p g j", j=128)[:, :, 0],
                vfix[:]), None)
            kb.bar("vector")
            kb.wait("vector", "vw")
            kb.op("vector", lambda v: v.tensor_tensor_scan(
                out=sout[:, 0:2048], data0=aq16[:, 0:2048], data1=psumA[:, 0:2048],
                initial=0.0, op0=ALU.mult, op1=ALU.add), "vs")
            kb.wait("tensor", "vs")
            kb.wait("tensor", "sy")
            for g in range(16):
                kb.op("tensor", lambda t, g=g, tt=tt: t.matmul(
                    psumB[:, g * 128:g * 128 + 1],
                    bigA[:, g * 512 + 128:g * 512 + 256],
                    sheffb[:, tt * 16 + g:tt * 16 + g + 1],
                    start=True, stop=False), None)
                kb.op("tensor", lambda t, g=g: t.matmul(
                    psumB[:, g * 128 + 1:(g + 1) * 128],
                    bigA[:, g * 512 + 128:g * 512 + 256],
                    sout[:, g * 128:g * 128 + 127],
                    start=False, stop=False), None)
                kb.op("tensor", lambda t, g=g, tt=tt: t.matmul(
                    psumB[:, g * 128:(g + 1) * 128],
                    bigA[:, g * 512 + 256:g * 512 + 384],
                    gbs[tt % 2][:, g * 128:(g + 1) * 128],
                    start=False, stop=True), "ty" if g == 15 else None)
            kb.wait("scalar", "ty")
            kb.wait("scalar", "qyd")
            for nb in range(4):
                kb.op("scalar", lambda s, tt=tt, nb=nb: s.copy(
                    out=ygbs[tt % 2][:, nb * 512:(nb + 1) * 512],
                    in_=psumB[:, nb * 512:(nb + 1) * 512]), "sy")
            kb.wait("scalar", "sy")
            kb.dma("scalar",
                   ydram.rearrange("(t g i) (tp j) -> t i tp g j", t=NT, g=16, tp=16)[tt],
                   ygbs[tt % 2].rearrange("p (g j) -> p g j", g=16), "qyd")
        v_qyd = kb.val("qyd")

        mark("s3")
        kb.wait("gpsimd", "vs")
        kb.wait("gpsimd", "ty")
        kb.wait("gpsimd", "qyd", v_qyd)
        for kk in range(NT):
            kb.dma("gpsimd", wob[:, kk * 1024:(kk + 1) * 1024],
                   ins["woutt"][kk * 128:(kk + 1) * 128, :], "qw")
        v_qw_wo = kb.val("qw")

        kb.wait("sync", "ty")
        kb.dma("sync", cbout_t[:], ins["boutB"], "qc")
        kb.dma("sync", clng_t[:], ins["lngB"], "qc")
        kb.dma("sync", clnb_t[:], ins["lnbB"], "qc")
        v_qc_t = kb.val("qc")

        ybuf = [bigB[:, 0:2048], bigB[:, 2048:4096],
                bigB[:, 4096:6144], bigB[:, 6144:8192]]
        kb.wait("sync", "qyd", v_qyd)
        kb.wait("scalar", "qyd", v_qyd)
        if True:
            for tl in range(NT):
                tt = tl
                pa = psumA if tl % 2 == 0 else psumB
                qeng = "sync" if tt % 2 == 0 else "scalar"
                if tt >= 4:
                    kb.wait(qeng, "tt2", tt - 3)
                kb.dma(qeng, ybuf[tt % 4].rearrange("p (k c) -> p k c", c=128),
                       ydram.rearrange("(k p) (t c) -> t p k c", k=NT, t=NT)[tt],
                       "qyl" + str(tt % 4))
                if tt >= 2:
                    kb.wait("sync", "vt", 7 * (tt - 2) + 7)
                kb.dma("sync", xts[tl % 2][:], ins["xin"][tt * 128:(tt + 1) * 128, :],
                       "qxr" + str(tl % 2))
                kb.wait("tensor", "qyl" + str(tt % 4), 16 * (tt // 4 + 1))
                if tt == 0:
                    kb.wait("tensor", "qw", v_qw_wo)
                if tt >= 2:
                    kb.wait("tensor", "vt", 7 * (tt - 2) + 2)
                for kk in range(NT):
                    for nb in range(2):
                        kb.op("tensor", lambda t, kk=kk, nb=nb, tt=tt, pa=pa: t.matmul(
                            pa[:, nb * 512:(nb + 1) * 512],
                            ybuf[tt % 4][:, kk * 128:(kk + 1) * 128],
                            wob[:, kk * 1024 + nb * 512:kk * 1024 + (nb + 1) * 512],
                            start=(kk == 0), stop=(kk == NT - 1)),
                            "tt2" if (kk == NT - 1 and nb == 1) else None)
                kb.wait("vector", "qxr" + str(tl % 2), 16 * (tl // 2 + 1))
                kb.op("vector", lambda v, tl=tl, tt=tt: v.tensor_scalar(
                    out=xnt[:], in0=xts[tl % 2][:],
                    scalar1=mstats[:, tt:tt + 1], scalar2=mstats[:, 16 + tt:16 + tt + 1],
                    op0=ALU.subtract, op1=ALU.mult), "vt")
                kb.bar("vector")
                kb.wait("vector", "tt2")
                kb.wait("vector", "qc", v_qc_t)
                kb.op("vector", lambda v, pa=pa: v.tensor_add(rtk[:], pa[:, 0:1024], xnt[:]), "vt")
                kb.bar("vector")
                kb.op("vector", lambda v: v.tensor_add(rtk[:], rtk[:], cbout_t[:]), "vt")
                kb.bar("vector")
                kb.op("vector", lambda v: v.reduce_sum(stat[:, 8:9], rtk[:], axis=AXX), "vt")
                kb.bar("vector")
                kb.wait("scalar", "vt")
                kb.op("scalar", lambda s: s.activation(out=sqsc[:], in_=rtk[:], func=AF.Square,
                                                       accum_out=stat[:, 9:10]), "st2")
                kb.wait("vector", "st2")
                kb.op("vector", lambda v: v.tensor_scalar_mul(stat[:, 8:9], stat[:, 8:9], 1.0 / D), None)
                kb.op("vector", lambda v: v.tensor_scalar_mul(stat[:, 9:10], stat[:, 9:10], 1.0 / D), None)
                kb.bar("vector")
                kb.op("vector", lambda v: v.tensor_mul(stat[:, 10:11], stat[:, 8:9], stat[:, 8:9]), None)
                kb.bar("vector")
                kb.op("vector", lambda v: v.tensor_sub(stat[:, 9:10], stat[:, 9:10], stat[:, 10:11]), "vt")
                kb.wait("scalar", "vt")
                kb.op("scalar", lambda s: s.activation(out=stat[:, 9:10], in_=stat[:, 9:10],
                                                       func=AF.Ln, bias=c_eps[:, 0:1]), None)
                kb.bar("scalar")
                kb.op("scalar", lambda s: s.activation(out=stat[:, 10:11], in_=stat[:, 9:10],
                                                       func=AF.Exp, scale=-0.5), "st2")
                kb.wait("vector", "st2")
                kb.bar("vector")
                kb.op("vector", lambda v: v.tensor_scalar(
                    out=snm[:], in0=rtk[:], scalar1=stat[:, 8:9], scalar2=stat[:, 10:11],
                    op0=ALU.subtract, op1=ALU.mult), "vt")
                kb.bar("vector")
                kb.op("vector", lambda v: v.tensor_mul(t3t[:], snm[:], clng_t[:]), None)
                kb.bar("vector")
                kb.op("vector", lambda v: v.tensor_add(t3t[:], t3t[:], clnb_t[:]), None)
                kb.bar("vector")
                kb.wait("vector", "sf2", v_sf2)
                kb.op("vector", lambda v, tt=tt: v.tensor_mul(
                    ptt[:], t3t[:], yffn[:, tt * 1024:(tt + 1) * 1024]), None)
                kb.bar("vector")
                kb.wait("vector", "qo")
                kb.op("vector", lambda v, tl=tl: v.tensor_add(ot0[:], ptt[:], xts[tl % 2][:]), "vt")
                kb.wait("sync", "vt")
                kb.dma("sync", xout[tt * 128:(tt + 1) * 128, :], ot0[:], "qo")
        kb.wait("sync", "qo")
        mark("tail")

        if CUT is not None:
            cut = marks[CUT]
            for e in kb.ENGS:
                kb.ops[e] = kb.ops[e][:cut[e]]
        block = E(nc.Block())

        @block.sync
        def _(sync):
            for f in kb.ops["sync"]:
                f(sync)

        @block.scalar
        def _(scalar):
            for f in kb.ops["scalar"]:
                f(scalar)

        @block.vector
        def _(vector):
            for f in kb.ops["vector"]:
                f(vector)

        @block.tensor
        def _(tensor):
            for f in kb.ops["tensor"]:
                f(tensor)

        @block.gpsimd
        def _(gpsimd):
            for f in kb.ops["gpsimd"]:
                f(gpsimd)

    return nc


def kernel(**inputs):
    in_maps = _host_inputs(inputs)
    nc = build_nc()
    res = run_bass_kernel_spmd(nc, in_maps, core_ids=list(range(NCORE)))
    perm = _tt_perm()
    out = np.zeros((B, L, D), np.float32)
    for c in range(NCORE):
        b, half = c >> 1, c & 1
        out[b, half * TL + perm] = res.results[c]["xout"]
    return out


# version 09 (final)
# speedup vs baseline: 1.4238x; 1.1937x over previous
import contextlib
import numpy as np
import ml_dtypes
import concourse.bass as bass
import concourse.mybir as mybir
from concourse.bass_utils import run_bass_kernel_spmd

F32 = mybir.dt.float32
BF16 = mybir.dt.bfloat16
AF = mybir.ActivationFunctionType
ALU = mybir.AluOpType
AXX = mybir.AxisListType.X

B, L, D, H, S, Q = 4, 4096, 1024, 2048, 16, 16
TL = 2048
NJ = TL // Q
NT = H // 128
ND = D // 128
NCORE = 8
EPS = 1e-5

DEBUG = False
CUT = None


def _tt_perm():
    slots = np.arange(TL)
    tau, j = slots // NJ, slots % NJ
    return j * Q + tau


def _rep(v):
    v = np.asarray(v, np.float32).reshape(1, -1)
    return np.ascontiguousarray(np.broadcast_to(v, (128, v.shape[1])))


def _host_inputs(inputs):
    perm = _tt_perm()
    x = np.ascontiguousarray(np.asarray(inputs["L0_attn_x"], np.float32))

    k = np.arange(D)
    ang = 2.0 * np.pi * np.outer(k, k) / D
    cdf = np.cos(ang).astype(np.float32)
    sdf = np.sin(ang).astype(np.float32)
    cd = cdf.astype(ml_dtypes.bfloat16)
    sd = sdf.astype(ml_dtypes.bfloat16)
    sn = (-sdf).astype(ml_dtypes.bfloat16)

    qq, rr = np.arange(Q), np.arange(Q)
    tcwv = _rep(np.broadcast_to((Q - 1.0 - qq)[:, None], (Q, Q)).reshape(-1))
    tcs2 = _rep(np.broadcast_to((rr + 1.0)[None, :], (Q, Q)).reshape(-1))
    tck8 = _rep(np.broadcast_to((qq + 0.0)[:, None], (Q, Q)).reshape(-1))

    dm = np.zeros((128, 128), np.float32)
    for p in range(128):
        dm[p, (p // 16) * 16:(p // 16 + 1) * 16] = 1.0
    dm = dm.astype(ml_dtypes.bfloat16)
    rst = np.ones((128, NT * NJ), np.float32)
    rst[:, ::NJ] = 0.0
    rst = rst.astype(ml_dtypes.bfloat16)

    def t2(w):
        return np.ascontiguousarray(np.asarray(w, np.float32).T)

    wxt, wgt, wdtt = t2(inputs["L0_attn_W_x"]), t2(inputs["L0_attn_W_g"]), t2(inputs["L0_attn_W_dt"])
    woutt = np.ascontiguousarray(np.asarray(inputs["L0_attn_W_out"], np.float32).T)
    wrt = np.ascontiguousarray(np.asarray(inputs["L0_ffn_w_freq_real"], np.float32).T)
    wit = np.ascontiguousarray(np.asarray(inputs["L0_ffn_w_freq_imag"], np.float32).T)

    def b2(b):
        return np.ascontiguousarray(np.asarray(b, np.float32).reshape(NT, 128).T)

    bx2, bg2, bdt2 = b2(inputs["L0_attn_b_x"]), b2(inputs["L0_attn_b_g"]), b2(inputs["L0_attn_b_dt"])
    cw = np.asarray(inputs["L0_attn_conv_w"], np.float32)
    convw2 = np.concatenate([b2(cw[:, 0, t]) for t in range(3)], axis=1)
    convb2 = b2(inputs["L0_attn_conv_b"])

    def hs2(a):
        return np.ascontiguousarray(np.asarray(a, np.float32)
                                    .reshape(NT, 128, S).transpose(1, 0, 2).reshape(128, NT * S))

    anat, bnat, cnat = hs2(inputs["L0_attn_A_raw"]), hs2(inputs["L0_attn_B_ssm"]), hs2(inputs["L0_attn_C_ssm"])

    boutB = _rep(inputs["L0_attn_b_out"])
    lngB = _rep(inputs["L0_attn_ln_g"])
    lnbB = _rep(inputs["L0_attn_ln_b"])
    fbB = _rep(inputs["L0_ffn_bias"])
    idf = np.eye(128, dtype=np.float32)

    in_maps = []
    for c in range(NCORE):
        b, half = c >> 1, c & 1
        t0 = half * TL
        xs = np.ascontiguousarray(x[b, t0 + perm])
        halo = np.zeros((2, D), np.float32)
        hm = np.zeros(2, np.float32)
        if t0 - 1 >= 0:
            halo[0] = x[b, t0 - 1]
            hm[0] = 1.0
        if t0 + TL < L:
            halo[1] = x[b, t0 + TL]
            hm[1] = 1.0
        cm = np.array([1.0 - half, float(half)], np.float32)
        sel16 = np.zeros((1, 16), np.float32)
        sel16[0, c] = 1.0
        sel16[0, 8 + c] = 1.0
        sel8 = np.zeros((1, 8), np.float32)
        sel8[0, c] = 1.0
        m = dict(
            xin=xs, xhalo=halo, halomask=_rep(hm), cmask=_rep(cm),
            wxt=wxt, wgt=wgt, wdtt=wdtt, woutt=woutt,
            wrtc=np.ascontiguousarray(wrt[:, c * 128:(c + 1) * 128]),
            witc=np.ascontiguousarray(wit[:, c * 128:(c + 1) * 128]),
            cdft=cd, sdft=sd, sneg=sn,
            cdcol=np.ascontiguousarray(cd[:, c * 128:(c + 1) * 128]),
            sdcol=np.ascontiguousarray(sd[:, c * 128:(c + 1) * 128]),
            bx2=bx2, bg2=bg2, bdt2=bdt2, convw2=convw2, convb2=convb2,
            anat=anat, bnat=bnat, cnat=cnat,
            tcwv=tcwv, tcs2=tcs2, tck8=tck8,
            dmask=dm, rstmask=rst, idf32=idf,
            boutB=boutB, lngB=lngB, lnbB=lnbB, fbB=fbB,
            sel16=_rep(sel16), sel8=_rep(sel8),
        )
        in_maps.append(m)
    return in_maps


class KB:
    ENGS = ("sync", "scalar", "vector", "tensor", "gpsimd")

    def __init__(self, nc, stack):
        self.nc = nc
        self.stack = stack
        self.ops = {e: [] for e in self.ENGS}
        self.sem = {}
        self.last = {e: (None, 0) for e in self.ENGS}

    def mksem(self, *names):
        for n in names:
            h = self.stack.enter_context(self.nc.semaphore(n))
            self.sem[n] = [h, 0]

    def dma(self, eng, out, in_, sem, **kw):
        rec = self.sem[sem]
        rec[1] += 16

        def f(e, out=out, in_=in_, h=rec[0], kw=kw):
            e.dma_start(out=out, in_=in_, **kw).then_inc(h, 16)
        self.ops[eng].append(f)

    def op(self, eng, build, sem=None):
        if sem is None:
            sem = "f" + eng[0]
        rec = self.sem[sem]
        rec[1] += 1
        h = rec[0]
        self.last[eng] = (sem, rec[1])

        def f(e, build=build, h=h):
            build(e).then_inc(h, 1)
        self.ops[eng].append(f)

    def bar(self, eng):
        name, v = self.last[eng]
        if v <= 0:
            return
        rec = self.sem[name]
        self.ops[eng].append(lambda e, h=rec[0], v=v: e.wait_ge(h, v))

    def wait(self, eng, sem, val=None):
        rec = self.sem[sem]
        v = rec[1] if val is None else val
        if v <= 0:
            return
        self.ops[eng].append(lambda e, h=rec[0], v=v: e.wait_ge(h, v))

    def val(self, sem):
        return self.sem[sem][1]


def build_nc():
    nc = bass.Bass("TRN2", target_bir_lowering=False, debug=False)
    DT = nc.dram_tensor
    skind = "ExternalOutput" if DEBUG else "Internal"

    ins = {}
    for name, shape, dt in [
        ("xin", [TL, D], F32), ("xhalo", [2, D], F32),
        ("halomask", [128, 2], F32), ("cmask", [128, 2], F32),
        ("wxt", [D, H], F32), ("wgt", [D, H], F32), ("wdtt", [D, H], F32),
        ("woutt", [H, D], F32), ("wrtc", [D, 128], F32), ("witc", [D, 128], F32),
        ("cdft", [D, D], BF16), ("sdft", [D, D], BF16), ("sneg", [D, D], BF16),
        ("cdcol", [D, 128], BF16), ("sdcol", [D, 128], BF16),
        ("bx2", [128, NT], F32), ("bg2", [128, NT], F32), ("bdt2", [128, NT], F32),
        ("convw2", [128, 3 * NT], F32), ("convb2", [128, NT], F32),
        ("anat", [128, NT * S], F32), ("bnat", [128, NT * S], F32), ("cnat", [128, NT * S], F32),
        ("tcwv", [128, 256], F32), ("tcs2", [128, 256], F32), ("tck8", [128, 256], F32),
        ("dmask", [128, 128], BF16), ("rstmask", [128, NT * NJ], BF16),
        ("idf32", [128, 128], F32),
        ("boutB", [128, D], F32), ("lngB", [128, D], F32), ("lnbB", [128, D], F32),
        ("fbB", [128, D], F32),
        ("sel16", [128, 16], F32), ("sel8", [128, 8], F32),
    ]:
        ins[name] = DT(name, shape, dt, kind="ExternalInput").ap()

    xout = DT("xout", [TL, D], F32, kind="ExternalOutput").ap()
    udram = DT("udram", [H, TL], BF16, kind=skind).ap()
    ydram = DT("ydram", [H, TL], BF16, kind=skind).ap()
    pdram = DT("pdram", [H, 1024], F32, kind=skind).ap()
    dtin = DT("dtin", [128, NT], F32, kind="Internal").ap()
    dtout = DT("dtout", [128, NT], F32, kind="Internal", addr_space="Shared").ap()
    u12in = DT("u12in", [16 * 128, D], BF16, kind="Internal").ap()
    u12out = DT("u12out", [16 * 128, D], BF16, kind="Internal", addr_space="Shared").ap()
    mtin = DT("mtin", [D, D], BF16, kind="Internal").ap()
    mtout = DT("mtout", [D, D], BF16, kind="Internal", addr_space="Shared").ap()
    ssin = DT("ssin", [128, NT * S], F32, kind="Internal").ap()
    if DEBUG:
        tap_dt = DT("tap_dt", [128, NT], F32, kind="ExternalOutput").ap()
        tap_xn = DT("tap_xn", [128, 16384], BF16, kind="ExternalOutput").ap()
        tap_u = DT("tap_u", [128, TL], F32, kind="ExternalOutput").ap()
        tap_uh = DT("tap_uh", [128, 2 * NT], F32, kind="ExternalOutput").ap()
        tap_u12 = DT("tap_u12", [16 * 128, D], BF16, kind="ExternalOutput").ap()
        tap_ms = DT("tap_ms", [128, 32], F32, kind="ExternalOutput").ap()
        tap_mt = DT("tap_mt", [D, D], BF16, kind="ExternalOutput").ap()
        tap_sh = DT("tap_sh", [128, NT * S], F32, kind="ExternalOutput").ap()
    ssout = DT("ssout", [128, NT * S], F32, kind="Internal").ap()

    with contextlib.ExitStack() as stack:
        E = stack.enter_context

        c_hm = E(nc.sbuf_tensor("c_hm", [128, 2], F32))
        c_cm = E(nc.sbuf_tensor("c_cm", [128, 2], F32))
        c_bx = E(nc.sbuf_tensor("c_bx", [128, NT], F32))
        c_bg = E(nc.sbuf_tensor("c_bg", [128, NT], F32))
        c_bdt = E(nc.sbuf_tensor("c_bdt", [128, NT], F32))
        c_cw = E(nc.sbuf_tensor("c_cw", [128, 3 * NT], F32))
        c_cb = E(nc.sbuf_tensor("c_cb", [128, NT], F32))
        c_an = E(nc.sbuf_tensor("c_an", [128, 256], F32))
        c_bn = E(nc.sbuf_tensor("c_bn", [128, 256], F32))
        c_cn = E(nc.sbuf_tensor("c_cn", [128, 256], F32))
        c_tcwv = E(nc.sbuf_tensor("c_tcwv", [128, 256], F32))
        c_tcs2 = E(nc.sbuf_tensor("c_tcs2", [128, 256], F32))
        c_tck8 = E(nc.sbuf_tensor("c_tck8", [128, 256], F32))
        c_dm = E(nc.sbuf_tensor("c_dm", [128, 128], BF16))
        c_rst = E(nc.sbuf_tensor("c_rst", [128, NT * NJ], BF16))
        c_id = E(nc.sbuf_tensor("c_id", [128, 128], F32))
        c_s16 = E(nc.sbuf_tensor("c_s16", [128, 16], F32))
        c_s8 = E(nc.sbuf_tensor("c_s8", [128, 8], F32))

        bigA = E(nc.sbuf_tensor("bigA", [128, 24576], BF16))
        bigB = E(nc.sbuf_tensor("bigB", [128, 32768], BF16))
        yffn = E(nc.sbuf_tensor("yffn", [128, NT * 1024], BF16))

        xnbf = bigA[:, 0:16384]
        mtres = bigA[:, 16384:24576]
        wv16 = bigA[:, 0:2048]
        ws16 = bigA[:, 2048:4096]
        wi16 = bigA[:, 4096:6144]
        aq16 = bigA[:, 20480:22528]
        strps = [bigA[:, 8192:9216], bigA[:, 9216:10240]]
        gbs = [bigA[:, 10240:12288], bigA[:, 12288:14336]]
        ygbs = [bigA[:, 14336:16384], bigA[:, 16384:18432]]
        sout = bigA[:, 18432:20480]
        wob = bigA[:, 0:16384]

        bigBf = bigB[:].bitcast(F32)
        ut = [bigBf[:, 0:2048], bigBf[:, 2048:4096]]
        cvt = bigBf[:, 4096:6144]
        dsd = bigBf[:, 6144:8192]
        ums = [bigB[:, 16384:18432], bigB[:, 18432:20480]]
        csb = bigB[:, 20480:23552]
        wfb = bigB[:, 23552:25600]
        u1b = bigB[:, 25600:26624]
        u2b = bigB[:, 26624:27648]
        mtblk = bigB[:, 27648:28672]
        stg = bigB[:, 28672:29696]
        mcs = bigB[:, 20480:22528]
        u12mv = bigB[:, 22528:23552]
        stg2 = bigB[:, 22528:23552]
        wpju = [bigB[:, 23552:24576], bigB[:, 24576:25600]]
        wpjg = [bigB[:, 29696:30720], bigB[:, 30720:31744]]
        cfb_f = bigBf[:, 0:1024]
        yhalf = bigB[:, 0:16384]
        tailf = bigBf
        rtk = tailf[:, 8192:9216]
        snm = tailf[:, 9216:10240]
        cbout_t = tailf[:, 10240:11264]
        ptt = tailf[:, 11264:12288]
        clng_t = tailf[:, 12288:13312]
        ot0 = tailf[:, 13312:14336]
        t3t = tailf[:, 14336:15360]
        clnb_t = tailf[:, 15360:16384]

        xt0 = E(nc.sbuf_tensor("xt0", [128, D], F32))
        xt1 = E(nc.sbuf_tensor("xt1", [128, D], F32))
        xts = [xt0, xt1]
        xnt = E(nc.sbuf_tensor("xnt", [128, D], F32))
        xh = E(nc.sbuf_tensor("xh", [128, D], F32))
        sqsc = E(nc.sbuf_tensor("sqsc", [128, D], F32))
        esc = E(nc.sbuf_tensor("esc", [128, 512], F32))
        stat = E(nc.sbuf_tensor("stat", [128, 16], F32))
        mstats = E(nc.sbuf_tensor("mstats", [128, 32], F32))
        xnhbf = E(nc.sbuf_tensor("xnhbf", [128, 16], BF16))
        uh = E(nc.sbuf_tensor("uh", [128, 2 * NT], F32))
        dtacc = E(nc.sbuf_tensor("dtacc", [128, 64], F32))
        dtv = E(nc.sbuf_tensor("dtv", [128, NT], F32))
        lnad = E(nc.sbuf_tensor("lnad", [128, 256], F32))
        a_f = E(nc.sbuf_tensor("a_f", [128, 256], F32))
        adt = E(nc.sbuf_tensor("adt", [128, 256], F32))
        bcn = E(nc.sbuf_tensor("bcn", [128, 256], F32))
        rca = E(nc.sbuf_tensor("rca", [128, 256], F32))
        dsc1 = E(nc.sbuf_tensor("dsc1", [128, 256], F32))
        dsc2 = E(nc.sbuf_tensor("dsc2", [128, 256], F32))
        cd0 = E(nc.sbuf_tensor("cd0", [128, 1024], F32))
        cd1 = E(nc.sbuf_tensor("cd1", [128, 1024], F32))
        kpad = E(nc.sbuf_tensor("kpad", [128, 31], F32))
        shall = E(nc.sbuf_tensor("shall", [128, 256], F32))
        sheff = E(nc.sbuf_tensor("sheff", [128, 256], F32))
        sendst = E(nc.sbuf_tensor("sendst", [128, 256], F32))
        vfix = E(nc.sbuf_tensor("vfix", [128, 16], F32))
        sheffb = E(nc.sbuf_tensor("sheffb", [128, 256], BF16))

        c_eps = E(nc.sbuf_tensor("c_eps", [128, 1], F32))
        c_cb2 = E(nc.sbuf_tensor("c_cb2", [128, NT], F32))
        c_mball = E(nc.sbuf_tensor("c_mball", [128, 2 * NT], F32))
        c_hm1 = E(nc.sbuf_tensor("c_hm1", [128, 2], F32))
        wearly = E(nc.sbuf_tensor("wearly", [128, 4096], BF16))
        psumA = E(nc.psum_tensor("psumA", [128, 2048], F32))
        psumB = E(nc.psum_tensor("psumB", [128, 2048], F32))

        kb = KB(nc, stack)
        marks = {}

        def mark(name):
            marks[name] = {e: len(kb.ops[e]) for e in kb.ENGS}
        kb.mksem("qc", "qx", "qw", "qm", "qpd", "qg", "qs", "qyd", "qyl",
                 "qf", "qdt", "qar", "qo", "qstg", "qxr",
                 "vx", "vxt", "tx", "sx", "tp", "tpt", "sdt", "se", "vc",
                 "vd", "gar", "vb", "sb", "vw", "tv", "vs", "ty", "sy",
                 "tf", "tfk", "tfs", "sf", "sf2", "tt2", "vt", "st2", "gp",
                 "fv", "fs", "ft", "fg", "fy",
                 "qx0", "qx1", "qwf", "qwd0", "qwd1", "qwu0", "qwu1",
                 "qwg0", "qwg1", "qg0", "qg1", "qs0", "qs1", "qxr0", "qxr1",
                 "qmt", "qfb", "qtap", "qtap2", "seu", "seg", "qwd2", "qwd3",
                 "qpd0", "qpd1", "qstg2", "qwe0", "qwe1",
                 "qyl0", "qyl1", "qyl2", "qyl3", "qu0", "qu1")

        for nm, dst in [("halomask", c_hm), ("cmask", c_cm), ("bx2", c_bx),
                        ("bg2", c_bg), ("bdt2", c_bdt), ("convw2", c_cw),
                        ("convb2", c_cb), ("anat", c_an), ("bnat", c_bn),
                        ("cnat", c_cn), ("tcwv", c_tcwv), ("tcs2", c_tcs2),
                        ("tck8", c_tck8), ("dmask", c_dm), ("rstmask", c_rst),
                        ("idf32", c_id), ("sel16", c_s16), ("sel8", c_s8)]:
            kb.dma("sync", dst[:], ins[nm], "qc")
        v_qc = kb.val("qc")

        kb.dma("gpsimd", wfb[:, 0:1024].rearrange("p (k d) -> p k d", d=128),
               ins["wrtc"].rearrange("(k p) d -> p k d", k=8), "qwf")
        kb.dma("gpsimd", wfb[:, 1024:2048].rearrange("p (k d) -> p k d", d=128),
               ins["witc"].rearrange("(k p) d -> p k d", k=8), "qwf")
        v_qw_wf = kb.val("qwf")
        for tt in range(2):
            kb.dma("gpsimd", wearly[:, tt * 2048:tt * 2048 + 1024].rearrange("p (k h) -> p k h", h=128),
                   ins["wxt"].rearrange("(k p) h -> p k h", k=8)[:, :, tt * 128:(tt + 1) * 128],
                   "qwe" + str(tt))
            kb.dma("gpsimd", wearly[:, tt * 2048 + 1024:tt * 2048 + 2048].rearrange("p (k h) -> p k h", h=128),
                   ins["wgt"].rearrange("(k p) h -> p k h", k=8)[:, :, tt * 128:(tt + 1) * 128],
                   "qwe" + str(tt))

        kb.wait("vector", "qc", v_qc)
        kb.op("vector", lambda v: v.tensor_add(dsc1[:, 0:NT], c_cw[:, 0:NT], c_cw[:, NT:2 * NT]), None)
        kb.bar("vector")
        kb.op("vector", lambda v: v.tensor_add(dsc1[:, 0:NT], dsc1[:, 0:NT], c_cw[:, 2 * NT:3 * NT]), None)
        kb.bar("vector")
        kb.op("vector", lambda v: v.tensor_mul(c_cb2[:], c_bx[:], dsc1[:, 0:NT]), None)
        kb.bar("vector")
        kb.op("vector", lambda v: v.tensor_add(c_cb2[:], c_cb2[:], c_cb[:]), None)
        kb.op("vector", lambda v: v.tensor_scalar_add(c_hm1[:], c_hm[:], -1.0), None)
        kb.bar("vector")
        kb.op("vector", lambda v: v.tensor_tensor(
            out=c_mball[:].rearrange("p (t e) -> p t e", e=2),
            in0=c_hm1[:].unsqueeze(1).broadcast_to([128, NT, 2]),
            in1=c_bx[:].unsqueeze(2).broadcast_to([128, NT, 2]),
            op=ALU.mult), "vb")

        kb.op("vector", lambda v: v.memset(c_eps[:], EPS), "vx")
        kb.op("vector", lambda v: v.memset(xh[:], 0.0), "vx")
        kb.wait("sync", "vx")
        kb.dma("sync", xh[0:2, :], ins["xhalo"], "qx")
        v_qx_h = kb.val("qx")
        for tt in range(NT):
            if tt >= 2:
                kb.wait("sync", "vxt", tt - 1)
            kb.dma("sync", xts[tt % 2][:], ins["xin"][tt * 128:(tt + 1) * 128, :],
                   "qx0" if tt % 2 == 0 else "qx1")

        def ln_tile(src, out_ap, scr, save_col=None):
            sm = stat[:, 0:1]
            sq = stat[:, 1:2]
            sv = stat[:, 2:3]
            sr = stat[:, 3:4]
            m2 = stat[:, 4:5]
            kb.bar("vector")
            kb.bar("scalar")
            kb.op("vector", lambda v, src=src, sm=sm: v.reduce_sum(sm, src, axis=AXX), "vx")
            kb.bar("vector")
            kb.wait("scalar", "vx")
            kb.op("scalar", lambda s, src=src, sq=sq, scr=scr: s.activation(
                out=scr, in_=src, func=AF.Square, accum_out=sq), "sx")
            kb.op("vector", lambda v, sm=sm: v.tensor_scalar_mul(sm, sm, 1.0 / D), "vx")
            kb.wait("vector", "sx")
            kb.op("vector", lambda v, sq=sq, sv=sv: v.tensor_scalar_mul(sv, sq, 1.0 / D), "vx")
            kb.bar("vector")
            kb.op("vector", lambda v, sm=sm, m2=m2: v.tensor_mul(m2, sm, sm), "vx")
            kb.bar("vector")
            kb.op("vector", lambda v, sv=sv, m2=m2: v.tensor_sub(sv, sv, m2), "vx")
            kb.wait("scalar", "vx")
            kb.op("scalar", lambda s, sv=sv: s.activation(out=sv, in_=sv, func=AF.Ln, bias=c_eps[:, 0:1]), "sx")
            kb.bar("scalar")
            kb.op("scalar", lambda s, sv=sv, sr=sr: s.activation(out=sr, in_=sv, func=AF.Exp, scale=-0.5), "sx")
            kb.wait("vector", "sx")
            kb.op("vector", lambda v, src=src, out_ap=out_ap, sm=sm, sr=sr: v.tensor_scalar(
                out=out_ap, in0=src, scalar1=sm, scalar2=sr,
                op0=ALU.subtract, op1=ALU.mult), "vx")
            if save_col is not None:
                kb.op("vector", lambda v, sm=sm, save_col=save_col: v.tensor_copy(
                    mstats[:, save_col:save_col + 1], sm), "vx")
                kb.op("vector", lambda v, sr=sr, save_col=save_col: v.tensor_copy(
                    mstats[:, 16 + save_col:16 + save_col + 1], sr), "vx")

        kb.wait("vector", "qx", v_qx_h)
        ln_tile(xh[:], xh[:], sqsc[:])
        kb.wait("tensor", "vx")
        kb.wait("tensor", "qc", v_qc)
        for kd in range(ND):
            kb.op("tensor", lambda t, kd=kd: t.transpose(
                psumA[:, 1024 + kd * 128:1024 + (kd + 1) * 128],
                xh[:, kd * 128:(kd + 1) * 128], c_id[:]), "tx")
        kb.wait("scalar", "tx")
        for kd in range(ND):
            kb.op("scalar", lambda s, kd=kd: s.copy(
                out=xnhbf[:, kd * 2:kd * 2 + 2],
                in_=psumA[:, 1024 + kd * 128:1024 + kd * 128 + 2]), "sx")
        v_sx_halo = kb.val("sx")

        for tt in range(NT):
            kb.wait("vector", "qx0" if tt % 2 == 0 else "qx1", 16 * (tt // 2 + 1))
            kb.wait("vector", "tx")
            ln_tile(xts[tt % 2][:], xnt[:], sqsc[:], save_col=tt)
            kb.op("vector", lambda v: v.tensor_copy(stat[:, 5:6], stat[:, 0:1]), "vxt")
            kb.bar("vector")
            kb.wait("tensor", "vx")
            for h4 in range(2):
                kb.wait("tensor", "sx")
                for kk in range(4):
                    kd = h4 * 4 + kk
                    kb.op("tensor", lambda t, h4=h4, kk=kk, kd=kd: t.transpose(
                        psumA[:, h4 * 512 + kk * 128:h4 * 512 + (kk + 1) * 128],
                        xnt[:, kd * 128:(kd + 1) * 128], c_id[:]), "tx")
                kb.wait("scalar", "tx")
                kb.op("scalar", lambda s, tt=tt, h4=h4: s.copy(
                    out=xnbf.rearrange("p (d t) -> p d t", t=TL)[:, h4 * 4:(h4 + 1) * 4, tt * 128:(tt + 1) * 128],
                    in_=psumA[:, h4 * 512:(h4 + 1) * 512].rearrange("p (d c) -> p d c", c=128)), "sx")
        v_sx_xn = kb.val("sx")
        if DEBUG:
            kb.wait("sync", "sx", v_sx_xn)
            kb.dma("sync", tap_xn, xnbf, "qtap")
            kb.wait("sync", "vx")
            kb.dma("sync", tap_ms, mstats[:], "qtap")

        mark("input")
        v_qf0 = kb.val("qf")
        for kk in range(ND):
            kb.wait("sync", "tfk")
            kb.dma("sync", csb[:, 0:1024], ins["cdft"][kk * 128:(kk + 1) * 128, :], "qf")
            kb.dma("sync", csb[:, 1024:2048], ins["sdft"][kk * 128:(kk + 1) * 128, :], "qf")
            kb.dma("sync", csb[:, 2048:3072], ins["sneg"][kk * 128:(kk + 1) * 128, :], "qf")
            kb.wait("tensor", "qf")
            if kk == 0:
                kb.wait("tensor", "qwf", v_qw_wf)
            for nb in range(2):
                kb.op("tensor", lambda t, kk=kk, nb=nb: t.matmul(
                    psumB[:, nb * 512:(nb + 1) * 512],
                    wfb[:, 0:1024].rearrange("p (k d) -> p k d", d=128)[:, kk, :],
                    csb[:, nb * 512:(nb + 1) * 512],
                    start=(kk == 0), stop=False), None)
                kb.op("tensor", lambda t, kk=kk, nb=nb: t.matmul(
                    psumB[:, nb * 512:(nb + 1) * 512],
                    wfb[:, 1024:2048].rearrange("p (k d) -> p k d", d=128)[:, kk, :],
                    csb[:, 2048 + nb * 512:2048 + (nb + 1) * 512],
                    start=False, stop=(kk == ND - 1)), None)
                kb.op("tensor", lambda t, kk=kk, nb=nb: t.matmul(
                    psumB[:, 1024 + nb * 512:1024 + (nb + 1) * 512],
                    wfb[:, 1024:2048].rearrange("p (k d) -> p k d", d=128)[:, kk, :],
                    csb[:, nb * 512:(nb + 1) * 512],
                    start=(kk == 0), stop=False), None)
                kb.op("tensor", lambda t, kk=kk, nb=nb: t.matmul(
                    psumB[:, 1024 + nb * 512:1024 + (nb + 1) * 512],
                    wfb[:, 0:1024].rearrange("p (k d) -> p k d", d=128)[:, kk, :],
                    csb[:, 1024 + nb * 512:1024 + (nb + 1) * 512],
                    start=False, stop=(kk == ND - 1)),
                    "tfk" if nb == 1 else None)
        kb.wait("scalar", "tfk", 8)
        kb.op("scalar", lambda s: s.copy(out=u1b[:], in_=psumB[:, 0:1024]), "sf")
        kb.op("scalar", lambda s: s.copy(out=u2b[:], in_=psumB[:, 1024:2048]), "sf")
        kb.wait("vector", "sf")
        for pos in range(16):
            srcb = u1b if pos < 8 else u2b
            kb.wait("vector", "qstg")
            kb.op("vector", lambda v, srcb=srcb, pos=pos: v.tensor_scalar_mul(
                stg[:], srcb[:], c_s16[:, pos:pos + 1]), "vb")
            kb.wait("scalar", "vb")
            kb.dma("scalar", u12in[pos * 128:(pos + 1) * 128, :], stg[:], "qstg")
        def pj_load(wname, tt, bufs, gate_sem, gate_val, sembase):
            if gate_val > 0:
                kb.wait("gpsimd", gate_sem, gate_val)
            kb.dma("gpsimd", bufs[tt % 2].rearrange("p (k h) -> p k h", h=128),
                   ins[wname].rearrange("(k p) h -> p k h", k=8)[:, :, tt * 128:(tt + 1) * 128],
                   sembase + str(tt % 2))

        wpj4 = [wpjg[0], wpjg[1], wpju[0], wpju[1]]

        def dt_load(tt):
            if tt == 2:
                kb.wait("gpsimd", "tfk", 8)
            if tt >= 4:
                kb.wait("gpsimd", "tp", 4 * (tt - 3))
            kb.dma("gpsimd", wpj4[tt % 4].rearrange("p (k h) -> p k h", h=128),
                   ins["wdtt"].rearrange("(k p) h -> p k h", k=8)[:, :, tt * 128:(tt + 1) * 128],
                   "qwd" + str(tt % 4))

        for tt in range(4):
            dt_load(tt)
        kb.wait("gpsimd", "qstg")
        kb.op("gpsimd", lambda g: g.collective_compute(
            "AllReduce", ALU.add, replica_groups=[list(range(8))],
            ins=[u12in], outs=[u12out]), "gar")
        v_gar_u12 = kb.val("gar")
        for tt in range(4, NT):
            dt_load(tt)
        pj_load("wgt", 2, wpjg, "tp", 52, "qwg")
        pj_load("wgt", 3, wpjg, "tp", 56, "qwg")
        pj_load("wxt", 2, wpju, "tp", 60, "qwu")
        pj_load("wxt", 3, wpju, "tp", 64, "qwu")
        kb.wait("gpsimd", "tfk", 8)

        if DEBUG:
            kb.wait("sync", "gar", v_gar_u12)
            kb.dma("sync", tap_u12, u12out, "qtap")

        mark("ubuild")
        kb.wait("tensor", "sx", v_sx_xn)
        for tt in range(NT):
            kb.wait("tensor", "qwd" + str(tt % 4), 16 * (tt // 4 + 1))
            for nb in range(4):
                if 4 * tt + nb - 1 > 0:
                    kb.wait("tensor", "sdt", 4 * tt + nb - 1)
                for kk in range(ND):
                    kb.op("tensor", lambda t, tt=tt, nb=nb, kk=kk: t.matmul(
                        psumA[:, (nb % 2) * 512:(nb % 2 + 1) * 512],
                        wpj4[tt % 4].rearrange("p (k h) -> p k h", h=128)[:, kk, :],
                        xnbf[:, kk * TL + nb * 512:kk * TL + (nb + 1) * 512],
                        start=(kk == 0), stop=(kk == ND - 1)), "tp" if kk == ND - 1 else None)
                kb.wait("scalar", "tp")
                kb.bar("scalar")
                kb.op("scalar", lambda s, tt=tt, nb=nb: s.activation(
                    out=esc[:], in_=psumA[:, (nb % 2) * 512:(nb % 2 + 1) * 512],
                    func=AF.Exp, bias=c_bdt[:, tt:tt + 1]), None)
                kb.bar("scalar")
                kb.op("scalar", lambda s, tt=tt, nb=nb: s.activation(
                    out=esc[:], in_=esc[:], func=AF.Ln, bias=1.0,
                    accum_out=dtacc[:, tt * 4 + nb:tt * 4 + nb + 1]), "sdt")

        kb.wait("scalar", "qc", v_qc)
        kb.op("scalar", lambda s: s.activation(out=a_f[:], in_=c_an[:], func=AF.Exp), None)
        kb.bar("scalar")
        kb.op("scalar", lambda s: s.activation(out=a_f[:], in_=a_f[:], func=AF.Ln, bias=1.0), "sb")
        kb.wait("vector", "sb")
        kb.op("vector", lambda v: v.tensor_scalar_mul(a_f[:], a_f[:], -1.0), "vb")
        v_vb_af = kb.val("vb")

        kb.wait("vector", "sdt")
        kb.op("vector", lambda v: v.tensor_reduce(
            out=dtv[:], in_=dtacc[:].rearrange("p (t f) -> p t f", f=4),
            axis=AXX, op=ALU.add), "vd")
        kb.wait("scalar", "vd")
        kb.dma("scalar", dtin, dtv[:], "qdt")
        for tt in range(4, 8):
            pj_load("wxt", tt, wpju, "tp", 64 + 9 * max(0, tt - 1), "qwu")
            pj_load("wgt", tt, wpjg, "tp", 64 + 9 * max(0, tt - 1), "qwg")
        kb.wait("gpsimd", "qdt")
        kb.op("gpsimd", lambda g: g.collective_compute(
            "AllReduce", ALU.add, replica_groups=[list(range(8))],
            ins=[dtin], outs=[dtout]), "gar")
        v_gar_dt = kb.val("gar")
        kb.wait("sync", "gar", v_gar_u12)
        kb.dma("sync", mcs[:, 0:1024].rearrange("p (k d) -> p k d", d=128),
               ins["cdcol"].rearrange("(k p) d -> p k d", k=8), "qf")
        kb.dma("sync", mcs[:, 1024:2048].rearrange("p (k d) -> p k d", d=128),
               ins["sdcol"].rearrange("(k p) d -> p k d", k=8), "qf")
        v_qf_mcs = kb.val("qf")
        u12q = [yffn[:, 0:4096], yffn[:, 4096:8192]]
        for q in range(2):
            kb.dma("sync", u12q[q % 2].rearrange("p (j c) -> p j c", j=4),
                   u12out[q * 512:(q + 1) * 512].rearrange("(j p) c -> p j c", j=4),
                   "qu" + str(q % 2))
        for q in range(2, 4):
            kb.wait("sync", "tfs", 4 * (q - 1))
            kb.dma("sync", u12q[q % 2].rearrange("p (j c) -> p j c", j=4),
                   u12out[q * 512:(q + 1) * 512].rearrange("(j p) c -> p j c", j=4),
                   "qu" + str(q % 2))
        kb.wait("sync", "gar", v_gar_dt)
        kb.dma("sync", dtv[:], dtout, "qdt")
        mark("dt")
        kb.wait("tensor", "qf", v_qf_mcs)
        kb.wait("tensor", "sx", v_sx_xn)
        for q in range(4):
            kb.wait("tensor", "qu" + str(q % 2), 16 * (q // 2 + 1))
            half = 0 if q < 2 else 1024
            for j in range(4):
                kk = (q % 2) * 4 + j
                for nb in range(2):
                    kb.op("tensor", lambda t, q=q, j=j, kk=kk, nb=nb, half=half: t.matmul(
                        psumA[:, 1024 + nb * 512:1024 + (nb + 1) * 512],
                        mcs[:, half:half + 1024].rearrange("p (k d) -> p k d", d=128)[:, kk, :],
                        u12q[q % 2][:, j * 1024 + nb * 512:j * 1024 + (nb + 1) * 512],
                        start=(q == 0 and j == 0), stop=(q == 3 and j == 3)),
                        "tfs" if nb == 1 else None)
        kb.wait("scalar", "tfs", kb.val("tfs"))
        kb.op("scalar", lambda s: s.activation(out=mtblk[:], in_=psumA[:, 1024:2048],
                                               func=AF.Copy, scale=1.0 / D), "sf")
        kb.wait("vector", "sf")
        kb.wait("vector", "tfs", 16)
        v_qstg_b = kb.val("qstg")
        v_qstg2_b = kb.val("qstg2")
        for pos in range(8):
            sbuf_s = stg if pos % 2 == 0 else stg2
            qs_ = "qstg" if pos % 2 == 0 else "qstg2"
            base = v_qstg_b if pos % 2 == 0 else v_qstg2_b
            if pos >= 2:
                kb.wait("vector", qs_, base + 16 * (pos // 2))
            kb.op("vector", lambda v, pos=pos, sbuf_s=sbuf_s: v.tensor_scalar_mul(
                sbuf_s[:], mtblk[:], c_s8[:, pos:pos + 1]), "vb")
            kb.wait("scalar", "vb")
            kb.dma("scalar", mtin[pos * 128:(pos + 1) * 128, :], sbuf_s[:], qs_)
        kb.wait("gpsimd", "qstg")
        kb.wait("gpsimd", "qstg2")
        kb.op("gpsimd", lambda g: g.collective_compute(
            "AllReduce", ALU.add, replica_groups=[list(range(8))],
            ins=[mtin], outs=[mtout]), "gar")
        v_gar_mt = kb.val("gar")
        kb.wait("sync", "gar", v_gar_mt)
        kb.dma("sync", mtres.rearrange("p (k d) -> p k d", d=1024),
               mtout.rearrange("(k p) d -> p k d", k=8), "qmt")
        v_qf_mt = kb.val("qmt")
        if DEBUG:
            kb.dma("sync", tap_mt, mtout, "qtap")

        mark("mt")
        for tt in range(8, NT):
            pj_load("wxt", tt, wpju, "tp", 64 + 9 * max(0, tt - 1), "qwu")
            pj_load("wgt", tt, wpjg, "tp", 64 + 9 * max(0, tt - 1), "qwg")

        kb.wait("tensor", "sdt", 64)
        kb.wait("tensor", "sf", 2)
        TPB = 64
        for tt in range(NT):
            upj = wearly[:, tt * 2048:tt * 2048 + 1024] if tt < 2 else wpju[tt % 2]
            gpj = wearly[:, tt * 2048 + 1024:tt * 2048 + 2048] if tt < 2 else wpjg[tt % 2]
            if DEBUG and tt == 2:
                kb.wait("scalar", "qtap2")
            if tt < 2:
                kb.wait("tensor", "qwe" + str(tt), 32)
            else:
                kb.wait("tensor", "qwu" + str(tt % 2), 16 * (tt // 2))
            for nb in range(4):
                if 4 * tt + nb - 1 > 0:
                    kb.wait("tensor", "seu", 4 * tt + nb - 1)
                for kk in range(ND):
                    kb.op("tensor", lambda t, upj=upj, nb=nb, kk=kk: t.matmul(
                        psumA[:, (nb % 2) * 512:(nb % 2 + 1) * 512],
                        upj.rearrange("p (k h) -> p k h", h=128)[:, kk, :],
                        xnbf[:, kk * TL + nb * 512:kk * TL + (nb + 1) * 512],
                        start=(kk == 0), stop=(kk == ND - 1)), "tp" if kk == ND - 1 else None)
                kb.wait("scalar", "tp", TPB + 9 * tt + nb + 1)
                if tt >= 2:
                    kb.wait("scalar", "vc", 2 * tt - 2)
                kb.op("scalar", lambda s, tt=tt, nb=nb: s.copy(
                    out=ut[tt % 2][:, nb * 512:(nb + 1) * 512],
                    in_=psumA[:, (nb % 2) * 512:(nb % 2 + 1) * 512]), "seu")
            if tt == 0:
                kb.wait("tensor", "sx", v_sx_halo)
                kb.wait("tensor", "sf", 3)
            if tt >= 1:
                kb.wait("tensor", "vc", 2 * tt - 1)
            for kk in range(ND):
                kb.op("tensor", lambda t, upj=upj, kk=kk: t.matmul(
                    psumA[:, 1024:1026],
                    upj.rearrange("p (k h) -> p k h", h=128)[:, kk, :],
                    xnhbf[:, kk * 2:(kk + 1) * 2],
                    start=(kk == 0), stop=(kk == ND - 1)), "tp" if kk == ND - 1 else None)
            kb.wait("vector", "tp", TPB + 9 * tt + 5)
            kb.op("vector", lambda v, tt=tt: v.tensor_mul(
                uh[:, tt * 2:(tt + 1) * 2], psumA[:, 1024:1026], c_hm[:]), None)
            kb.bar("vector")
            kb.op("vector", lambda v, tt=tt: v.tensor_add(
                uh[:, tt * 2:(tt + 1) * 2], uh[:, tt * 2:(tt + 1) * 2],
                c_mball[:, tt * 2:(tt + 1) * 2]), "vc")
            kb.bar("vector")
            if tt >= 2:
                kb.wait("tensor", "qwg" + str(tt % 2), 16 * (tt // 2))
            if tt >= 1:
                kb.wait("tensor", "vc", 2 * tt)
            for nb in range(4):
                for kk in range(ND):
                    kb.op("tensor", lambda t, gpj=gpj, nb=nb, kk=kk: t.matmul(
                        psumB[:, nb * 512:(nb + 1) * 512],
                        gpj.rearrange("p (k h) -> p k h", h=128)[:, kk, :],
                        xnbf[:, kk * TL + nb * 512:kk * TL + (nb + 1) * 512],
                        start=(kk == 0), stop=(kk == ND - 1)), "tp" if kk == ND - 1 else None)
            for nb in range(4):
                kb.wait("scalar", "tp", TPB + 9 * tt + 6 + nb)
                kb.op("scalar", lambda s, tt=tt, nb=nb: s.activation(
                    out=psumB[:, nb * 512:(nb + 1) * 512],
                    in_=psumB[:, nb * 512:(nb + 1) * 512],
                    func=AF.Sigmoid, bias=c_bg[:, tt:tt + 1]), "seg")
            if DEBUG and tt == 0:
                kb.wait("scalar", "seu", 4)
                kb.dma("scalar", tap_u, ut[0][:], "qtap2")
            u_c = ut[tt % 2]
            um = ums[tt % 2]
            w0 = c_cw[:, tt:tt + 1]
            w1 = c_cw[:, NT + tt:NT + tt + 1]
            w2 = c_cw[:, 2 * NT + tt:2 * NT + tt + 1]
            cb = c_cb2[:, tt:tt + 1]
            kb.wait("vector", "seu", 4 * tt + 4)
            kb.op("vector", lambda v, u_c=u_c, w1=w1, cb=cb: v.tensor_scalar(
                out=cvt[:, 0:2048], in0=u_c[:], scalar1=w1, scalar2=cb,
                op0=ALU.mult, op1=ALU.add), None)
            kb.bar("vector")
            kb.op("vector", lambda v, u_c=u_c, w0=w0: v.scalar_tensor_tensor(
                out=cvt[:, 128:2048], in0=u_c[:, 0:1920], scalar=w0,
                in1=cvt[:, 128:2048], op0=ALU.mult, op1=ALU.add), None)
            kb.op("vector", lambda v, u_c=u_c, w0=w0: v.scalar_tensor_tensor(
                out=cvt[:, 1:128], in0=u_c[:, 1920:2047], scalar=w0,
                in1=cvt[:, 1:128], op0=ALU.mult, op1=ALU.add), None)
            kb.op("vector", lambda v, tt=tt, w0=w0: v.scalar_tensor_tensor(
                out=cvt[:, 0:1], in0=uh[:, tt * 2:tt * 2 + 1], scalar=w0,
                in1=cvt[:, 0:1], op0=ALU.mult, op1=ALU.add), None)
            kb.bar("vector")
            kb.op("vector", lambda v, u_c=u_c, w2=w2: v.scalar_tensor_tensor(
                out=cvt[:, 0:1920], in0=u_c[:, 128:2048], scalar=w2,
                in1=cvt[:, 0:1920], op0=ALU.mult, op1=ALU.add), None)
            kb.op("vector", lambda v, u_c=u_c, w2=w2: v.scalar_tensor_tensor(
                out=cvt[:, 1920:2047], in0=u_c[:, 1:128], scalar=w2,
                in1=cvt[:, 1920:2047], op0=ALU.mult, op1=ALU.add), None)
            kb.op("vector", lambda v, tt=tt, w2=w2: v.scalar_tensor_tensor(
                out=cvt[:, 2047:2048], in0=uh[:, tt * 2 + 1:tt * 2 + 2], scalar=w2,
                in1=cvt[:, 2047:2048], op0=ALU.mult, op1=ALU.add), None)
            kb.wait("vector", "seg", 4 * tt + 4)
            kb.bar("vector")
            kb.op("vector", lambda v, u_c=u_c, tt=tt: v.scalar_tensor_tensor(
                out=dsd[:, 0:2048], in0=u_c[:], scalar=c_bx[:, tt:tt + 1],
                in1=cvt[:, 0:2048], op0=ALU.add, op1=ALU.subtract), None)
            kb.bar("vector")
            kb.op("vector", lambda v: v.tensor_mul(dsd[:, 0:2048], dsd[:, 0:2048], psumB[:, 0:2048]), None)
            kb.bar("vector")
            kb.wait("vector", "qm")
            kb.op("vector", lambda v, um=um: v.tensor_add(um[:], dsd[:, 0:2048], cvt[:, 0:2048]), "vc")
            kb.wait("scalar", "vc")
            kb.dma("scalar", udram[tt * 128:(tt + 1) * 128, :], um[:], "qm")
        v_qm = kb.val("qm")
        if DEBUG:
            kb.wait("sync", "vc")
            kb.dma("sync", tap_uh, uh[:], "qtap")

        mark("proj")
        kb.wait("sync", "vc")
        kb.dma("sync", cfb_f[:], ins["fbB"], "qfb")
        v_qf_fb = kb.val("qfb")
        kb.wait("tensor", "qmt", v_qf_mt)
        kb.wait("tensor", "seu")
        kb.wait("tensor", "seg")
        kb.wait("tensor", "vc")
        for tt in range(NT):
            pa = psumA if tt % 2 == 0 else psumB
            kb.wait("tensor", "sf")
            for kk in range(ND):
                for nb in range(2):
                    kb.op("tensor", lambda t, tt=tt, kk=kk, nb=nb, pa=pa: t.matmul(
                        pa[:, nb * 512:(nb + 1) * 512],
                        xnbf[:, kk * TL + tt * 128:kk * TL + (tt + 1) * 128],
                        mtres[:, kk * 1024 + nb * 512:kk * 1024 + (nb + 1) * 512],
                        start=(kk == 0), stop=(kk == ND - 1)),
                        "tf" if (kk == ND - 1 and nb == 1) else None)
            kb.wait("vector", "tf")
            kb.wait("vector", "qfb", v_qf_fb)
            kb.op("vector", lambda v, tt=tt, pa=pa: v.tensor_add(
                yffn[:, tt * 1024:(tt + 1) * 1024], pa[:, 0:1024], cfb_f[:]), "sf")
        kb.wait("scalar", "sf")
        for tt in range(NT):
            kb.op("scalar", lambda s, tt=tt: s.activation(
                out=yffn[:, tt * 1024:(tt + 1) * 1024],
                in_=yffn[:, tt * 1024:(tt + 1) * 1024], func=AF.Silu), "sf2")
        v_sf2 = kb.val("sf2")

        mark("ffn")
        kb.wait("vector", "qdt")
        kb.op("vector", lambda v: v.tensor_scalar(
            out=dtv[:], in0=dtv[:], scalar1=1.0 / (B * L), scalar2=0.001,
            op0=ALU.mult, op1=ALU.max), "vd")
        kb.bar("vector")
        kb.op("vector", lambda v: v.tensor_scalar_min(dtv[:], dtv[:], 2.0), "vd")
        kb.bar("vector")
        v_vd_dt = kb.val("vd")
        if DEBUG:
            kb.wait("sync", "vd", v_vd_dt)
            kb.dma("sync", tap_dt, dtv[:], "qtap")
        kb.wait("vector", "vd", v_vd_dt)
        kb.wait("vector", "vb", v_vb_af)
        kb.op("vector", lambda v: v.tensor_tensor(
            out=lnad[:].rearrange("p (t s) -> p t s", s=S),
            in0=a_f[:].rearrange("p (t s) -> p t s", s=S),
            in1=dtv[:].unsqueeze(2).broadcast_to([128, NT, S]),
            op=ALU.mult), "vb")
        kb.wait("scalar", "vb")
        kb.op("scalar", lambda s: s.activation(out=adt[:], in_=lnad[:], func=AF.Exp), "sb")
        kb.op("vector", lambda v: v.tensor_scalar_add(rca[:], a_f[:], 1e-6), None)
        kb.bar("vector")
        kb.op("vector", lambda v: v.reciprocal(rca[:], rca[:]), "vb")
        kb.wait("vector", "sb")
        kb.op("vector", lambda v: v.tensor_scalar_add(bcn[:], adt[:], -1.0), None)
        kb.bar("vector")
        kb.op("vector", lambda v: v.tensor_mul(bcn[:], bcn[:], rca[:]), None)
        kb.bar("vector")
        kb.op("vector", lambda v: v.tensor_mul(bcn[:], bcn[:], c_bn[:]), None)
        kb.bar("vector")
        kb.op("vector", lambda v: v.tensor_mul(bcn[:], bcn[:], c_cn[:]), "vb")
        kb.bar("vector")

        def dense_tile(tt):
            sl = slice(tt * 16, (tt + 1) * 16)
            lnsl = lnad[:, sl]
            bcsl = bcn[:, sl]
            cdb = cd0 if tt % 2 == 0 else cd1
            if tt >= 2:
                kb.wait("vector", "qpd" + str(tt % 2), 16 * (tt // 2))
            kb.op("vector", lambda v, lnsl=lnsl: v.tensor_tensor(
                out=dsc1[:], in0=c_tcwv[:],
                in1=lnsl.unsqueeze(1).broadcast_to([128, Q, S]), op=ALU.mult), "vb")
            kb.wait("scalar", "vb")
            kb.op("scalar", lambda s, cdb=cdb: s.activation(
                out=cdb[:].rearrange("p (q f r) -> p q f r", q=16, f=4)[:, :, 0, :],
                in_=dsc1[:], func=AF.Exp), "sb")
            kb.wait("vector", "sb")
            kb.op("vector", lambda v, lnsl=lnsl: v.tensor_tensor(
                out=dsc1[:], in0=c_tcs2[:],
                in1=lnsl.unsqueeze(2).broadcast_to([128, S, Q]), op=ALU.mult), "vb")
            kb.wait("scalar", "vb")
            kb.op("scalar", lambda s: s.activation(out=dsc2[:], in_=dsc1[:], func=AF.Exp), "sb")
            kb.wait("vector", "sb")
            kb.op("vector", lambda v, cdb=cdb: v.tensor_copy(
                cdb[:].rearrange("p (q f r) -> p q f r", q=16, f=4)[:, :, 3, :],
                dsc2[:].rearrange("p (q r) -> p q r", q=16)), "vb")
            kb.op("vector", lambda v, bcsl=bcsl, cdb=cdb: v.tensor_tensor(
                out=cdb[:].rearrange("p (q f r) -> p q f r", q=16, f=4)[:, :, 1, :],
                in0=dsc2[:].rearrange("p (q r) -> p q r", q=16),
                in1=bcsl.unsqueeze(2).broadcast_to([128, S, Q]), op=ALU.mult), "vb")
            kb.wait("vector", "sb")
            kb.op("vector", lambda v, lnsl=lnsl: v.tensor_tensor(
                out=dsc1[:], in0=c_tck8[:],
                in1=lnsl.unsqueeze(1).broadcast_to([128, Q, S]), op=ALU.mult), "vb")
            kb.wait("scalar", "vb")
            kb.op("scalar", lambda s: s.activation(out=dsc2[:], in_=dsc1[:], func=AF.Exp), "sb")
            kb.wait("vector", "sb")
            kb.op("vector", lambda v, bcsl=bcsl: v.tensor_tensor(
                out=dsc2[:].rearrange("p (q r) -> p q r", q=16),
                in0=dsc2[:].rearrange("p (q r) -> p q r", q=16),
                in1=bcsl.unsqueeze(1).broadcast_to([128, Q, S]), op=ALU.mult), None)
            kb.bar("vector")
            kb.op("vector", lambda v: v.memset(kpad[:, 0:15], 0.0), None)
            kb.op("vector", lambda v: v.tensor_reduce(
                out=kpad[:, 15:31], in_=dsc2[:].rearrange("p (q r) -> p q r", q=16),
                axis=AXX, op=ALU.add), "vb")
            kb.bar("vector")
            for tq in range(Q):
                kb.op("vector", lambda v, tq=tq, cdb=cdb: v.tensor_copy(
                    cdb[:, tq * 64 + 32:tq * 64 + 48],
                    kpad[:, 15 - tq:31 - tq]), "vb")
            kb.wait("scalar", "vb")
            kb.wait("scalar", "sb")
            kb.dma("scalar", pdram[tt * 128:(tt + 1) * 128, :], cdb[:],
                   "qpd" + str(tt % 2))

        def strips_load(tt):
            kb.dma("gpsimd", strps[tt % 2].rearrange("p (g f) -> p g f", g=16),
                   pdram.rearrange("(t g i) (q f) -> t i q g f", t=NT, g=16, q=16)[tt],
                   "qs" + str(tt % 2))

        def g_load(tt):
            kb.dma("sync", gbs[tt % 2].rearrange("p (g j) -> p g j", g=16),
                   udram.rearrange("(t g i) (tau j) -> t i tau g j", t=NT, g=16, tau=16)[tt],
                   "qg" + str(tt % 2))

        def expand(dst, fam, strp, sem):
            kb.op("vector", lambda v, dst=dst, fam=fam, strp=strp: v.tensor_tensor(
                out=dst.rearrange("p (g i r) -> p g i r", i=8, r=16),
                in0=c_dm[:].rearrange("p (i r) -> p i r", r=16)
                          .unsqueeze(1).broadcast_to([128, 16, 8, 16]),
                in1=strp.rearrange("p (g f) -> p g f", g=16)[:, :, fam * 16:(fam + 1) * 16]
                        .unsqueeze(2).broadcast_to([128, 16, 8, 16]),
                op=ALU.mult), sem)

        def aq_build_to(dst, strp, sem):
            kb.op("vector", lambda v, dst=dst, strp=strp: v.tensor_tensor(
                out=dst.rearrange("p (g j) -> p g j", g=16),
                in0=c_rst[:].rearrange("p (g j) -> p g j", g=16),
                in1=strp.rearrange("p (g f) -> p g f", f=64)[:, :, 63]
                        .unsqueeze(2).broadcast_to([128, 16, 128]),
                op=ALU.mult), sem)

        def aq_build(strp, sem):
            aq_build_to(aq16, strp, sem)

        kb.wait("gpsimd", "sf2", v_sf2)
        if DEBUG:
            kb.wait("gpsimd", "qtap")
        for tt in range(NT):
            kb.wait("gpsimd", "qpd" + str(tt % 2), 16 * (tt // 2 + 1))
            if tt >= 2:
                kb.wait("gpsimd", "vw", 2 * (tt - 1))
            strips_load(tt)
        kb.wait("sync", "qm", v_qm)
        kb.wait("sync", "sf2", v_sf2)
        if DEBUG:
            kb.wait("sync", "qtap")
        for tt in range(NT):
            if tt >= 2:
                kb.wait("sync", "tv", 16 * (tt - 1))
            g_load(tt)

        kb.wait("tensor", "sf")
        kb.wait("tensor", "seu")
        kb.wait("tensor", "seg")
        kb.wait("tensor", "vc")

        wvb = [bigA[:, 0:2048], bigA[:, 2048:4096]]
        aqb = [bigA[:, 20480:22528], bigA[:, 22528:24576]]
        pVs = [psumA, psumB]
        TV0 = kb.val("tv")
        VS0 = kb.val("vs")

        def s1_front(tt):
            strp = strps[tt % 2]
            kb.wait("vector", "qs" + str(tt % 2), 16 * (tt // 2 + 1))
            if tt >= 2:
                kb.wait("vector", "tv", TV0 + 16 * (tt - 1))
            expand(wvb[tt % 2], 0, strp, "vw")
            aq_build_to(aqb[tt % 2], strp, "vw")
            kb.wait("tensor", "vw")
            kb.wait("tensor", "qg" + str(tt % 2), 16 * (tt // 2 + 1))
            if tt >= 2:
                kb.wait("tensor", "vs", VS0 + 2 * (tt - 1))
            for g in range(16):
                kb.op("tensor", lambda t, g=g, tt=tt: t.matmul(
                    pVs[tt % 2][:, g * 128:(g + 1) * 128],
                    wvb[tt % 2][:, g * 128:(g + 1) * 128],
                    gbs[tt % 2][:, g * 128:(g + 1) * 128],
                    start=True, stop=True), "tv")

        def s1_back(tt):
            kb.wait("vector", "tv", TV0 + 16 * (tt + 1))
            kb.wait("vector", "vw", 2 * (tt + 1))
            if tt >= 1:
                kb.wait("vector", "vs", VS0 + 2 * tt)
            kb.op("vector", lambda v, tt=tt: v.tensor_tensor_scan(
                out=sout[:, 0:2048], data0=aqb[tt % 2][:, 0:2048],
                data1=pVs[tt % 2][:, 0:2048],
                initial=0.0, op0=ALU.mult, op1=ALU.add), "vs")
            kb.bar("vector")
            kb.op("vector", lambda v, tt=tt: v.tensor_copy(
                sendst[:, tt * 16:(tt + 1) * 16],
                sout.rearrange("p (g j) -> p g j", j=128)[:, :, 127]), "vs")

        for k in range(NT + 2):
            if k < NT:
                dense_tile(k)
            if 1 <= k <= NT:
                s1_front(k - 1)
            if k >= 2:
                s1_back(k - 2)
        mark("dense")
        kb.bar("vector")
        kb.op("vector", lambda v: v.tensor_scalar_mul(sendst[:], sendst[:], c_cm[:, 0:1]), "vs")
        kb.bar("vector")
        kb.wait("scalar", "vs")
        kb.dma("scalar", ssin, sendst[:], "qar")
        kb.wait("gpsimd", "qar")
        kb.op("gpsimd", lambda g: g.collective_compute(
            "AllReduce", ALU.add, replica_groups=[[0, 1], [2, 3], [4, 5], [6, 7]],
            ins=[ssin], outs=[ssout]), "gar")
        v_gar_ss = kb.val("gar")
        kb.wait("sync", "gar", v_gar_ss)
        kb.dma("sync", shall[:], ssout, "qar")
        if DEBUG:
            kb.dma("sync", tap_sh, ssout, "qtap")
        kb.wait("vector", "qar")
        kb.op("vector", lambda v: v.tensor_scalar_mul(sheff[:], shall[:], c_cm[:, 1:2]), "vs")
        kb.bar("vector")
        kb.op("vector", lambda v: v.tensor_copy(sheffb[:], sheff[:]), "vs")
        kb.bar("vector")
        v_vs_sh = kb.val("vs")

        mark("s1")
        v_vw_s1 = kb.val("vw")
        for tt in range(NT):
            if tt >= 2:
                kb.wait("gpsimd", "vw", v_vw_s1 + 3 * (tt - 1))
            strips_load(tt)
            if tt == 1:
                kb.wait("gpsimd", "vc", 32)
                for kk in range(8):
                    kb.dma("gpsimd", bigB[:, 8192 + kk * 1024:8192 + (kk + 1) * 1024],
                           ins["woutt"][kk * 128:(kk + 1) * 128, :], "qw")
        for tt in range(NT):
            if tt >= 2:
                kb.wait("sync", "ty", tt - 1)
            g_load(tt)

        kb.wait("vector", "vs", v_vs_sh)
        for tt in range(NT):
            strp = strps[tt % 2]
            kb.wait("vector", "qs" + str(tt % 2), 16 * (NT // 2 + tt // 2 + 1))
            kb.wait("vector", "tv")
            kb.wait("vector", "ty")
            kb.op("vector", lambda v, strp=strp: v.tensor_tensor(
                out=bigA[:, 0:8192].rearrange("p (c i r) -> p c i r", c=64, i=8),
                in0=c_dm[:].rearrange("p (i r) -> p i r", r=16)
                          .unsqueeze(1).broadcast_to([128, 64, 8, 16]),
                in1=strp.rearrange("p (c r) -> p c r", r=16)
                        .unsqueeze(2).broadcast_to([128, 64, 8, 16]),
                op=ALU.mult), "vw")
            aq_build(strp, "vw")
            kb.wait("tensor", "vw")
            kb.wait("tensor", "qg" + str(tt % 2), 16 * (NT // 2 + tt // 2 + 1))
            kb.wait("tensor", "vs")
            for g in range(16):
                kb.op("tensor", lambda t, g=g, tt=tt: t.matmul(
                    psumA[:, g * 128:(g + 1) * 128],
                    bigA[:, g * 512:g * 512 + 128],
                    gbs[tt % 2][:, g * 128:(g + 1) * 128],
                    start=True, stop=True), "tv")
            kb.wait("vector", "tv")
            kb.op("vector", lambda v, tt=tt, strp=strp: v.tensor_tensor(
                out=vfix[:], in0=sheff[:, tt * 16:(tt + 1) * 16],
                in1=strp.rearrange("p (g f) -> p g f", f=64)[:, :, 63],
                op=ALU.mult), "vw")
            kb.bar("vector")
            kb.op("vector", lambda v: v.tensor_add(
                psumA[:, 0:2048].rearrange("p (g j) -> p g j", j=128)[:, :, 0],
                psumA[:, 0:2048].rearrange("p (g j) -> p g j", j=128)[:, :, 0],
                vfix[:]), None)
            kb.bar("vector")
            kb.wait("vector", "vw")
            kb.op("vector", lambda v: v.tensor_tensor_scan(
                out=sout[:, 0:2048], data0=aq16[:, 0:2048], data1=psumA[:, 0:2048],
                initial=0.0, op0=ALU.mult, op1=ALU.add), "vs")
            kb.wait("tensor", "vs")
            kb.wait("tensor", "sy")
            for g in range(16):
                kb.op("tensor", lambda t, g=g, tt=tt: t.matmul(
                    psumB[:, g * 128:g * 128 + 1],
                    bigA[:, g * 512 + 128:g * 512 + 256],
                    sheffb[:, tt * 16 + g:tt * 16 + g + 1],
                    start=True, stop=False), None)
                kb.op("tensor", lambda t, g=g: t.matmul(
                    psumB[:, g * 128 + 1:(g + 1) * 128],
                    bigA[:, g * 512 + 128:g * 512 + 256],
                    sout[:, g * 128:g * 128 + 127],
                    start=False, stop=False), None)
                kb.op("tensor", lambda t, g=g, tt=tt: t.matmul(
                    psumB[:, g * 128:(g + 1) * 128],
                    bigA[:, g * 512 + 256:g * 512 + 384],
                    gbs[tt % 2][:, g * 128:(g + 1) * 128],
                    start=False, stop=True), "ty" if g == 15 else None)
            kb.wait("scalar", "ty")
            kb.wait("scalar", "qyd")
            for nb in range(4):
                kb.op("scalar", lambda s, tt=tt, nb=nb: s.copy(
                    out=ygbs[tt % 2][:, nb * 512:(nb + 1) * 512],
                    in_=psumB[:, nb * 512:(nb + 1) * 512]), "sy")
            kb.wait("scalar", "sy")
            kb.dma("scalar",
                   ydram.rearrange("(t g i) (tp j) -> t i tp g j", t=NT, g=16, tp=16)[tt],
                   ygbs[tt % 2].rearrange("p (g j) -> p g j", g=16), "qyd")
        v_qyd = kb.val("qyd")

        mark("s3")
        kb.wait("gpsimd", "vs")
        kb.wait("gpsimd", "ty")
        kb.wait("gpsimd", "qyd", v_qyd)
        for kk in range(8, NT):
            kb.dma("gpsimd", wob[:, (kk - 8) * 1024:(kk - 7) * 1024],
                   ins["woutt"][kk * 128:(kk + 1) * 128, :], "qw")
        v_qw_wo = kb.val("qw")

        kb.wait("sync", "ty")
        kb.dma("sync", cbout_t[:], ins["boutB"], "qc")
        kb.dma("sync", clng_t[:], ins["lngB"], "qc")
        kb.dma("sync", clnb_t[:], ins["lnbB"], "qc")
        v_qc_t = kb.val("qc")

        ypair = [bigB[:, 0:4096], bigB[:, 4096:8192]]

        def ypair_load(P):
            kb.dma("sync", ypair[P % 2].rearrange("p (k c) -> p k c", c=256),
                   ydram.rearrange("(k p) (q c) -> q p k c", k=NT, c=256)[P],
                   "qyl" + str(P % 2))

        kb.wait("sync", "qyd", v_qyd)
        ypair_load(0)
        ypair_load(1)
        if True:
            for tl in range(NT):
                tt = tl
                pa = psumA if tl % 2 == 0 else psumB
                if tl % 2 == 0 and 2 <= tl <= 12:
                    kb.wait("sync", "tt2", tl)
                    ypair_load(tl // 2 + 1)
                if tt >= 2:
                    kb.wait("scalar", "vt", 7 * (tt - 2) + 7)
                kb.dma("scalar", xts[tl % 2][:], ins["xin"][tt * 128:(tt + 1) * 128, :],
                       "qxr" + str(tl % 2))
                kb.wait("tensor", "qyl" + str((tt // 2) % 2), 16 * (tt // 4 + 1))
                if tt == 0:
                    kb.wait("tensor", "qw", v_qw_wo)
                if tt >= 2:
                    kb.wait("tensor", "vt", 7 * (tt - 2) + 2)
                for kk in range(NT):
                    wsl = (bigB[:, 8192 + kk * 1024:8192 + (kk + 1) * 1024] if kk < 8
                           else wob[:, (kk - 8) * 1024:(kk - 7) * 1024])
                    for nb in range(2):
                        kb.op("tensor", lambda t, wsl=wsl, nb=nb, tt=tt, pa=pa, kk=kk: t.matmul(
                            pa[:, nb * 512:(nb + 1) * 512],
                            ypair[(tt // 2) % 2][:, kk * 256 + (tt % 2) * 128:kk * 256 + (tt % 2) * 128 + 128],
                            wsl[:, nb * 512:(nb + 1) * 512],
                            start=(kk == 0), stop=(kk == NT - 1)),
                            "tt2" if (kk == NT - 1 and nb == 1) else None)
                kb.wait("vector", "qxr" + str(tl % 2), 16 * (tl // 2 + 1))
                kb.op("vector", lambda v, tl=tl, tt=tt: v.tensor_scalar(
                    out=xnt[:], in0=xts[tl % 2][:],
                    scalar1=mstats[:, tt:tt + 1], scalar2=mstats[:, 16 + tt:16 + tt + 1],
                    op0=ALU.subtract, op1=ALU.mult), "vt")
                kb.bar("vector")
                kb.wait("vector", "tt2")
                kb.wait("vector", "qc", v_qc_t)
                kb.op("vector", lambda v, pa=pa: v.tensor_add(rtk[:], pa[:, 0:1024], xnt[:]), "vt")
                kb.bar("vector")
                kb.op("vector", lambda v: v.tensor_add(rtk[:], rtk[:], cbout_t[:]), "vt")
                kb.bar("vector")
                kb.op("vector", lambda v: v.reduce_sum(stat[:, 8:9], rtk[:], axis=AXX), "vt")
                kb.bar("vector")
                kb.wait("scalar", "vt")
                kb.op("scalar", lambda s: s.activation(out=sqsc[:], in_=rtk[:], func=AF.Square,
                                                       accum_out=stat[:, 9:10]), "st2")
                kb.wait("vector", "st2")
                kb.op("vector", lambda v: v.tensor_scalar_mul(stat[:, 8:9], stat[:, 8:9], 1.0 / D), None)
                kb.op("vector", lambda v: v.tensor_scalar_mul(stat[:, 9:10], stat[:, 9:10], 1.0 / D), None)
                kb.bar("vector")
                kb.op("vector", lambda v: v.tensor_mul(stat[:, 10:11], stat[:, 8:9], stat[:, 8:9]), None)
                kb.bar("vector")
                kb.op("vector", lambda v: v.tensor_sub(stat[:, 9:10], stat[:, 9:10], stat[:, 10:11]), "vt")
                kb.wait("scalar", "vt")
                kb.op("scalar", lambda s: s.activation(out=stat[:, 9:10], in_=stat[:, 9:10],
                                                       func=AF.Ln, bias=c_eps[:, 0:1]), None)
                kb.bar("scalar")
                kb.op("scalar", lambda s: s.activation(out=stat[:, 10:11], in_=stat[:, 9:10],
                                                       func=AF.Exp, scale=-0.5), "st2")
                kb.wait("vector", "st2")
                kb.bar("vector")
                kb.op("vector", lambda v: v.tensor_scalar(
                    out=snm[:], in0=rtk[:], scalar1=stat[:, 8:9], scalar2=stat[:, 10:11],
                    op0=ALU.subtract, op1=ALU.mult), "vt")
                kb.bar("vector")
                kb.op("vector", lambda v: v.tensor_mul(t3t[:], snm[:], clng_t[:]), None)
                kb.bar("vector")
                kb.op("vector", lambda v: v.tensor_add(t3t[:], t3t[:], clnb_t[:]), None)
                kb.bar("vector")
                kb.wait("vector", "sf2", v_sf2)
                kb.op("vector", lambda v, tt=tt: v.tensor_mul(
                    ptt[:], t3t[:], yffn[:, tt * 1024:(tt + 1) * 1024]), None)
                kb.bar("vector")
                kb.wait("vector", "qo")
                kb.op("vector", lambda v, tl=tl: v.tensor_add(ot0[:], ptt[:], xts[tl % 2][:]), "vt")
                kb.wait("sync", "vt")
                kb.dma("sync", xout[tt * 128:(tt + 1) * 128, :], ot0[:], "qo")
        kb.wait("sync", "qo")
        mark("tail")

        if CUT is not None:
            cut = marks[CUT]
            for e in kb.ENGS:
                kb.ops[e] = kb.ops[e][:cut[e]]
        block = E(nc.Block())

        @block.sync
        def _(sync):
            for f in kb.ops["sync"]:
                f(sync)

        @block.scalar
        def _(scalar):
            for f in kb.ops["scalar"]:
                f(scalar)

        @block.vector
        def _(vector):
            for f in kb.ops["vector"]:
                f(vector)

        @block.tensor
        def _(tensor):
            for f in kb.ops["tensor"]:
                f(tensor)

        @block.gpsimd
        def _(gpsimd):
            for f in kb.ops["gpsimd"]:
                f(gpsimd)

    return nc


def kernel(**inputs):
    in_maps = _host_inputs(inputs)
    nc = build_nc()
    res = run_bass_kernel_spmd(nc, in_maps, core_ids=list(range(NCORE)))
    perm = _tt_perm()
    out = np.zeros((B, L, D), np.float32)
    for c in range(NCORE):
        b, half = c >> 1, c & 1
        out[b, half * TL + perm] = res.results[c]["xout"]
    return out
